# version 07
# baseline (speedup 1.0000x reference)
import contextlib
import numpy as np
import ml_dtypes

import concourse.bass as bass
import concourse.mybir as mybir
from concourse.bass_utils import run_bass_kernel_spmd

F32 = mybir.dt.float32
BF16 = mybir.dt.bfloat16
AF = mybir.ActivationFunctionType
ALU = mybir.AluOpType
AX = mybir.AxisListType
BF = ml_dtypes.bfloat16

NCORES = 8
B, S, DM = 4, 8192, 1024
TOK = 4096
HALO = 128
TOKP = HALO + TOK
DI = 2048
C2 = 2 * DI
NS = 16
RK = 64
SL = 512
NSLICE = TOK // SL
NDN = DI // 8
EPS = 1e-5

NEXP = 8
HS = 2048
HR = 1024

ADIM = 1024
HEADS, DH, WIN, INNER = 8, 64, 512, 512
HID = 2730
FFH = 5460

PAIR_GROUPS = [[0, 1], [2, 3], [4, 5], [6, 7]]



def _col_panel(v, p=128):
    k = v.shape[0] // p
    return np.ascontiguousarray(np.asarray(v, np.float32).reshape(k, p).T)


def _host_constants():
    c = {}
    c["ident_bf"] = np.eye(128, dtype=np.float32)
    st = np.zeros((16, 128, 128), np.float32)
    rd = np.zeros((16, 128, 128), np.float32)
    for tp in range(16):
        for p in range(128):
            st[tp, 8 * tp + p // 16, p] = 1.0
            rd[tp, p, 8 * tp + p // 16] = 1.0
    c["st01"] = np.ascontiguousarray(st.transpose(1, 0, 2).reshape(128, 2048))
    c["red01"] = np.ascontiguousarray(rd.transpose(1, 0, 2).reshape(128, 2048))
    br = np.zeros((16, 128), np.float32)
    for p in range(128):
        br[p % 16, p] = 1.0
    c["brep01"] = br
    c["ones_row"] = np.ones((1, 128), np.float32)
    hd = np.zeros((128, 2), np.float32)
    hd[0:64, 0] = 1.0; hd[64:128, 1] = 1.0
    c["hd01"] = hd
    c["hd01T"] = np.ascontiguousarray(hd.T)
    ii = np.arange(128)
    c["mask_ge"] = (ii[:, None] >= ii[None, :]).astype(np.float32)
    c["mask_le"] = (ii[:, None] <= ii[None, :]).astype(np.float32)
    c["trilmask_add"] = np.where(ii[:, None] <= ii[None, :], 0.0, -1.0e30).astype(np.float32)
    return c


_HC = _host_constants()


def _prep_core_inputs(inputs, c):
    b, h = c // 2, c % 2
    x = np.asarray(inputs["L0_attn_x"], np.float32)
    own = x[b, h * TOK:(h + 1) * TOK]
    halo = np.zeros((HALO, DM), np.float32) if h == 0 else x[b, h * TOK - HALO:h * TOK]
    m = {"x": np.ascontiguousarray(np.concatenate([halo, own], axis=0))}

    A = lambda k: np.asarray(inputs[k], np.float32)
    m["inw"] = A("L0_attn_in_w")
    m["normw_cols"] = _col_panel(A("L0_attn_norm_w"))
    m["convw_cols"] = np.ascontiguousarray(
        A("L0_attn_conv_w").reshape(DI, 4).reshape(16, 128, 4).transpose(1, 0, 2).reshape(128, 64))
    m["convb_cols"] = _col_panel(A("L0_attn_conv_b"))
    m["xproj_t"] = np.ascontiguousarray(
        A("L0_attn_x_proj_w").reshape(16, 128, 96).transpose(1, 0, 2).reshape(128, 16 * 96))
    m["dtw"] = A("L0_attn_dt_w")
    m["dtb_cols"] = _col_panel(A("L0_attn_dt_b"))
    m["alog_cols"] = np.ascontiguousarray(
        A("L0_attn_A_log").reshape(NDN, 8, NS).transpose(1, 2, 0).reshape(128, NDN))
    m["D_cols"] = _col_panel(A("L0_attn_D"))
    m["outw"] = A("L0_attn_out_w")

    m["sw1"] = A("L0_ffn_sw1"); m["sw3"] = A("L0_ffn_sw3"); m["sw2"] = A("L0_ffn_sw2")
    m["sb1_cols"] = _col_panel(A("L0_ffn_sb1"))
    m["sb3_cols"] = _col_panel(A("L0_ffn_sb3"))
    m["sb2_cols"] = _col_panel(A("L0_ffn_sb2"))
    m["w1e"] = A("L0_ffn_w1"); m["w2e"] = A("L0_ffn_w2")
    m["b1e_cols"] = np.ascontiguousarray(A("L0_ffn_b1").reshape(NEXP, 8, 128).transpose(2, 0, 1).reshape(128, 64))
    m["b2e_cols"] = np.ascontiguousarray(A("L0_ffn_b2").reshape(NEXP, 8, 128).transpose(2, 0, 1).reshape(128, 64))

    m["wqkvT"] = np.ascontiguousarray(A("L1_attn_W_qkv").T)
    m["bqkv_cols"] = _col_panel(A("L1_attn_b_qkv"))
    m["bqkv_vrow"] = np.ascontiguousarray(A("L1_attn_b_qkv")[None, 2048:3072])

    m["f_wqkv"] = A("L1_ffn_wqkv")
    qs = A("L1_ffn_q_scale"); ks = A("L1_ffn_k_scale")
    m["f_qs_cols"] = np.ascontiguousarray(np.concatenate([np.tile(qs[l], 2)[:, None] for l in range(2)], 1))
    m["f_ks_cols"] = np.ascontiguousarray(np.concatenate([np.tile(ks[l], 2)[:, None] for l in range(2)], 1))
    m["f_wo"] = A("L1_ffn_wo")
    m["f_lng_cols"] = np.ascontiguousarray(A("L1_ffn_ln_g").reshape(2, 8, 128).transpose(2, 0, 1).reshape(128, 16))
    m["f_lnb_cols"] = np.ascontiguousarray(A("L1_ffn_ln_b").reshape(2, 8, 128).transpose(2, 0, 1).reshape(128, 16))
    w1 = A("L1_ffn_w1")
    w1a = np.zeros((2, DM, 2816), np.float32); w1a[:, :, :HID] = w1[:, :, 0:HID]
    w1g = np.zeros((2, DM, 2816), np.float32); w1g[:, :, :HID] = w1[:, :, HID:FFH]
    m["f_w1a"] = w1a; m["f_w1g"] = w1g
    b1 = A("L1_ffn_b1")
    b1a = np.zeros((2, 2816), np.float32); b1a[:, :HID] = b1[:, 0:HID]
    b1g = np.zeros((2, 2816), np.float32); b1g[:, :HID] = b1[:, HID:FFH]
    m["f_b1a_cols"] = np.ascontiguousarray(b1a.reshape(2, 22, 128).transpose(2, 0, 1).reshape(128, 44))
    m["f_b1g_cols"] = np.ascontiguousarray(b1g.reshape(2, 22, 128).transpose(2, 0, 1).reshape(128, 44))
    w2 = A("L1_ffn_w2")
    w2p = np.zeros((2, 2816, DM), np.float32); w2p[:, :HID, :] = w2
    m["f_w2p"] = w2p
    m["f_b2_cols"] = np.ascontiguousarray(A("L1_ffn_b2").reshape(2, 8, 128).transpose(2, 0, 1).reshape(128, 16))
    P = np.zeros((128, 128), np.float32)
    for hh in range(2):
        for dd in range(64):
            P[hh * 64 + (dd + 32) % 64, hh * 64 + dd] = 1.0
    m["swap64"] = P

    m["hflag"] = np.full((128, 1), float(h), np.float32)
    for k, v in _HC.items():
        m[k] = v

    own0 = h * TOK
    pos = np.arange(own0 - WIN, own0 + TOK, dtype=np.float64)
    half = np.arange(0, DH, 2, dtype=np.float64)
    inv = 1.0 / 10000.0 ** (half / DH)
    base = (half + 0.4 * DH) / (1.4 * DH)
    freqs = pos[:, None] * inv[None, :]
    freqs = np.concatenate([freqs, freqs], axis=1)
    power = (pos - S // 2) / float(WIN // 2)
    scale = base[None, :] ** power[:, None]
    scale = np.concatenate([scale, scale], axis=1)
    cos, sin = np.cos(freqs), np.sin(freqs)
    sgn = np.concatenate([-np.ones(32), np.ones(32)])[None, :]
    m["rot_q_cs"] = np.ascontiguousarray(np.tile((cos * scale * 8.0).astype(np.float32).T, (2, 1)))
    m["rot_q_sn"] = np.ascontiguousarray(np.tile((sin * scale * sgn * 8.0).astype(np.float32).T, (2, 1)))
    m["rot_k_cs"] = np.ascontiguousarray(np.tile((cos / scale).astype(np.float32).T, (2, 1)))
    m["rot_k_sn"] = np.ascontiguousarray(np.tile((sin / scale * sgn).astype(np.float32).T, (2, 1)))

    i_l = np.arange(WIN); j_l = np.arange(2 * WIN)
    allow = ((j_l[:, None] >= i_l[None, :]) & (j_l[:, None] <= i_l[None, :] + WIN)).astype(np.float32)
    m["lmask_first"] = np.ascontiguousarray(allow * (j_l[:, None] >= WIN)) if h == 0 \
        else np.ascontiguousarray(allow)
    m["lmask_rest"] = np.ascontiguousarray(allow)
    return m



class Sem:
    def __init__(self, es, nc, name):
        self.h = es.enter_context(nc.semaphore(name))
        self.v = 0

    def inc(self, instr, k=1):
        instr.then_inc(self.h, k)
        self.v += k
        return self.v


DBG_SPECS = {
    "dbg_xn": ((DM, TOKP), BF16),
    "dbg_srow": ((1, TOKP), F32),
    "dbg_s": ((DM, TOK), BF16),
    "dbg_g": ((DM, TOK), BF16),
    "dbg_x1": ((TOK, DM), F32),
    "dbg_xn1": ((2 * DM, TOK), BF16),
    "dbg_s1": ((TOK, DM), F32),
    "dbg_g1": ((TOK, DM), BF16),
    "dbg_xf": ((DM, 4610), BF16),
}


class Prog:
    def __init__(self, upto="all", debug=()):
        self.upto = upto
        self.debug = tuple(debug)
        self.nc = bass.Bass("TRN2", target_bir_lowering=False)
        self.es = contextlib.ExitStack()
        self.t = {}
        self.ap = {}
        nc = self.nc
        self.V, self.SC, self.TE, self.G, self.SY = nc.vector, nc.scalar, nc.tensor, nc.gpsimd, nc.sync
        self._declare_dram()
        self._declare_sbuf()
        self._declare_sems()
        self._emit()

    def sb(self, name, shape, dt=F32):
        self.t[name] = self.es.enter_context(self.nc.sbuf_tensor(name, shape, dt))

    def psum(self, name, shape, dt=F32):
        self.t[name] = self.es.enter_context(self.nc.psum_tensor(name, shape, dt))

    def dram(self, name, shape, dt=F32, kind=None):
        if kind is None:
            h = self.nc.dram_tensor(name, list(shape), dt)
        else:
            h = self.nc.dram_tensor(name, list(shape), dt, kind=kind)
        self.ap[name] = h.ap()
        setattr(self, "d_" + name, h)

    def dma(self, eng, sem, dst, src):
        e = {"S": self.SY, "A": self.SC, "G": self.G}[eng]
        return sem.inc(e.dma_start(out=dst, in_=src), 16)

    def dma_nc(self, eng, sem, dst, src):
        e = {"S": self.SY, "A": self.SC, "G": self.G}[eng]
        return sem.inc(e.dma_start(out=dst, in_=src, allow_slow_non_contiguous=True), 16)

    def after(self, eng, *deps):
        for x in deps:
            if isinstance(x, tuple):
                eng.wait_ge(x[0].h, x[1])
            else:
                eng.wait_ge(x.h, x.v)

    def phase_barrier(self, *sems):
        for e in (self.V, self.SC, self.TE, self.G, self.SY):
            for s in sems:
                if s.v:
                    e.wait_ge(s.h, s.v)

    def _declare_dram(self):
        ins = [
            ("x", (TOKP, DM), F32), ("inw", (DM, C2), F32), ("normw_cols", (128, 8), F32),
            ("convw_cols", (128, 64), F32), ("convb_cols", (128, 16), F32),
            ("xproj_t", (128, 16 * 96), F32), ("dtw", (RK, DI), F32), ("dtb_cols", (128, 16), F32),
            ("alog_cols", (128, NDN), F32), ("D_cols", (128, 16), F32), ("outw", (DI, DM), F32),
            ("sw1", (DM, HS), F32), ("sw3", (DM, HS), F32), ("sw2", (HS, DM), F32),
            ("sb1_cols", (128, 16), F32), ("sb3_cols", (128, 16), F32), ("sb2_cols", (128, 8), F32),
            ("w1e", (NEXP, DM, HR), F32), ("w2e", (NEXP, HR, DM), F32),
            ("b1e_cols", (128, 64), F32), ("b2e_cols", (128, 64), F32),
            ("wqkvT", (DM, 3 * ADIM), F32), ("bqkv_cols", (128, 24), F32),
            ("bqkv_vrow", (1, ADIM), F32),
            ("f_wqkv", (2, DM, 3 * INNER), F32), ("f_qs_cols", (128, 2), F32), ("f_ks_cols", (128, 2), F32),
            ("f_wo", (2, INNER, DM), F32), ("f_lng_cols", (128, 16), F32), ("f_lnb_cols", (128, 16), F32),
            ("f_w1a", (2, DM, 2816), F32), ("f_w1g", (2, DM, 2816), F32),
            ("f_b1a_cols", (128, 44), F32), ("f_b1g_cols", (128, 44), F32),
            ("f_w2p", (2, 2816, DM), F32), ("f_b2_cols", (128, 16), F32), ("swap64", (128, 128), F32),
            ("hflag", (128, 1), F32), ("ident_bf", (128, 128), F32), ("st01", (128, 2048), F32),
            ("red01", (128, 2048), F32), ("brep01", (16, 128), F32), ("ones_row", (1, 128), F32),
            ("rot_q_cs", (128, TOK + WIN), F32), ("rot_q_sn", (128, TOK + WIN), F32),
            ("rot_k_cs", (128, TOK + WIN), F32), ("rot_k_sn", (128, TOK + WIN), F32),
            ("lmask_first", (2 * WIN, WIN), F32), ("lmask_rest", (2 * WIN, WIN), F32),
            ("mask_ge", (128, 128), F32), ("hd01", (128, 2), F32), ("hd01T", (2, 128), F32),
            ("mask_le", (128, 128), F32), ("trilmask_add", (128, 128), F32),
        ]
        for name, shape, dt in ins:
            self.dram(name, shape, dt, kind="ExternalInput")
        self.dram("out", (TOK, DM), F32, kind="ExternalOutput")
        self.dram("xnT", (DM, TOKP), BF16)
        self.dram("srow_d", (1, TOKP), F32)
        self.dram("y0_d", (16, 128, SL), BF16)
        self.dram("gacc_d", (8, 128, TOK), BF16)
        self.dram("inw_t", (128, 32 * 8 * 128), BF16)
        self.dram("smamba", (DM, TOK), BF16)
        self.dram("x1_d", (TOK, DM), F32)
        self.dram("xeP", (16, 128, 8 * SL), BF16)
        self.dram("ag_state_in", (128, NDN), F32)
        self.dram("ag_state_out", (2 * 128, NDN), F32)
        self.dram("xn1T_in", (DM, TOK), BF16)
        self.dram("xn1T_out", (2 * DM, TOK), BF16)
        self.dram("s1_d", (TOK, DM), F32)
        self.dram("attacc", (4096, 1040), F32)
        self.dram("xf_d", (DM, 4610), BF16)
        self.dram("w1a_bf", (2, DM, 2816), BF16)
        self.dram("w1g_bf", (2, DM, 2816), BF16)
        self.dram("w2_bf", (2, 2816, DM), BF16)
        self.dram("ag_s1_in", (2048, DM), F32)
        self.dram("ag_s1_out", (2 * 2048, DM), F32)
        self.dram("ag_ft_in", (DM, 516), BF16)
        self.dram("ag_ft_out", (2 * DM, 516), BF16)
        self.dram("ag_fc_in", (DM, 4), BF16)
        self.dram("ag_fc_out", (2 * DM, 4), BF16)
        for nm in self.debug:
            shape, dt = DBG_SPECS[nm]
            self.dram(nm, shape, dt, kind="ExternalOutput")

    def _declare_sbuf(self):
        sb = self.sb
        sb("ident", [128, 128], BF16)
        sb("st01_sb", [128, 2048], BF16)
        sb("red01_sb", [128, 2048], BF16)
        sb("brep01_sb", [16, 128], BF16)
        sb("ones_row_sb", [1, 128], BF16)
        sb("colbank", [128, 512])
        sb("acols", [128, NDN])
        sb("hcol", [128, 1])
        sb("dtw_sb", [64, DI], BF16)
        sb("xproj_sb", [128, 16 * 96], BF16)
        sb("stats", [128, 8])
        sb("zeros512", [128, SL])
        sb("states", [128, NDN])
        sb("s0init", [128, NDN])
        sb("xdbl", [96, SL])
        sb("dtb16", [64, SL], BF16)
        sb("Bbuf", [16, SL], BF16)
        sb("Cbuf", [16, SL], BF16)
        sb("Cbuf0", [16, SL], BF16)
        sb("trilmask_sb", [128, 128])
        sb("hd01_sb", [128, 2], BF16)
        sb("hd01T_sb", [2, 128], BF16)
        sb("swap64_sb", [128, 128], BF16)
        sb("onescol", [128, 2], BF16)
        sb("bvrow", [1, ADIM], BF16)
        sb("onesrow1", [1, 128], BF16)
        sb("srsl", [1, SL + 16])
        sb("srsl_bf", [1, SL + 16], BF16)
        sb("stg512", [128, SL], BF16)
        sb("poolB", [128, 65536], BF16)
        sb("poolF", [128, 11168], F32)
        for i in range(7):
            self.psum(f"pb{i}", [128, 512], F32)
        self.psum("ptr", [128, 1024], BF16)

        PB, PF = self.t["poolB"], self.t["poolF"]

        def bview(name, off, n):
            self.t[name] = None
            setattr(self, "v_" + name, (PB, off, n))
            return PB[:, off:off + n]

        B_ = {}
        def vb(name, off, n):
            B_[name] = (off, n)
            self.t[name] = PB[:, off:off + n]
        def vf(name, off, n):
            self.t[name] = PF[:, off:off + n]

        vb("ucb", 0, 8192)
        vb("resb", 8192, 8192)
        vb("db", 16384, 8192)
        vb("dub", 24576, 8192)
        vb("sgrp", 32768, 8192)
        vb("ygat", 40960, 8192)
        vb("inw_tile", 49152, 2048)
        vb("outw_tile", 51200, 4096)
        vb("xnsl", 55296, 4224)
        vb("xnres", 59520, 512)
        vb("ycb", 60032, 1024)
        vb("xnb", 0, 1024)
        vb("xnb2", 1024, 1024)
        vb("trstage", 2048, 1024)
        vb("wexp", 0, 8192)
        vb("hssP", 0, 16384)
        vb("shwP", 16384, 2048)
        vb("xep0", 18432, 4096)
        vb("xep1", 22528, 4096)
        vb("h13", 26624, 512)
        vb("he0", 8192, 4096)
        vb("he1", 12288, 4096)
        vb("xe0", 16384, 4096)
        vb("xe1", 20480, 4096)
        vb("hss", 24576, 8192)
        vb("shw", 40960, 2048)
        vb("gslice", 43008, 4096)
        vb("gsilu", 47104, 4096)
        vb("ssl", 51200, 4096)
        vb("a_KT", 0, 8192)
        vb("a_VT", 8192, 8192)
        vb("a_expS", 16384, 4096)
        vb("a_QT", 20480, 4096)
        vb("a_XS", 24576, 4096)
        vb("a_wq", 28672, 8192)
        vb("a_wkv", 36864, 16384)
        vb("a_stg", 53248, 2064)
        vb("f_qhat", 0, 2048)
        vb("f_khat", 2048, 4096)
        vb("f_vtok", 6144, 4096)
        vb("f_expS", 10240, 4096)
        vb("f_oh", 14336, 4096)
        vb("f_upd", 18432, 8192)
        vb("f_xw", 26624, 8448)
        vb("f_wqkv", 35072, 12288)
        vb("f_wo", 47360, 8192)
        vb("f_mfirst", 55552, 4096)
        vb("f_mrest", 59648, 4096)
        vb("f_kqtmp", 63744, 1024)
        vb("f_hln", 0, 8192)
        vb("f_h2", 8192, 22528)
        vb("f_w1s", 30720, 3072)
        vb("f_w2s", 33792, 2816)
        vb("f_bst", 36608, 8)
        vb("f_xc", 36864, 8448)

        vf("xt", 0, 2048)
        vf("sqscrap", 2048, 1024)
        vf("uraw", 3072, SL + 32)
        vf("uacc", 3616, 2 * SL)
        vf("srep", 4640, SL + 16)
        vf("dA", 5184, 2 * SL)
        vf("dBu", 6208, 2 * SL)
        vf("sps", 7232, SL)
        vf("ytmp", 7744, SL)
        vf("silures", 8256, SL)
        vf("Brep", 8768, SL)
        vf("Crep", 9280, SL)
        vf("y0rl", 9792, SL)
        vf("a_stgF", 0, 2080)
        vf("f_rcs", 2080, 512)
        vf("f_rsn", 2592, 512)
        vf("f_mrep", 4128, 512)
        vf("f_rrep", 4640, 512)
        self._poolmaps_l1_done = False

    def _declare_sems(self):
        for nm in ["eT", "eV", "eS", "eG", "ccs"]:
            setattr(self, nm, Sem(self.es, self.nc, nm))
        for nm in ["cS", "cG", "sX", "sIW", "sIWst", "sXN", "sSR", "sXS", "sSRL", "sWa", "sWb",
                   "sBC", "sOWa", "sOWb", "sXR", "sSM", "sY0", "sY0L", "sST", "sSTL"]:
            setattr(self, nm, Sem(self.es, self.nc, nm))
        self._dyn_sems = {}

    def dsem(self, name):
        if name not in self._dyn_sems:
            self._dyn_sems[name] = Sem(self.es, self.nc, name)
        return self._dyn_sems[name]

    def _emit(self):
        self.p0_consts()
        self.p0_inw_prep()
        self.p0_layernorm()
        if self.upto != "p0":
            self.l0_mamba()
            self.l0_moe()
            self.l0_fixup_gate()
        if self.upto in ("all", "l1a", "l1f"):
            self.l1_attn()
        if self.upto in ("all", "l1f"):
            self.l1_ffn()
        if self.upto == "all":
            self.l1_gate_out()
        self.finalize()
        self.es.close()

    def p0_consts(self):
        t, ap = self.t, self.ap
        V, SC = self.V, self.SC
        dma = self.dma
        dma("G", self.cG, t["ident"][:], ap["ident_bf"])
        dma("G", self.cG, t["st01_sb"][:], ap["st01"])
        dma("G", self.cG, t["red01_sb"][:], ap["red01"])
        dma("G", self.cG, t["brep01_sb"][:], ap["brep01"])
        dma("G", self.cG, t["ones_row_sb"][:], ap["ones_row"])
        dma("G", self.cG, t["dtw_sb"][:], ap["dtw"])
        dma("G", self.cG, t["xproj_sb"][:], ap["xproj_t"])
        dma("S", self.cS, t["colbank"][:, 0:16], ap["convb_cols"])
        dma("S", self.cS, t["colbank"][:, 16:32], ap["dtb_cols"])
        dma("S", self.cS, t["colbank"][:, 32:48], ap["D_cols"])
        dma("S", self.cS, t["colbank"][:, 48:56], ap["normw_cols"])
        dma("S", self.cS, t["colbank"][:, 64:128], ap["convw_cols"])
        dma("S", self.cS, t["colbank"][:, 128:144], ap["sb1_cols"])
        dma("S", self.cS, t["colbank"][:, 144:160], ap["sb3_cols"])
        dma("S", self.cS, t["colbank"][:, 160:168], ap["sb2_cols"])
        dma("S", self.cS, t["colbank"][:, 168:232], ap["b1e_cols"])
        dma("S", self.cS, t["colbank"][:, 232:296], ap["b2e_cols"])
        dma("S", self.cS, t["hcol"][:], ap["hflag"])
        dma("S", self.cS, t["acols"][:], ap["alog_cols"])
        self.after(SC, self.cS)
        self.eS.inc(SC.activation(t["acols"][:], t["acols"][:], AF.Exp))
        self.after(V, self.eS, self.cS)
        self.eV.inc(V.tensor_scalar_mul(t["acols"][:], t["acols"][:], -1.0))
        self.eV.inc(V.tensor_scalar_mul(t["colbank"][:, 232:296], t["colbank"][:, 232:296], 0.5))
        self.eV.inc(V.memzero(t["zeros512"][:]))
        self.eV.inc(V.memzero(t["states"][:]))
        self.eV.inc(V.memzero(t["s0init"][:]))

    def p0_inw_prep(self):
        t, ap = self.t, self.ap
        V, G, SY = self.V, self.G, self.SY
        it = ap["inw_t"].rearrange("p (blk m) -> p blk m", m=128)
        for kc in range(8):
            for half in range(2):
                it_i = kc * 2 + half
                if it_i > 0:
                    self.after(SY, (self.eV, self._iw_vdone))
                self.dma("S", self.sIW, t["xt"][:, 0:2048],
                         ap["inw"][kc * 128:(kc + 1) * 128, half * 2048:(half + 1) * 2048])
                self.after(V, self.sIW, self.sIWst, self.cS)
                self._iw_vdone = self.eV.inc(V.tensor_scalar_mul(
                    t["inw_tile"][:], t["xt"][:, 0:2048], t["colbank"][:, 48 + kc:49 + kc]))
                self.after(G, (self.eV, self._iw_vdone))
                self.dma("G", self.sIWst, it[:, half * 128 + kc:half * 128 + 128:8, :],
                         t["inw_tile"][:].rearrange("p (c m) -> p c m", m=128))

    def p0_layernorm(self):
        t, ap = self.t, self.ap
        V, SC, TE, G, SY = self.V, self.SC, self.TE, self.G, self.SY
        eV, eS, eT = self.eV, self.eS, self.eT
        NT = TOKP // 128
        xnT3 = ap["xnT"].rearrange("(c p) tp -> c p tp", p=128)
        xdone = [0] * NT
        self.after(SY, (eV, self._iw_vdone))
        for i in range(min(2, NT)):
            self.dma("S", self.sX, t["xt"][:, (i % 2) * DM:((i % 2) + 1) * DM],
                     ap["x"][i * 128:(i + 1) * 128, :])
        for i in range(NT):
            buf = t["xt"][:, (i % 2) * DM:((i % 2) + 1) * DM]
            st = t["stats"]
            self.after(V, self.sX, eS, self.sSR)
            eV.inc(V.tensor_reduce(st[:, 0:1], buf, axis=AX.X, op=ALU.add))
            self.after(SC, self.sX, eV)
            eS.inc(SC.activation(t["sqscrap"][:], buf, AF.Square, accum_out=st[:, 1:2]))
            self.after(V, eS)
            eV.inc(V.tensor_scalar_mul(st[:, 2:3], st[:, 0:1], 1.0 / DM))
            self.after(V, eV)
            eV.inc(V.tensor_tensor(st[:, 3:4], st[:, 2:3], st[:, 2:3], ALU.mult))
            eV.inc(V.tensor_scalar(st[:, 4:5], st[:, 1:2], 1.0 / DM, EPS, ALU.mult, ALU.add))
            self.after(V, eV)
            eV.inc(V.tensor_tensor(st[:, 4:5], st[:, 4:5], st[:, 3:4], ALU.subtract))
            self.after(SC, eV)
            eS.inc(SC.activation(st[:, 5:6], st[:, 4:5], AF.Sqrt))
            self.after(V, eS)
            eV.inc(V.reciprocal(st[:, 6:7], st[:, 5:6]))
            self.after(V, eV)
            eV.inc(V.tensor_scalar(t["sqscrap"][:], buf, st[:, 2:3], st[:, 6:7],
                                   ALU.subtract, ALU.mult))
            self.after(V, eT, eV)
            xdone[i] = eV.inc(V.tensor_copy(t["xnb"][:], t["sqscrap"][:]))
            if i + 2 < NT:
                self.after(SY, (eV, xdone[i]))
                self.dma("S", self.sX, t["xt"][:, (i % 2) * DM:((i % 2) + 1) * DM],
                         ap["x"][(i + 2) * 128:(i + 3) * 128, :])
            self.after(SC, (eV, xdone[i]))
            eS.inc(SC.activation(t["sqscrap"][:], t["sqscrap"][:], AF.Square, accum_out=st[:, 7:8]))
            self.after(V, eS, self.sSR)
            eV.inc(V.tensor_scalar(st[:, 7:8], st[:, 7:8], 1.0 / DM, EPS, ALU.mult, ALU.add))
            self.after(SC, eV)
            eS.inc(SC.activation(st[:, 7:8], st[:, 7:8], AF.Sqrt))
            self.after(V, eS)
            srd = eV.inc(V.reciprocal(st[:, 7:8], st[:, 7:8]))
            self.after(G, (eV, srd))
            self.dma("G", self.sSR, ap["srow_d"][0:1, i * 128:(i + 1) * 128], st[:, 7:8])
            if i == 0:
                self.after(TE, self.cG)
            self.after(TE, (eV, xdone[i]), eV)
            for j in range(8):
                eT.inc(TE.transpose(t["ptr"][:, j * 128:(j + 1) * 128],
                                    t["xnb"][:, j * 128:(j + 1) * 128], t["ident"][:]))
            self.after(V, eT, self.sXN)
            self._tr_vdone = eV.inc(V.tensor_copy(t["trstage"][:], t["ptr"][:, 0:1024]))
            self.after(G, (eV, self._tr_vdone))
            self.dma("G", self.sXN, xnT3[:, :, i * 128:(i + 1) * 128].rearrange("c p m -> p c m"),
                     t["trstage"][:].rearrange("p (c m) -> p c m", m=128))
        if "dbg_xn" in self.debug:
            self.after(G, self.sXN)
            self.dma("G", self.dsem("dbgs"), ap["dbg_xn"], ap["xnT"])
        if "dbg_srow" in self.debug:
            self.after(G, self.sSR)
            self.dma("G", self.dsem("dbgs"), ap["dbg_srow"], ap["srow_d"])

    def _mamba_frontend(self, sl):
        t, ap = self.t, self.ap
        V, SC, TE, G, SY = self.V, self.SC, self.TE, self.G, self.SY
        eV, eS, eT = self.eV, self.eS, self.eT
        pb = [t[f"pb{i}"] for i in range(7)]
        col0 = HALO + sl * SL - 16
        self.after(SY, (self.eT, self._xnsl_tok), (self.eV, self._srep_tok))
        self.dma("S", self.sXS, t["xnsl"][:].rearrange("p (c m) -> p c m", m=SL + 16),
                 ap["xnT"].rearrange("(c p) tp -> c p tp", p=128)[:, :, col0:col0 + SL + 16]
                 .rearrange("c p m -> p c m"))
        self.dma("S", self.sSRL, t["srsl"][0:1, :], ap["srow_d"][0:1, col0:col0 + SL + 16])
        self.after(V, self.sSRL)
        srb = eV.inc(V.tensor_copy(t["srsl_bf"][0:1, :], t["srsl"][0:1, :]))
        self.after(TE, (eV, srb), eV)
        sr_mm = eT.inc(TE.matmul(pb[6][:, 0:SL], t["ones_row_sb"][:],
                                 t["srsl_bf"][0:1, 16:SL + 16], start=True, stop=True))
        self.after(V, (eT, sr_mm))
        sv1 = eV.inc(V.tensor_copy(t["srep"][:, 16:SL + 16], pb[6][:, 0:SL]))
        self.after(TE, (eV, sv1))
        sr_mm2 = eT.inc(TE.matmul(pb[6][:, 0:16], t["ones_row_sb"][:],
                                  t["srsl_bf"][0:1, 0:16], start=True, stop=True))
        self.after(V, (eT, sr_mm2))
        self._srep_tok = eV.inc(V.tensor_copy(t["srep"][:, 0:16], pb[6][:, 0:16]))

        for cpt in range(32):
            wbuf = cpt % 2
            wsem = self.sWa if wbuf == 0 else self.sWb
            self.after(SY, (eT, self._inw_tok[wbuf]))
            self.dma("S", wsem, t["inw_tile"][:, wbuf * 1024:(wbuf + 1) * 1024],
                     ap["inw_t"][:, cpt * 1024:(cpt + 1) * 1024])
            ppick = cpt % 2
            ps = pb[ppick]
            self.after(TE, wsem, self.sXS, (eV, self._psA_tok[ppick]))
            for kc in range(8):
                eT.inc(TE.matmul(ps[:, 0:SL],
                                 t["inw_tile"][:, wbuf * 1024 + kc * 128:wbuf * 1024 + (kc + 1) * 128],
                                 t["xnsl"][:, kc * (SL + 16) + 16:(kc + 1) * (SL + 16)],
                                 start=(kc == 0), stop=(kc == 7)))
            if cpt < 16:
                self.after(TE, (eV, self._pre_tok))
                for kc in range(8):
                    eT.inc(TE.matmul(pb[2][:, 0:16],
                                     t["inw_tile"][:, wbuf * 1024 + kc * 128:wbuf * 1024 + (kc + 1) * 128],
                                     t["xnsl"][:, kc * (SL + 16):kc * (SL + 16) + 16],
                                     start=(kc == 0), stop=(kc == 7)))
            self._inw_tok[wbuf] = eT.v
            if cpt < 16:
                ur = t["uraw"]
                self.after(V, eT, (eV, self._srep_tok))
                eV.inc(V.tensor_tensor(ur[:, 0:16], pb[2][:, 0:16], t["srep"][:, 0:16], ALU.mult))
                self._pre_tok = eV.v
                self.after(V, eV)
                self._psA_tok[ppick] = eV.inc(V.tensor_tensor(
                    ur[:, 16:16 + SL], ps[:, 0:SL], t["srep"][:, 16:16 + SL], ALU.mult))
                cw = lambda j: t["colbank"][:, 64 + cpt * 4 + j:64 + cpt * 4 + j + 1]
                ua, ub = t["uacc"][:, 0:SL], t["uacc"][:, SL:2 * SL]
                self.after(V, eV, eS)
                eV.inc(V.tensor_scalar(ua, ur[:, 13:13 + SL], cw(0),
                                       t["colbank"][:, cpt:cpt + 1], ALU.mult, ALU.add))
                self.after(V, eV)
                eV.inc(V.scalar_tensor_tensor(ub, ur[:, 14:14 + SL], cw(1), ua, ALU.mult, ALU.add))
                self.after(V, eV)
                eV.inc(V.scalar_tensor_tensor(ua, ur[:, 15:15 + SL], cw(2), ub, ALU.mult, ALU.add))
                self.after(V, eV)
                cvd = eV.inc(V.scalar_tensor_tensor(ub, ur[:, 16:16 + SL], cw(3), ua, ALU.mult, ALU.add))
                self.after(SC, (eV, cvd))
                eS.inc(SC.activation(t["ucb"][:, cpt * SL:(cpt + 1) * SL], ub, AF.Silu))
            else:
                j = cpt - 16
                self.after(V, eT, (eV, self._srep_tok))
                self._psA_tok[ppick] = eV.inc(V.tensor_tensor(
                    t["resb"][:, j * SL:(j + 1) * SL], ps[:, 0:SL],
                    t["srep"][:, 16:16 + SL], ALU.mult))
        self.after(TE, eS, (eV, self._pre_tok), self.cG)
        for jt in range(16):
            eT.inc(TE.matmul(pb[2][0:96, 0:SL], t["xproj_sb"][:, jt * 96:(jt + 1) * 96],
                             t["ucb"][:, jt * SL:(jt + 1) * SL], start=(jt == 0), stop=(jt == 15)))
        self.after(V, eT)
        xdv = eV.inc(V.tensor_copy(t["xdbl"][:], pb[2][0:96, 0:SL]))
        self.after(V, (eV, xdv))
        dtv = eV.inc(V.tensor_copy(t["dtb16"][:], pb[2][0:64, 0:SL]))
        self._pre_tok = dtv
        self.after(G, (eV, xdv))
        self.dma("G", self.sBC, t["Bbuf"][:], t["xdbl"][64:80, :])
        self.dma("G", self.sBC, t["Cbuf"][:], t["xdbl"][80:96, :])
        self.after(TE, self.sBC, (eV, self._brep_tok))
        b_mm = eT.inc(TE.matmul(pb[3][:, 0:SL], t["brep01_sb"][:], t["Bbuf"][:], start=True, stop=True))
        self.after(V, (eT, b_mm))
        brv = eV.inc(V.tensor_copy(t["Brep"][:], pb[3][:, 0:SL]))
        self.after(TE, (eV, brv))
        c_mm = eT.inc(TE.matmul(pb[3][:, 0:SL], t["brep01_sb"][:], t["Cbuf"][:], start=True, stop=True))
        self.after(V, (eT, c_mm))
        self._brep_tok = eV.inc(V.tensor_copy(t["Crep"][:], pb[3][:, 0:SL]))
        dprev = [eV.v, self._brep_tok]
        for jt in range(16):
            pp = pb[2] if jt % 2 == 0 else pb[3]
            self.after(TE, eS, (eV, dprev[jt % 2]))
            m = eT.inc(TE.matmul(pp[:, 0:SL], t["dtw_sb"][:, jt * 128:(jt + 1) * 128],
                                 t["dtb16"][:], start=True, stop=True))
            self.after(SC, (eT, m), eS)
            eS.inc(SC.activation(t["sps"][:], pp[:, 0:SL], AF.Exp,
                                 bias=t["colbank"][:, 16 + jt:17 + jt]))
            self.after(SC, eS)
            dsv = eS.inc(SC.activation(t["db"][:, jt * SL:(jt + 1) * SL], t["sps"][:], AF.Ln, bias=1.0))
            dprev[jt % 2] = 0
        self._delta_tok = eS.v

    def _mamba_scan_group(self, g, fixup):
        t, ap = self.t, self.ap
        V, SC, TE, G, SY = self.V, self.SC, self.TE, self.G, self.SY
        eV, eS, eT, eG = self.eV, self.eS, self.eT, self.eG
        pb = [t[f"pb{i}"] for i in range(7)]
        for j in range(16):
            T = 16 * g + j
            pr = pb[4 + (T % 2)]
            pu = pb[2 + (T % 2)]
            st01 = t["st01_sb"][:, j * 128:(j + 1) * 128]
            self.after(TE, (eS, self._delta_tok), eS)
            m1 = eT.inc(TE.matmul(pr[:, 0:SL], st01, t["db"][:, g * SL:(g + 1) * SL],
                                  start=True, stop=True))
            self.after(SC, (eT, m1), self.cS)
            dav = eS.inc(SC.activation(t["dA"][:, (T % 2) * SL:((T % 2) + 1) * SL], pr[:, 0:SL],
                                       AF.Exp, scale=t["acols"][:, T:T + 1]))
            if not fixup:
                self.after(TE, eV)
                m2 = eT.inc(TE.matmul(pu[:, 0:SL], st01, t["dub"][:, g * SL:(g + 1) * SL],
                                      start=True, stop=True))
                self.after(V, (eT, m2), (eV, self._brep_tok), eV)
                dbv = eV.inc(V.tensor_tensor(t["dBu"][:, (T % 2) * SL:((T % 2) + 1) * SL],
                                             pu[:, 0:SL], t["Brep"][:], ALU.mult))
                data1 = t["dBu"][:, (T % 2) * SL:((T % 2) + 1) * SL]
                init = t["states"][:, T:T + 1]
                self.after(V, (eS, dav), (eV, dbv), eG)
            else:
                data1 = t["zeros512"][:]
                init = t["s0init"][:, T:T + 1]
                self.after(V, (eS, dav), eG)
            scv = eV.inc(V.tensor_tensor_scan(t["sgrp"][:, j * SL:(j + 1) * SL],
                                              t["dA"][:, (T % 2) * SL:((T % 2) + 1) * SL],
                                              data1, init, ALU.mult, ALU.add))
            self.after(G, (eV, scv), (eV, self._brep_tok), eT)
            ycv = eG.inc(G.tensor_tensor(t["ycb"][:, (T % 2) * SL:((T % 2) + 1) * SL],
                                         t["sgrp"][:, j * SL:(j + 1) * SL], t["Crep"][:], ALU.mult))
            self.after(TE, (eG, ycv), eV)
            eT.inc(TE.matmul(pb[6][:, 0:SL], t["red01_sb"][:, j * 128:(j + 1) * 128],
                             t["ycb"][:, (T % 2) * SL:((T % 2) + 1) * SL],
                             start=(j == 0), stop=(j == 15)))
        if not fixup:
            self.after(V, eV, eG)
            self.eV.inc(V.tensor_copy(
                t["states"][:, 16 * g:16 * g + 16],
                t["sgrp"][:].rearrange("p (j m) -> p j m", m=SL)[:, :, SL - 1]))

    def _mamba_tail(self, sl, g, fixup):
        t, ap = self.t, self.ap
        V, SC, G = self.V, self.SC, self.G
        eV, eS, eT = self.eV, self.eS, self.eT
        pb6 = self.t["pb6"]
        if sl == 0 and not fixup:
            self.after(V, eT, self.sY0)
            vv = eV.inc(V.tensor_copy(t["stg512"][:], pb6[:, 0:SL]))
            self.after(G, (eV, vv))
            self.dma("G", self.sY0, ap["y0_d"][g], t["stg512"][:])
            return
        self.after(SC, eS)
        sil = eS.inc(SC.activation(t["silures"][:], t["resb"][:, g * SL:(g + 1) * SL], AF.Silu))
        self.after(V, eT, eV)
        if fixup:
            self.after(V, self.sY0L)
            vv = eV.inc(V.tensor_tensor(t["ytmp"][:], pb6[:, 0:SL], t["dub"][:, 0:SL], ALU.add))
            self.after(V, (eV, vv))
            vv = eV.inc(V.scalar_tensor_tensor(t["ytmp"][:], t["ucb"][:, g * SL:(g + 1) * SL],
                                               t["colbank"][:, 32 + g:33 + g], t["ytmp"][:],
                                               ALU.mult, ALU.add))
        else:
            vv = eV.inc(V.scalar_tensor_tensor(t["ytmp"][:], t["ucb"][:, g * SL:(g + 1) * SL],
                                               t["colbank"][:, 32 + g:33 + g], pb6[:, 0:SL],
                                               ALU.mult, ALU.add))
        self.after(V, (eV, vv), (eS, sil))
        self.eV.inc(V.tensor_tensor(t["ygat"][:, g * SL:(g + 1) * SL], t["ytmp"][:],
                                    t["silures"][:], ALU.mult))

    def _mamba_outproj(self, sl):
        t, ap = self.t, self.ap
        V, TE, G, SY, SC = self.V, self.TE, self.G, self.SY, self.SC
        eV, eT = self.eV, self.eT
        pb = [t[f"pb{i}"] for i in range(7)]
        col0 = HALO + sl * SL
        for dmt in range(8):
            wbuf = dmt % 2
            wsem = self.sOWa if wbuf == 0 else self.sOWb
            self.after(G, (eT, self._ow_tok[wbuf]))
            self.dma("G", wsem,
                     t["outw_tile"][:, wbuf * 2048:(wbuf + 1) * 2048].rearrange(
                         "p (gch m) -> p gch m", m=128),
                     ap["outw"].rearrange("(gch p) dm -> p gch dm", p=128)[:, :, dmt * 128:(dmt + 1) * 128])
            self.after(SY, (self.eV, self._xnres_tok))
            self.dma("S", self.sXR, t["xnres"][:, 0:SL],
                     ap["xnT"][dmt * 128:(dmt + 1) * 128, col0:col0 + SL])
            ps = pb[4 + (dmt % 2)]
            self.after(TE, wsem, eV, (self.eS, self._delta_tok))
            for gg in range(16):
                eT.inc(TE.matmul(ps[:, 0:SL],
                                 t["outw_tile"][:, wbuf * 2048 + gg * 128:wbuf * 2048 + (gg + 1) * 128],
                                 t["ygat"][:, gg * SL:(gg + 1) * SL],
                                 start=(gg == 0), stop=(gg == 15)))
            self._ow_tok[wbuf] = eT.v
            self.after(V, eT, self.sXR, self.sSM, self.sY0)
            self._xnres_tok = eV.inc(V.tensor_tensor(t["stg512"][:], ps[:, 0:SL],
                                                     t["xnres"][:, 0:SL], ALU.add))
            self.after(G, (eV, self._xnres_tok))
            self.dma("G", self.sSM, ap["smamba"][dmt * 128:(dmt + 1) * 128, sl * SL:(sl + 1) * SL],
                     t["stg512"][:])

    def l0_mamba(self):
        t, ap = self.t, self.ap
        V, G = self.V, self.G
        self._xnsl_tok = 0
        self._srep_tok = self.eV.v
        self._inw_tok = [0, 0]
        self._ow_tok = [0, 0]
        self._psA_tok = [0, 0]
        self._pre_tok = 0
        self._brep_tok = 0
        self._xnres_tok = 0
        self._delta_tok = 0
        self._y0rl_tok = 0
        self._moe_h1_tok = [0, 0]
        self._moe_hs_tok = [0, 0]
        self._moe_sw2_tok = 0
        self._moe_sh_evac = [0, 0]
        self._moe_he_tok = 0
        self._moe_w_tok = 0
        self._moe_gelu_tok = [0, 0]
        self._moe_y_tok = [0, 0]
        self._sg_evac = 0
        self._p5_xt_tok = 0
        self.phase_barrier(self.sIWst, self.sXN, self.sSR, self.sX, self.cS, self.cG,
                           self.eV, self.eS, self.eT)
        for sl in range(NSLICE):
            self._mamba_frontend(sl)
            self._xnsl_tok = self.eT.v
            for jt in range(16):
                self.after(V, (self.eS, self._delta_tok), self.eV, self.eT)
                self.eV.inc(V.tensor_tensor(t["dub"][:, jt * SL:(jt + 1) * SL],
                                            t["db"][:, jt * SL:(jt + 1) * SL],
                                            t["ucb"][:, jt * SL:(jt + 1) * SL], ALU.mult))
            for g in range(16):
                self._mamba_scan_group(g, fixup=False)
                self._mamba_tail(sl, g, fixup=False)
            if sl > 0:
                self._mamba_outproj(sl)
        self.after(G, self.eV)
        self.dma("G", self.sST, ap["ag_state_in"], t["states"][:])
        self.after(G, self.sST)
        self.ccs.inc(G.collective_compute(
            "AllGather", ALU.bypass, replica_groups=PAIR_GROUPS,
            ins=[self.d_ag_state_in.ap().opt()], outs=[self.d_ag_state_out.ap().opt()]))

    def l0_fixup_gate(self):
        t, ap = self.t, self.ap
        V, SC, TE, G, SY = self.V, self.SC, self.TE, self.G, self.SY
        eV, eS, eT = self.eV, self.eS, self.eT
        self.phase_barrier(self.eV, self.eS, self.eT, self.eG, self.dsem("sGA"),
                           self.dsem("sWS"), self.dsem("sWE"), self.dsem("sXR2"),
                           self.dsem("sXE"), self.dsem("sXL"))
        self.after(SY, self.ccs)
        self.dma("S", self.sSTL, t["dA"][:, 0:NDN], ap["ag_state_out"][0:128, :])
        self.after(V, self.sSTL, self.eG, eV)
        eV.inc(V.tensor_scalar_mul(t["s0init"][:], t["dA"][:, 0:NDN], t["hcol"][:, 0:1]))
        self._mamba_frontend(0)
        self._xnsl_tok = self.eT.v
        for g in range(16):
            self.after(self.SY, (self.eV, self._y0rl_tok))
            self.dma("S", self.sY0L, t["dub"][:, 0:SL], ap["y0_d"][g, :, :])
            self._mamba_scan_group(g, fixup=True)
            self._mamba_tail(0, g, fixup=True)
            self._y0rl_tok = self.eV.v
        self._mamba_outproj(0)
        if "dbg_s" in self.debug:
            self.after(self.G, self.sSM)
            self.dma("G", self.dsem("dbgs"), self.ap["dbg_s"], self.ap["smamba"])
        self.l0_gate()

    def l0_gate(self):
        pass

    def l0_moe(self):
        t, ap = self.t, self.ap
        V, SC, TE, G, SY = self.V, self.SC, self.TE, self.G, self.SY
        eV, eS, eT = self.eV, self.eS, self.eT
        pb = [t[f"pb{i}"] for i in range(7)]
        self.phase_barrier(self.eV, self.eS, self.eT, self.eG, self.sSM, self.sY0,
                           self.sXS, self.sWa, self.sWb, self.sOWa, self.sOWb, self.sXR)
        sXE = self.dsem("sXE")
        sXL = self.dsem("sXL")
        sWS = self.dsem("sWS")
        sGA = self.dsem("sGA")
        sXR2 = self.dsem("sXR2")
        sWE = self.dsem("sWE")
        xnT3 = ap["xnT"].rearrange("(c p) tp -> c p tp", p=128)
        xeP3 = ap["xeP"]

        stg_tok = eV.v
        for sl in range(NSLICE):
            self.after(SY, (eV, stg_tok))
            self.dma("S", sXL, t["xnsl"][:].rearrange("p (c m) -> p c m", m=SL + 16)[:, :, 0:SL],
                     xnT3[:, :, HALO + sl * SL:HALO + (sl + 1) * SL].rearrange("c p m -> p c m"))
            for r in range(8):
                self.after(V, sXL, self.dsem("sXE"))
                cp = eV.inc(V.tensor_copy(
                    t["stg512"][:].rearrange("p (c j) -> p c j", j=64),
                    t["xnsl"][:].rearrange("p (c m) -> p c m", m=SL + 16)[:, :, r:SL:8]))
                stg_tok = cp
                self.after(G, (eV, cp))
                self.dma("G", sXE, xeP3[r, :, :].rearrange("p (c m) -> p c m", m=SL)[:, :, sl * 64:(sl + 1) * 64],
                         t["stg512"][:].rearrange("p (c j) -> p c j", j=64))

        hblk = lambda hpt: t["hssP"][:, hpt * 1024:hpt * 1024 + 1024]
        sh_tok = [eT.v, eT.v]
        xl_tok = eT.v
        for q in range(4):
            r0, r1 = 2 * q, 2 * q + 1
            self.after(SY, (eT, xl_tok), sXE)
            self.dma("S", sXR2, t["xep0"][:], xeP3[r0, :, :])
            self.dma("S", sXR2, t["xep1"][:], xeP3[r1, :, :])
            for hpt in range(16):
                self.after(G, (eT, sh_tok[0]))
                self.dma("G", sWS, t["shwP"][:, 0:1024].rearrange("p (c m) -> p c m", m=128),
                         ap["sw1"].rearrange("(c p) hs -> p c hs", p=128)[:, :, hpt * 128:(hpt + 1) * 128])
                self.dma("G", sWS, t["shwP"][:, 1024:2048].rearrange("p (c m) -> p c m", m=128),
                         ap["sw3"].rearrange("(c p) hs -> p c hs", p=128)[:, :, hpt * 128:(hpt + 1) * 128])
                for ri, rr in enumerate((r0, r1)):
                    xe_t = t["xep0"] if ri == 0 else t["xep1"]
                    p1, p3 = pb[0 + ri], pb[2 + ri]
                    self.after(TE, sWS, sXR2, (eS, self._moe_h1_tok[ri]), (eV, self._moe_hs_tok[ri]))
                    for kc in range(8):
                        eT.inc(TE.matmul(p1[:, 0:SL], t["shwP"][:, kc * 128:(kc + 1) * 128],
                                         xe_t[:, kc * SL:(kc + 1) * SL], start=(kc == 0), stop=(kc == 7)))
                    for kc in range(8):
                        eT.inc(TE.matmul(p3[:, 0:SL], t["shwP"][:, 1024 + kc * 128:1024 + (kc + 1) * 128],
                                         xe_t[:, kc * SL:(kc + 1) * SL], start=(kc == 0), stop=(kc == 7)))
                    self.after(SC, eT, eV)
                    h1v = eS.inc(SC.activation(t["h13"][:], p1[:, 0:SL], AF.Silu,
                                               bias=t["colbank"][:, 128 + hpt:129 + hpt]))
                    self._moe_h1_tok[ri] = h1v
                    self.after(V, (eS, h1v))
                    self._moe_hs_tok[ri] = eV.inc(V.scalar_tensor_tensor(
                        hblk(hpt)[:, ri * SL:(ri + 1) * SL], p3[:, 0:SL],
                        t["colbank"][:, 144 + hpt:145 + hpt], t["h13"][:], ALU.add, ALU.mult))
                sh_tok[0] = eT.v
            for dmt in range(8):
                self.after(G, eT)
                self.dma("G", sWS, t["shwP"][:, 0:2048].rearrange("p (c m) -> p c m", m=128),
                         ap["sw2"].rearrange("(c p) dm -> p c dm", p=128)[:, :, dmt * 128:(dmt + 1) * 128])
                for ri, rr in enumerate((r0, r1)):
                    ps = pb[4 + ri]
                    self.after(TE, sWS, eV, (eV, self._moe_sh_evac[ri]))
                    for hpt in range(16):
                        eT.inc(TE.matmul(ps[:, 0:SL], t["shwP"][:, hpt * 128:(hpt + 1) * 128],
                                         hblk(hpt)[:, ri * SL:(ri + 1) * SL],
                                         start=(hpt == 0), stop=(hpt == 15)))
                    self._moe_sw2_tok = eT.v
                    self.after(V, eT, self.dsem("sGA"))
                    ev = eV.inc(V.tensor_scalar_add(t["stg512"][:], ps[:, 0:SL],
                                                    t["colbank"][:, 160 + dmt:161 + dmt]))
                    self._moe_sh_evac[ri] = ev
                    self.after(G, (eV, ev))
                    self.dma("G", sGA, ap["gacc_d"][dmt, :, rr * SL:(rr + 1) * SL], t["stg512"][:])
            xl_tok = eT.v

        wexp3 = t["wexp"][:].rearrange("p (b m) -> p b m", m=128)
        self._moe_he_tok = eT.v
        self._moe_w_tok = eT.v
        for e in range(NEXP):
            rr = [(3 * e) % 8, (3 * e + 3) % 8]
            self.after(SY, (eT, self._moe_he_tok))
            self.dma("S", sXR2, t["xe0"][:], xeP3[rr[0], :, :])
            self.dma("S", sXR2, t["xe1"][:], xeP3[rr[1], :, :])
            self.after(G, (eT, self._moe_w_tok))
            self.dma("G", sWE, t["wexp"][:].rearrange("p (ht kc m) -> p ht kc m", ht=8, kc=8),
                     ap["w1e"].rearrange("e (kc p) hr -> e p kc hr", p=128)[e]
                     .rearrange("p kc (ht m) -> p ht kc m", m=128))
            for ki in range(2):
                xe_t = t["xe0"] if ki == 0 else t["xe1"]
                he_t = t["he0"] if ki == 0 else t["he1"]
                for ht in range(8):
                    ps = pb[ht % 2]
                    self.after(TE, sWE, sXR2, (eS, self._moe_gelu_tok[ht % 2]))
                    for kc in range(8):
                        eT.inc(TE.matmul(ps[:, 0:SL], wexp3[:, ht * 8 + kc, :],
                                         xe_t[:, kc * SL:(kc + 1) * SL], start=(kc == 0), stop=(kc == 7)))
                    self.after(SC, eT)
                    self._moe_gelu_tok[ht % 2] = eS.inc(SC.activation(
                        he_t[:, ht * SL:(ht + 1) * SL], ps[:, 0:SL], AF.Gelu,
                        bias=t["colbank"][:, 168 + e * 8 + ht:169 + e * 8 + ht]))
            self._moe_he_tok = eT.v
            self.after(G, (eT, self._moe_he_tok))
            self.dma("G", sWE, t["wexp"][:].rearrange("p (dmt ht m) -> p dmt ht m", dmt=8, ht=8),
                     ap["w2e"].rearrange("e (ht p) dm -> e p ht dm", p=128)[e]
                     .rearrange("p ht (dmt m) -> p dmt ht m", m=128))
            for ki in range(2):
                he_t = t["he0"] if ki == 0 else t["he1"]
                for dmt in range(8):
                    ps = pb[4 + dmt % 2]
                    self.after(TE, sWE, eS, (eV, self._moe_y_tok[dmt % 2]))
                    for ht in range(8):
                        eT.inc(TE.matmul(ps[:, 0:SL], wexp3[:, dmt * 8 + ht, :],
                                         he_t[:, ht * SL:(ht + 1) * SL], start=(ht == 0), stop=(ht == 7)))
                    self.after(V, eT, sGA)
                    yv = eV.inc(V.tensor_scalar(t["stg512"][:], ps[:, 0:SL], 0.5,
                                                t["colbank"][:, 232 + e * 8 + dmt:233 + e * 8 + dmt],
                                                ALU.mult, ALU.add))
                    self._moe_y_tok[dmt % 2] = yv
                    self.after(G, (eV, yv))
                    self.G.dma_start(out=ap["gacc_d"][dmt, :, rr[ki] * SL:(rr[ki] + 1) * SL],
                                     in_=t["stg512"][:], accum_op=ALU.add).then_inc(sGA.h, 16)
                    sGA.v += 16
            self._moe_w_tok = eT.v
        self._moe_done = True

    def l0_gate(self):
        t, ap = self.t, self.ap
        V, SC, TE, G, SY = self.V, self.SC, self.TE, self.G, self.SY
        eV, eS, eT = self.eV, self.eS, self.eT
        self.phase_barrier(self.eV, self.eS, self.eT, self.eG,
                           self.dsem("sGA"), self.sSM, self.sY0L)
        sGL = self.dsem("sGL")
        sSG = self.dsem("sSG")
        sX1 = self.dsem("sX1")
        sX1S = self.dsem("sX1S")
        sXN1 = self.dsem("sXN1")
        self.dram("sgT", (TOK, DM), BF16)
        apsg = self.ap["sgT"]
        vA = eV.v
        for dmt in range(8):
            self.after(SY, (eV, vA))
            self.dma("S", sGL, t["gslice"][:], ap["gacc_d"][dmt, :, :])
            self.after(SC, sGL, (eV, vA), eT)
            gs = eS.inc(SC.activation(t["gsilu"][:], t["gslice"][:], AF.Silu, scale=0.5))
            for r in range(8):
                self.after(V, (eS, gs), self.dsem("dbgs"))
                eV.inc(V.tensor_copy(t["ssl"][:, 0:TOK][:, r:TOK:8], t["gsilu"][:, r * SL:(r + 1) * SL]))
            self.after(SY, (eS, gs))
            self.dma("S", sGL, t["gslice"][:], ap["smamba"][dmt * 128:(dmt + 1) * 128, :])
            self.after(V, sGL, eV)
            vA = eV.inc(V.tensor_tensor(t["gsilu"][:], t["gslice"][:], t["ssl"][:, 0:TOK], ALU.mult))
            if "dbg_g" in self.debug:
                self.after(G, eV)
                self.dma("G", self.dsem("dbgs"), ap["dbg_g"][dmt * 128:(dmt + 1) * 128, :], t["ssl"][:, 0:TOK])
            for tt in range(32):
                pcol = (tt % 8) * 128
                if tt % 8 == 0:
                    self.after(TE, (eV, vA), eV if tt else eV)
                    if tt > 0 or dmt > 0:
                        self.after(TE, (eV, self._sg_evac))
                eT.inc(TE.transpose(t["ptr"][:, pcol:pcol + 128],
                                    t["gsilu"][:, tt * 128:(tt + 1) * 128], t["ident"][:]))
                if tt % 8 == 7:
                    self.after(V, eT, sSG)
                    self._sg_evac = eV.inc(V.tensor_copy(t["trstage"][:], t["ptr"][:, 0:1024]))
                    self.after(G, (eV, self._sg_evac))
                    self.dma("G", sSG, apsg[(tt - 7) * 128:(tt + 1) * 128, dmt * 128:(dmt + 1) * 128]
                             .rearrange("(nt p) dm -> p nt dm", p=128),
                             t["trstage"][:].rearrange("p (nt m) -> p nt m", m=128))
        NTT = TOK // 128
        xnT1 = ap["xn1T_in"].rearrange("(c p) tp -> c p tp", p=128)
        for i in range(NTT):
            self.after(SY, (eV, self._p5_xt_tok), sSG)
            self.dma("S", sX1, t["xt"][:, 0:DM], ap["x"][HALO + i * 128:HALO + (i + 1) * 128, :])
            self.dma("S", sX1, t["xt"][:, DM:2 * DM].bitcast(BF16)[:, 0:DM] if False else t["xnb2"][:],
                     apsg[i * 128:(i + 1) * 128, :])
            st = t["stats"]
            self.after(V, sX1, sSG, eS, eV, sX1S, self.dsem("dbgs"))
            x1v = eV.inc(V.tensor_tensor(t["sqscrap"][:], t["xt"][:, 0:DM], t["xnb2"][:], ALU.add))
            self._p5_xt_tok = x1v
            self.after(G, (eV, x1v))
            self.dma("G", sX1S, ap["x1_d"][i * 128:(i + 1) * 128, :], t["sqscrap"][:])
            if self.upto == "l0":
                self.dma("G", sX1S, ap["out"][i * 128:(i + 1) * 128, :], t["sqscrap"][:])
            if "dbg_x1" in self.debug:
                self.dma("G", self.dsem("dbgs"), ap["dbg_x1"][i * 128:(i + 1) * 128, :], t["sqscrap"][:])
            self.after(V, (eV, x1v))
            eV.inc(V.tensor_reduce(st[:, 0:1], t["sqscrap"][:], axis=AX.X, op=ALU.add))
            self.after(SC, (eV, x1v), eV)
            eS.inc(SC.activation(t["xt"][:, DM:2 * DM], t["sqscrap"][:], AF.Square, accum_out=st[:, 1:2]))
            self.after(V, eS)
            eV.inc(V.tensor_scalar_mul(st[:, 2:3], st[:, 0:1], 1.0 / DM))
            self.after(V, eV)
            eV.inc(V.tensor_tensor(st[:, 3:4], st[:, 2:3], st[:, 2:3], ALU.mult))
            eV.inc(V.tensor_scalar(st[:, 4:5], st[:, 1:2], 1.0 / DM, EPS, ALU.mult, ALU.add))
            self.after(V, eV)
            eV.inc(V.tensor_tensor(st[:, 4:5], st[:, 4:5], st[:, 3:4], ALU.subtract))
            self.after(SC, eV)
            eS.inc(SC.activation(st[:, 5:6], st[:, 4:5], AF.Sqrt))
            self.after(V, eS)
            eV.inc(V.reciprocal(st[:, 6:7], st[:, 5:6]))
            self.after(V, eV, eT)
            xnv = eV.inc(V.tensor_scalar(t["xnb"][:], t["sqscrap"][:], st[:, 2:3], st[:, 6:7],
                                         ALU.subtract, ALU.mult))
            self.after(TE, (eV, xnv), eV)
            for j in range(8):
                eT.inc(TE.transpose(t["ptr"][:, j * 128:(j + 1) * 128],
                                    t["xnb"][:, j * 128:(j + 1) * 128], t["ident"][:]))
            self.after(V, eT, sXN1)
            trv = eV.inc(V.tensor_copy(t["trstage"][:], t["ptr"][:, 0:1024]))
            self.after(G, (eV, trv))
            self.dma("G", sXN1, xnT1[:, :, i * 128:(i + 1) * 128].rearrange("c p m -> p c m"),
                     t["trstage"][:].rearrange("p (c m) -> p c m", m=128))
        self.after(G, sXN1)
        self.ccs.inc(G.collective_compute(
            "AllGather", ALU.bypass, replica_groups=PAIR_GROUPS,
            ins=[self.d_xn1T_in.ap().opt()], outs=[self.d_xn1T_out.ap().opt()]))
        if "dbg_xn1" in self.debug:
            self.after(G, self.ccs)
            self.dma("G", self.dsem("dbgs"), ap["dbg_xn1"], ap["xn1T_out"])

    def _attn_body(self, blocks):
        t, ap = self.t, self.ap
        V, SC, TE, G, SY = self.V, self.SC, self.TE, self.G, self.SY
        eV, eS, eT = self.eV, self.eS, self.eT
        pb = [t[f"pb{i}"] for i in range(7)]
        sKV = self.dsem("sKV")
        sWQ = self.dsem("sWQ")
        sAC = self.dsem("sAC")
        sQB = self.dsem("sQB")
        xnpair = ap["xn1T_out"]

        def xn_cols(g0, n):
            slot = g0 // TOK
            c0 = g0 % TOK
            return xnpair.rearrange("(s c p) tp -> s c p tp", s=2, p=128)[slot, :, :, c0:c0 + n] \
                .rearrange("c p m -> p c m")

        kps = sorted({kp for blk in blocks for kp in blk[1]})
        pad_kps = [0] * (8 - len(kps))
        KT, VT, QT, XS = t["a_KT"], t["a_VT"], t["a_QT"], t["a_XS"]
        expS = t["a_expS"]
        astg = t["a_stgF"]
        wq, wkv = t["a_wq"], t["a_wkv"]
        self.after(self.G, (eT, self._at_w_tok))
        wt3 = ap["wqkvT"].rearrange("(c p) e -> p c e", p=128)
        self.dma("G", sWQ, wq[:].rearrange("p (c m) -> p c m", m=1024), wt3[:, :, 0:ADIM])
        self.dma("G", sWQ, wkv[:].rearrange("p (c m) -> p c m", m=2048), wt3[:, :, ADIM:3 * ADIM])
        def build_kv(kp):
            for ks in range(2):
                k0 = kp * 1024 + ks * 512
                self.after(SY, (eT, self._at_xs_tok), (eT, self._at_q_tok), sKV)
                self.dma("S", sKV, XS[:].rearrange("p (c m) -> p c m", m=512), xn_cols(k0, 512))
                for et in range(8):
                    ps = pb[5]
                    self.after(TE, sKV, sWQ, (eV, self._at_kv_evac))
                    for kc in range(8):
                        eT.inc(TE.matmul(ps[:, 0:512],
                                         wkv[:, kc * 2048 + et * 128:kc * 2048 + (et + 1) * 128],
                                         XS[:, kc * 512:(kc + 1) * 512],
                                         start=(kc == 0), stop=(kc == 7)))
                    self.after(V, eT)
                    self._at_kv_evac = eV.inc(V.tensor_scalar_add(
                        KT[:, et * 1024 + ks * 512:et * 1024 + (ks + 1) * 512], ps[:, 0:512],
                        t["colbank"][:, 304 + 8 + et:305 + 8 + et]))
                for ktl in range(4):
                    for vh in range(2):
                        ps = pb[6]
                        self.after(TE, (eV, self._at_kv_evac), eV)
                        for kc in range(8):
                            eT.inc(TE.matmul(
                                ps[:, 0:512],
                                XS[:, kc * 512 + ktl * 128:kc * 512 + (ktl + 1) * 128],
                                wkv[:, kc * 2048 + 1024 + vh * 512:kc * 2048 + 1024 + (vh + 1) * 512],
                                start=(kc == 0), stop=False))
                        eT.inc(TE.matmul(ps[:, 0:512], t["onesrow1"][:],
                                         t["bvrow"][0:1, vh * 512:(vh + 1) * 512],
                                         start=False, stop=True))
                        self.after(V, eT)
                        self._at_kv_evac = eV.inc(V.tensor_copy(
                            VT[:, (ks * 4 + ktl) * 1024 + vh * 512:
                               (ks * 4 + ktl) * 1024 + (vh + 1) * 512],
                            ps[:, 0:512]))
                self._at_xs_tok = eT.v

        for kp in kps:
            build_kv(kp)
            for blk_i, (g0b, kplist) in enumerate(blocks):
                if kp not in kplist:
                    continue
                first_kp = (kp == kplist[0])
                for q5 in range(4):
                    g0 = g0b + q5 * 512
                    self.after(SY, (eT, self._at_q_tok), (eT, self._at_xs_tok), sKV)
                    self.dma("S", sKV, XS[:].rearrange("p (c m) -> p c m", m=512), xn_cols(g0, 512))
                    for et in range(8):
                        ps = pb[5]
                        self.after(TE, sKV, (eV, self._at_q_evac))
                        for kc in range(8):
                            eT.inc(TE.matmul(ps[:, 0:512],
                                             wq[:, kc * 1024 + et * 128:kc * 1024 + (et + 1) * 128],
                                             XS[:, kc * 512:(kc + 1) * 512],
                                             start=(kc == 0), stop=(kc == 7)))
                        self.after(V, eT)
                        self._at_q_evac = eV.inc(V.tensor_scalar_add(
                            QT[:, et * 512:(et + 1) * 512], ps[:, 0:512],
                            t["colbank"][:, 304 + et:305 + et]))
                    self._at_q_tok = eT.v
                    for ktl in range(8):
                        k0 = kp * 1024 + ktl * 128
                        if k0 > g0 + 384:
                            continue
                        ps = pb[ktl % 2]
                        self.after(TE, (eV, self._at_q_evac), (eS, self._at_exp_tok[ktl % 2]),
                                   (eV, self._at_mask_tok[ktl % 2]))
                        for et in range(8):
                            eT.inc(TE.matmul(ps[:, 0:512],
                                             KT[:, et * 1024 + (k0 % 1024):et * 1024 + (k0 % 1024) + 128],
                                             QT[:, et * 512:(et + 1) * 512],
                                             start=(et == 0), stop=(et == 7)))
                        sdiag = (k0 - g0) // 128
                        if 0 <= sdiag < 4:
                            self.after(V, eT, self.cS)
                            self._at_mask_tok[ktl % 2] = eV.inc(V.tensor_tensor(
                                ps[:, sdiag * 128:(sdiag + 1) * 128],
                                ps[:, sdiag * 128:(sdiag + 1) * 128],
                                t["trilmask_sb"][:], ALU.add))
                        else:
                            self._at_mask_tok[ktl % 2] = eV.v
                        self.after(SC, eT, (eV, self._at_mask_tok[ktl % 2]), eS)
                        self._at_exp_tok[ktl % 2] = eS.inc(SC.activation(
                            expS[:, ktl * 512:(ktl + 1) * 512], ps[:, 0:512], AF.Exp,
                            scale=1.0 / 32.0, bias=t["colbank"][:, 440:441]))
                    for s in range(4):
                        q0 = g0 + s * 128
                        kt_list = [ktl for ktl in range(8) if kp * 1024 + ktl * 128 <= q0]
                        if not kt_list:
                            continue
                        self.after(TE, eS, (eV, self._at_pv_evac))
                        for ktl in kt_list:
                            st_ap = expS[:, ktl * 512 + s * 128:ktl * 512 + (s + 1) * 128]
                            first = (ktl == kt_list[0])
                            last = (ktl == kt_list[-1])
                            eT.inc(TE.matmul(pb[2][:, 0:512], st_ap, VT[:, ktl * 1024:ktl * 1024 + 512],
                                             start=first, stop=last))
                            eT.inc(TE.matmul(pb[3][:, 0:512], st_ap, VT[:, ktl * 1024 + 512:(ktl + 1) * 1024],
                                             start=first, stop=last))
                            eT.inc(TE.matmul(pb[4][:, 0:1], st_ap, t["onescol"][:, 0:1],
                                             start=first, stop=last))
                        self.after(V, eT, sAC)
                        eV.inc(V.tensor_copy(astg[:, 0:512], pb[2][:, 0:512]))
                        self.after(V, eV)
                        eV.inc(V.tensor_copy(astg[:, 512:1024], pb[3][:, 0:512]))
                        self.after(V, eV)
                        self._at_pv_evac = eV.inc(V.tensor_copy(astg[:, 1024:1025], pb[4][:, 0:1]))
                        self.after(G, (eV, self._at_pv_evac))
                        lrow = g0b // 2048
                        arows = ap["attacc"][(blk_i * 2048 + q5 * 512 + s * 128):
                                             (blk_i * 2048 + q5 * 512 + (s + 1) * 128), 0:1025]
                        if first_kp:
                            self.dma("G", sAC, arows, astg[:, 0:1025])
                        else:
                            self.G.dma_start(out=arows, in_=astg[:, 0:1025],
                                             accum_op=ALU.add).then_inc(sAC.h, 16)
                            sAC.v += 16
        for kp in pad_kps:
            build_kv(kp)

    def _attn_norm_out(self, blocks, h):
        t, ap = self.t, self.ap
        V, SC, G, SY = self.V, self.SC, self.G, self.SY
        eV, eS = self.eV, self.eS
        sAC = self.dsem("sAC")
        sS1 = self.dsem("sS1")
        astg = t["a_stgF"]
        for blk_i, (g0b, kplist) in enumerate(blocks):
            own = (g0b // TOK) == h
            for ss in range(16):
                self.after(SY, sAC, (eV, self._at_no_tok), sS1)
                self.dma("S", self.dsem("sNL"), astg[:, 0:1025],
                         ap["attacc"][blk_i * 2048 + ss * 128:blk_i * 2048 + (ss + 1) * 128, 0:1025])
                self.after(V, self.dsem("sNL"), eV)
                eV.inc(V.reciprocal(t["stats"][:, 0:1], astg[:, 1024:1025]))
                self.after(V, eV, sS1)
                nv = eV.inc(V.tensor_scalar_mul(astg[:, 1040:2064], astg[:, 0:1024], t["stats"][:, 0:1]))
                self._at_no_tok = nv
                self.after(G, (eV, nv))
                dst_row = (g0b % TOK) + ss * 128
                if own:
                    self.dma("G", sS1, ap["s1_d"][dst_row:dst_row + 128, :], astg[:, 1040:2064])
                else:
                    self.dma("G", sS1, ap["ag_s1_in"][ss * 128:(ss + 1) * 128, :], astg[:, 1040:2064])

    def l1_attn(self):
        t, ap = self.t, self.ap
        eV, eS, eT = self.eV, self.eS, self.eT
        self.phase_barrier(self.eV, self.eS, self.eT, self.eG, self.dsem("sGL"),
                           self.dsem("sSG"), self.dsem("sX1"), self.dsem("sX1S"),
                           self.dsem("sXN1"), self.dsem("dbgs"))
        self.dma("S", self.cS, t["colbank"][:, 304:304 + 24], ap["bqkv_cols"])
        self.dma("S", self.cS, t["trilmask_sb"][:], ap["trilmask_add"])
        self.phase_barrier(self.cS)
        self.eG.inc(self.G.memset(t["onescol"][:], 1.0))
        self.eG.inc(self.G.memset(t["onesrow1"][:], 1.0))
        self.eG.inc(self.G.memset(t["colbank"][:, 440:441], -8.0))
        self.dma("G", self.cG, t["bvrow"][:], ap["bqkv_vrow"])
        self.phase_barrier(self.cG, self.eG)
        self.after(self.G, self.ccs)
        self.after(self.SY, self.ccs)

        def reset_toks():
            self._at_w_tok = 0
            self._at_xs_tok = 0
            self._at_kv_evac = 0
            self._at_q_tok = 0
            self._at_q_evac = 0
            self._at_exp_tok = [0, 0]
            self._at_mask_tok = [0, 0]
            self._at_pv_evac = 0
            self._at_no_tok = 0
        reset_toks()

        body_blocks = [
            [(0, [0, 1]), (6144, [0, 1, 2, 3, 4, 5, 6, 7])],
            [(2048, [0, 1, 2, 3]), (4096, [0, 1, 2, 3, 4, 5])],
        ]
        engines = [self.V, self.SC, self.TE, self.G, self.SY]
        idx = []
        for e in engines:
            pid = e.partition_id()
            idx.append(e.scalar_reg_alu(ALU.bitwise_and, pid, 1))
        sems_before = None
        for body in self.nc.Switch(engines=engines, index=idx, n=2):
            snap = {k: s.v for k, s in self.__dict__.items() if isinstance(s, Sem)}
            snap.update({k: s.v for k, s in self._dyn_sems.items()})
            if body == 0:
                self._attn_body(body_blocks[0])
                self._attn_norm_out(body_blocks[0], 0)
                sems_before = ({k: s.v for k, s in self.__dict__.items() if isinstance(s, Sem)},
                               {k: s.v for k, s in self._dyn_sems.items()}, snap)
            else:
                b0_named, b0_dyn, snap0 = sems_before
                for k, s in self.__dict__.items():
                    if isinstance(s, Sem):
                        s.v = snap0.get(k, s.v)
                for k, s in self._dyn_sems.items():
                    s.v = snap0.get(k, 0)
                reset_toks()
                self._attn_body(body_blocks[1])
                self._attn_norm_out(body_blocks[1], 1)
                for k, s in self.__dict__.items():
                    if isinstance(s, Sem):
                        assert s.v == b0_named.get(k, s.v), f"sem {k} mismatch {s.v} vs {b0_named.get(k)}"
                for k, s in self._dyn_sems.items():
                    assert s.v == b0_dyn.get(k, s.v), f"dyn sem {k} mismatch"
        sS1 = self.dsem("sS1")
        self.after(self.G, sS1)
        self.ccs.inc(self.G.collective_compute(
            "AllGather", ALU.bypass, replica_groups=PAIR_GROUPS,
            ins=[self.d_ag_s1_in.ap().opt()], outs=[self.d_ag_s1_out.ap().opt()]))
        sS1L = self.dsem("sS1L")
        idx2 = [self.SY.scalar_reg_alu(ALU.bitwise_and, self.SY.partition_id(), 1)]
        self.after(self.SY, self.ccs)
        v0 = sS1L.v
        for body in self.nc.Switch(engines=[self.SY], index=idx2, n=2):
            slot = 1 - body
            self.SY.dma_start(out=ap["s1_d"][2048:4096, :],
                              in_=ap["ag_s1_out"][slot * 2048:(slot + 1) * 2048, :]).then_inc(sS1L.h, 16)
        sS1L.v = v0 + 16

    def _ffn_qknorm_rot(self, buf_off, tbl0, is_q, layer):
        t, ap = self.t, self.ap
        V, SC, TE, SY = self.V, self.SC, self.TE, self.SY
        eV, eS, eT = self.eV, self.eS, self.eT
        pb = [t[f"pb{i}"] for i in range(7)]
        PB = t["poolB"]
        sRT = self.dsem("sRT")
        self.after(SY, (self.eV, self._f_rt_tok))
        cs_ap = ap["rot_q_cs"] if is_q else ap["rot_k_cs"]
        sn_ap = ap["rot_q_sn"] if is_q else ap["rot_k_sn"]
        self.dma("S", sRT, t["f_rcs"][:], cs_ap[:, tbl0:tbl0 + 512])
        self.dma("S", sRT, t["f_rsn"][:], sn_ap[:, tbl0:tbl0 + 512])
        qscol = t["colbank"][:, 360 + layer:361 + layer] if is_q else \
            t["colbank"][:, 362 + layer:363 + layer]
        for pt in range(4):
            blk = PB[:, buf_off + pt * 512:buf_off + (pt + 1) * 512]
            tmp = t["f_kqtmp"]
            self.after(SC, eV, eS)
            sq = eS.inc(SC.activation(tmp[:, 0:512], blk, AF.Square))
            self.after(TE, (eS, sq), eV, eS)
            m1 = eT.inc(TE.matmul(pb[5][0:2, 0:512], t["hd01_sb"][:], tmp[:, 0:512],
                                  start=True, stop=True))
            self.after(SC, (eT, m1), eS)
            eS.inc(SC.activation(t["sps"][0:2, 0:512], pb[5][0:2, 0:512], AF.Sqrt))
            self.after(V, eS, eV)
            eV.inc(V.tensor_scalar_max(t["sps"][0:2, 0:512], t["sps"][0:2, 0:512], 1e-12))
            self.after(V, eV)
            eV.inc(V.reciprocal(t["sps"][0:2, 0:512], t["sps"][0:2, 0:512]))
            self.after(V, eV)
            rb = eV.inc(V.tensor_copy(tmp[0:2, 0:512], t["sps"][0:2, 0:512]))
            self.after(TE, (eV, rb))
            m2 = eT.inc(TE.matmul(pb[5][:, 0:512], t["hd01T_sb"][:], tmp[0:2, 0:512],
                                  start=True, stop=True))
            self.after(V, (eT, m2), eV)
            nv = eV.inc(V.scalar_tensor_tensor(blk, blk, qscol, pb[5][:, 0:512],
                                               ALU.mult, ALU.mult))
            self.after(TE, (eV, nv), eV)
            m3 = eT.inc(TE.matmul(pb[6][:, 0:512], t["swap64_sb"][:], blk, start=True, stop=True))
            self.after(V, sRT, eV)
            c1 = eV.inc(V.tensor_tensor(tmp[:, 0:512], blk, t["f_rcs"][:], ALU.mult))
            self.after(V, (eT, m3), (eV, c1))
            c2 = eV.inc(V.tensor_tensor(tmp[:, 512:1024], pb[6][:, 0:512], t["f_rsn"][:], ALU.mult))
            self.after(V, (eV, c2), eV)
            self._f_rt_tok = eV.inc(V.tensor_tensor(blk, tmp[:, 0:512], tmp[:, 512:1024], ALU.add))

    def _ffn_qkv_window(self, layer, w):
        t, ap = self.t, self.ap
        V, TE, SY = self.V, self.TE, self.SY
        eV, eT = self.eV, self.eT
        pb = [t[f"pb{i}"] for i in range(7)]
        PB = t["poolB"]
        sXW = self.dsem("sXW")
        t0 = w * 512
        self.after(SY, (eT, self._f_xw_tok), sXW)
        for c in range(8):
            off = 513 + t0 if c < 4 else 512 + t0
            self.dma("S", sXW, t["f_xw"][:, c * 1056:c * 1056 + 512],
                     ap["xf_d"][c * 128:(c + 1) * 128, off:off + 512])
        kbuf = 2048 + ((w + 2) % 2) * 2048
        vbuf = 6144 + ((w + 2) % 2) * 2048
        wq = t["f_wqkv"]
        for pt in range(4):
            ps = pb[5]
            self.after(TE, sXW, self.dsem("sWQ2"), (eV, self._f_kq_evac))
            for c in range(8):
                eT.inc(TE.matmul(ps[:, 0:512],
                                 wq[:, c * 1536 + 512 + pt * 128:c * 1536 + 512 + (pt + 1) * 128],
                                 t["f_xw"][:, c * 1056:c * 1056 + 512],
                                 start=(c == 0), stop=(c == 7)))
            self.after(V, eT)
            self._f_kq_evac = eV.inc(V.tensor_copy(PB[:, kbuf + pt * 512:kbuf + (pt + 1) * 512],
                                                   ps[:, 0:512]))
        for ktl in range(4):
            ps = pb[6]
            self.after(TE, (eV, self._f_kq_evac), eV)
            for c in range(8):
                eT.inc(TE.matmul(ps[:, 0:512],
                                 t["f_xw"][:, c * 1056 + ktl * 128:c * 1056 + (ktl + 1) * 128],
                                 wq[:, c * 1536 + 1024:c * 1536 + 1536],
                                 start=(c == 0), stop=(c == 7)))
            self.after(V, eT)
            self._f_kq_evac = eV.inc(V.tensor_copy(PB[:, vbuf + ktl * 512:vbuf + (ktl + 1) * 512],
                                                   ps[:, 0:512]))
        self._ffn_qknorm_rot(kbuf, (w + 1) * 512, False, layer)
        if w >= 0:
            for pt in range(4):
                ps = pb[5]
                self.after(TE, sXW, (eV, self._f_kq_evac), eV)
                for c in range(8):
                    eT.inc(TE.matmul(ps[:, 0:512],
                                     wq[:, c * 1536 + pt * 128:c * 1536 + (pt + 1) * 128],
                                     t["f_xw"][:, c * 1056:c * 1056 + 512],
                                     start=(c == 0), stop=(c == 7)))
                self.after(V, eT)
                self._f_kq_evac = eV.inc(V.tensor_copy(PB[:, pt * 512:(pt + 1) * 512], ps[:, 0:512]))
            self._ffn_qknorm_rot(0, (w + 1) * 512, True, layer)
        self._f_xw_tok = eT.v

    def _ffn_window_attn(self, layer, w):
        t, ap = self.t, self.ap
        V, SC, TE = self.V, self.SC, self.TE
        eV, eS, eT = self.eV, self.eS, self.eT
        pb = [t[f"pb{i}"] for i in range(7)]
        PB = t["poolB"]
        mask_off = 55552 if w == 0 else 59648
        for h in range(HEADS):
            pt = h // 2
            po = (h % 2) * 64
            for kwin in range(2):
                kb = 2048 + ((w - 1 + kwin + 2) % 2) * 2048
                for ktl in range(4):
                    kt_g = kwin * 4 + ktl
                    ps = pb[kt_g % 2]
                    self.after(TE, eS, (eV, self._f_msk))
                    eT.inc(TE.matmul(
                        ps[:, 0:512],
                        PB[po:po + 64, kb + pt * 512 + ktl * 128:kb + pt * 512 + (ktl + 1) * 128],
                        PB[po:po + 64, pt * 512:(pt + 1) * 512],
                        start=True, stop=True))
                    self.after(SC, eT, eS)
                    ex = eS.inc(SC.activation(t["f_expS"][:, kt_g * 512:(kt_g + 1) * 512],
                                              ps[:, 0:512], AF.Exp))
                    self.after(V, (eS, ex), eT)
                    self._f_msk = eV.inc(V.tensor_tensor(
                        t["f_expS"][:, kt_g * 512:(kt_g + 1) * 512],
                        t["f_expS"][:, kt_g * 512:(kt_g + 1) * 512],
                        PB[:, mask_off + kt_g * 512:mask_off + (kt_g + 1) * 512], ALU.mult))
            self.after(TE, (eV, self._f_msk), eV, eS)
            for kt_g in range(8):
                vb = 6144 + ((w - 1 + (kt_g // 4) + 2) % 2) * 2048
                vtile = PB[:, vb + (kt_g % 4) * 512:vb + (kt_g % 4) * 512 + 512]
                eT.inc(TE.matmul(pb[2][0:64, 0:512], vtile[:, 64 * h:64 * h + 64],
                                 t["f_expS"][:, kt_g * 512:(kt_g + 1) * 512],
                                 start=(kt_g == 0), stop=(kt_g == 7)))
                eT.inc(TE.matmul(pb[3][0:1, 0:512], t["onescol"][:, 0:1],
                                 t["f_expS"][:, kt_g * 512:(kt_g + 1) * 512],
                                 start=(kt_g == 0), stop=(kt_g == 7)))
            self.after(V, eT, eV)
            ov = eV.inc(V.tensor_copy(t["f_kqtmp"][0:64, 0:512], pb[2][0:64, 0:512]))
            self.after(V, eV)
            rv = eV.inc(V.reciprocal(t["sps"][0:1, 0:512], pb[3][0:1, 0:512]))
            self.after(V, (eV, rv))
            rb = eV.inc(V.tensor_copy(t["f_kqtmp"][0:1, 512:1024], t["sps"][0:1, 0:512]))
            self.after(TE, (eV, rb), eV)
            mb = eT.inc(TE.matmul(pb[4][0:64, 0:512], t["hd01T_sb"][0:1, 0:64],
                                  t["f_kqtmp"][0:1, 512:1024], start=True, stop=True))
            self.after(V, (eT, mb), (eV, ov))
            self._f_o_tok = eV.inc(V.tensor_tensor(t["f_oh"][0:64, h * 512:(h + 1) * 512],
                                                   t["f_kqtmp"][0:64, 0:512],
                                                   pb[4][0:64, 0:512], ALU.mult))
        ub = 18432 + (w % 2) * 4096
        for dmt in range(8):
            ps = pb[5 + dmt % 2]
            self.after(TE, (eV, self._f_o_tok), (eV, self._f_upd_evac), eV)
            for h in range(HEADS):
                eT.inc(TE.matmul(ps[:, 0:512],
                                 t["f_wo"][0:64, h * 1024 + dmt * 128:h * 1024 + (dmt + 1) * 128],
                                 t["f_oh"][0:64, h * 512:(h + 1) * 512],
                                 start=(h == 0), stop=(h == 7)))
            self.after(V, eT, self.dsem("sUP"))
            self._f_upd_evac = eV.inc(V.tensor_copy(PB[:, ub + dmt * 512:ub + (dmt + 1) * 512],
                                                    ps[:, 0:512]))

    def _ffn_apply_update(self, w):
        t, ap = self.t, self.ap
        V, SY = self.V, self.SY
        eV = self.eV
        sUP = self.dsem("sUP")
        sUR = self.dsem("sUR")
        ub = 18432 + (w % 2) * 4096
        ur = 18432 + ((w + 1) % 2) * 4096
        self.after(SY, self.dsem("sXW"), sUP, (eV, self._f_upd_evac), sUR)
        for c in range(8):
            self.dma("S", sUR, t["poolB"][:, ur + c * 512:ur + (c + 1) * 512],
                     ap["xf_d"][c * 128:(c + 1) * 128, 513 + w * 512:513 + (w + 1) * 512])
        self.after(V, sUR, eV)
        addtok = 0
        for c in range(8):
            addtok = eV.inc(V.tensor_tensor(
                t["poolB"][:, ur + c * 512:ur + (c + 1) * 512],
                t["poolB"][:, ur + c * 512:ur + (c + 1) * 512],
                t["poolB"][:, ub + c * 512:ub + (c + 1) * 512], ALU.add))
        self.after(self.G, (eV, addtok))
        for c in range(8):
            self.dma("G", sUP, ap["xf_d"][c * 128:(c + 1) * 128, 513 + w * 512:513 + (w + 1) * 512],
                     t["poolB"][:, ur + c * 512:ur + (c + 1) * 512])

    def _ffn_mlp(self, layer):
        t, ap = self.t, self.ap
        V, SC, TE, G, SY = self.V, self.SC, self.TE, self.G, self.SY
        eV, eS, eT = self.eV, self.eS, self.eT
        pb = [t[f"pb{i}"] for i in range(7)]
        self._f_w1_btok = [eT.v, eT.v]
        self._f_w2_btok = [eT.v, eT.v]
        sXC = self.dsem("sXC")
        sW1 = self.dsem("sW1")
        sW2 = self.dsem("sW2")
        sUP = self.dsem("sUP")
        for tc in range(4):
            t0 = tc * 1024
            self.after(SY, (eT, self._f_xc_tok), sXC, sUP)
            for c in range(8):
                off = (513 if c < 4 else 512) + t0
                ncol = 1024 if c < 4 else 1025
                self.dma("S", sXC, t["f_xc"][:, c * 1056:c * 1056 + ncol],
                         ap["xf_d"][c * 128:(c + 1) * 128, off:off + ncol])
            self.after(V, sXC, eV)
            for c in range(4, 8):
                if tc > 0:
                    eV.inc(V.tensor_copy(
                        t["f_xc"][:, c * 1056:c * 1056 + 1],
                        t["f_bst"][:, ((tc + 1) % 2) * 4 + c - 4:((tc + 1) % 2) * 4 + c - 3]))
                eV.inc(V.tensor_copy(
                    t["f_bst"][:, (tc % 2) * 4 + c - 4:(tc % 2) * 4 + c - 3],
                    t["f_xc"][:, c * 1056 + 1024:c * 1056 + 1025]))
            for half in range(2):
                self.after(TE, sXC, eV, eS)
                for c in range(8):
                    eT.inc(TE.matmul(pb[5][0:1, 0:512], t["onescol"][:, 0:1],
                                     t["f_xc"][:, c * 1056 + half * 512:c * 1056 + half * 512 + 512],
                                     start=(c == 0), stop=(c == 7)))
                for c in range(8):
                    self.after(SC, eT, eS)
                    eS.inc(SC.activation(t["f_kqtmp"][:, 0:512],
                                         t["f_xc"][:, c * 1056 + half * 512:c * 1056 + half * 512 + 512],
                                         AF.Square))
                    self.after(TE, eS)
                    eT.inc(TE.matmul(pb[6][0:1, 0:512], t["onescol"][:, 0:1],
                                     t["f_kqtmp"][:, 0:512], start=(c == 0), stop=(c == 7)))
                ua = t["uacc"]
                self.after(V, eT, eV)
                eV.inc(V.tensor_scalar_mul(t["sps"][0:1, 0:512], pb[5][0:1, 0:512], 1.0 / DM))
                self.after(V, eV)
                eV.inc(V.tensor_tensor(ua[0:1, 0:512], t["sps"][0:1, 0:512],
                                       t["sps"][0:1, 0:512], ALU.mult))
                eV.inc(V.tensor_scalar(ua[0:1, 512:1024], pb[6][0:1, 0:512], 1.0 / DM, EPS,
                                       ALU.mult, ALU.add))
                self.after(V, eV)
                eV.inc(V.tensor_tensor(ua[0:1, 512:1024], ua[0:1, 512:1024],
                                       ua[0:1, 0:512], ALU.subtract))
                self.after(SC, eV, eS)
                eS.inc(SC.activation(ua[0:1, 512:1024], ua[0:1, 512:1024], AF.Sqrt))
                self.after(V, eS, eV)
                eV.inc(V.reciprocal(ua[0:1, 512:1024], ua[0:1, 512:1024]))
                self.after(V, eV)
                bv = eV.inc(V.tensor_copy(t["f_kqtmp"][0:1, 0:512], t["sps"][0:1, 0:512]))
                self.after(V, eV)
                bv2 = eV.inc(V.tensor_copy(t["f_kqtmp"][0:1, 512:1024], ua[0:1, 512:1024]))
                self.after(TE, (eV, bv2), eV)
                eT.inc(TE.matmul(pb[5][:, 0:512], t["ones_row_sb"][:], t["f_kqtmp"][0:1, 0:512],
                                 start=True, stop=True))
                mm2 = eT.inc(TE.matmul(pb[6][:, 0:512], t["ones_row_sb"][:],
                                       t["f_kqtmp"][0:1, 512:1024], start=True, stop=True))
                self.after(V, (eT, mm2))
                eV.inc(V.tensor_copy(t["f_mrep"][:], pb[5][:, 0:512]))
                self.after(V, eV)
                mrv = eV.inc(V.tensor_copy(t["f_rrep"][:], pb[6][:, 0:512]))
                for c in range(8):
                    self.after(V, (eV, mrv), eV)
                    eV.inc(V.tensor_tensor(t["f_kqtmp"][:, 0:512],
                                           t["f_xc"][:, c * 1056 + half * 512:c * 1056 + half * 512 + 512],
                                           t["f_mrep"][:], ALU.subtract))
                    self.after(V, eV)
                    eV.inc(V.tensor_tensor(t["f_kqtmp"][:, 0:512], t["f_kqtmp"][:, 0:512],
                                           t["f_rrep"][:], ALU.mult))
                    self.after(V, eV)
                    eV.inc(V.tensor_scalar(
                        t["f_hln"][:, c * 1024 + half * 512:c * 1024 + half * 512 + 512],
                        t["f_kqtmp"][:, 0:512],
                        t["colbank"][:, 328 + layer * 8 + c:329 + layer * 8 + c],
                        t["colbank"][:, 344 + layer * 8 + c:345 + layer * 8 + c],
                        ALU.mult, ALU.add))
            PB = t["poolB"]
            for ht in range(22):
                wb = 30720 if ht % 2 == 0 else 47360
                sw = self.dsem("sW1e") if ht % 2 == 0 else self.dsem("sW1o")
                self.after(G, (eT, self._f_w1_btok[ht % 2]))
                self.dma("G", sw, PB[:, wb:wb + 1024].rearrange("p (c m) -> p c m", m=128),
                         ap["w1a_bf"].rearrange("l (c p) hh -> l p c hh", p=128)[layer, :, :, ht * 128:(ht + 1) * 128])
                self.dma("G", sw, PB[:, wb + 1024:wb + 2048].rearrange("p (c m) -> p c m", m=128),
                         ap["w1g_bf"].rearrange("l (c p) hh -> l p c hh", p=128)[layer, :, :, ht * 128:(ht + 1) * 128])
                for half in range(2):
                    pa, pg = pb[5], pb[6]
                    self.after(TE, sw, eV, eS)
                    for c in range(8):
                        eT.inc(TE.matmul(pa[:, 0:512], PB[:, wb + c * 128:wb + (c + 1) * 128],
                                         t["f_hln"][:, c * 1024 + half * 512:c * 1024 + half * 512 + 512],
                                         start=(c == 0), stop=(c == 7)))
                    for c in range(8):
                        eT.inc(TE.matmul(pg[:, 0:512], PB[:, wb + 1024 + c * 128:wb + 1024 + (c + 1) * 128],
                                         t["f_hln"][:, c * 1024 + half * 512:c * 1024 + half * 512 + 512],
                                         start=(c == 0), stop=(c == 7)))
                    self.after(SC, eT, eS)
                    gl = eS.inc(SC.activation(t["f_kqtmp"][:, 0:512], pg[:, 0:512], AF.Gelu,
                                              bias=t["colbank"][:, 408 + layer * 22 + ht:409 + layer * 22 + ht]))
                    self.after(V, (eS, gl), eV)
                    eV.inc(V.scalar_tensor_tensor(
                        t["f_h2"][:, ht * 1024 + half * 512:ht * 1024 + half * 512 + 512],
                        pa[:, 0:512], t["colbank"][:, 364 + layer * 22 + ht:365 + layer * 22 + ht],
                        t["f_kqtmp"][:, 0:512], ALU.add, ALU.mult))
                self._f_w1_btok[ht % 2] = eT.v
            h2v = eV.v
            for dmt in range(8):
                wb2 = 33792 if dmt % 2 == 0 else 50432
                sw2 = self.dsem("sW2e") if dmt % 2 == 0 else self.dsem("sW2o")
                self.after(G, (eT, self._f_w2_btok[dmt % 2]))
                self.dma("G", sw2, PB[:, wb2:wb2 + 2816].rearrange("p (hc m) -> p hc m", m=128),
                         ap["w2_bf"].rearrange("l (hc p) dm -> l p hc dm", p=128)[layer, :, :, dmt * 128:(dmt + 1) * 128])
                for half in range(2):
                    ps = pb[dmt % 2]
                    self.after(TE, sw2, (eV, h2v), (eV, self._f_w2e_tok[dmt % 2]))
                    for ht in range(22):
                        eT.inc(TE.matmul(ps[:, 0:512], PB[:, wb2 + ht * 128:wb2 + (ht + 1) * 128],
                                         t["f_h2"][:, ht * 1024 + half * 512:ht * 1024 + half * 512 + 512],
                                         start=(ht == 0), stop=(ht == 21)))
                    self._f_w2_btok[dmt % 2] = eT.v
                    self.after(V, eT, sUP)
                    uv = eV.inc(V.tensor_scalar_add(
                        t["f_kqtmp"][:, 0:512], ps[:, 0:512],
                        t["colbank"][:, 452 + layer * 8 + dmt:453 + layer * 8 + dmt]))
                    self._f_w2e_tok[dmt % 2] = uv
                    cc = dmt * 1056 + half * 512 + (0 if dmt < 4 else 1)
                    self.after(V, eV)
                    uv2 = eV.inc(V.tensor_tensor(
                        t["f_xc"][:, cc:cc + 512], t["f_xc"][:, cc:cc + 512],
                        t["f_kqtmp"][:, 0:512], ALU.add))
                    self.after(G, (eV, uv2))
                    self.dma("G", sUP,
                             ap["xf_d"][dmt * 128:(dmt + 1) * 128,
                                        513 + t0 + half * 512:513 + t0 + half * 512 + 512],
                             t["f_xc"][:, cc:cc + 512])
            self._f_xc_tok = eT.v

    def _ffn_halo_bounce(self, src_ap_fn, dst_col0, ncols, mult_flag):
        t, ap = self.t, self.ap
        V, SY, G = self.V, self.SY, self.G
        eV = self.eV
        sHB = self.dsem("sHB")
        sHBl = self.dsem("sHBl")
        assert not (mult_flag and ncols > 1056), "staged copy would overlap src"
        for c in range(8):
            self.after(SY, (eV, self._f_hb_tok), sHB)
            self.dma("S", sHBl, t["f_xw"][:, 0:ncols], src_ap_fn(c))
            if ncols > 1056:
                self.after(G, sHBl)
                self.dma("G", sHB, ap["xf_d"][c * 128:(c + 1) * 128, dst_col0:dst_col0 + ncols],
                         t["f_xw"][:, 0:ncols])
                continue
            self.after(V, sHBl, eV)
            if mult_flag:
                hv = eV.inc(V.tensor_scalar_mul(t["f_xw"][:, 1056:1056 + ncols],
                                                t["f_xw"][:, 0:ncols], t["hcol"][:, 0:1]))
            else:
                hv = eV.inc(V.tensor_copy(t["f_xw"][:, 1056:1056 + ncols], t["f_xw"][:, 0:ncols]))
            self._f_hb_tok = hv
            self.after(G, (eV, hv))
            self.dma("G", sHB, ap["xf_d"][c * 128:(c + 1) * 128, dst_col0:dst_col0 + ncols],
                     t["f_xw"][:, 1056:1056 + ncols])

    def l1_ffn(self):
        t, ap = self.t, self.ap
        V, SC, TE, G, SY = self.V, self.SC, self.TE, self.G, self.SY
        eV, eS, eT = self.eV, self.eS, self.eT
        self.phase_barrier(self.eV, self.eS, self.eT, self.eG, self.dsem("sAC"),
                           self.dsem("sS1"), self.dsem("sS1L"), self.dsem("sNL"),
                           self.dsem("sKV"), self.dsem("sWQ"))
        self.dma("S", self.cS, t["colbank"][:, 328:344], ap["f_lng_cols"])
        self.dma("S", self.cS, t["colbank"][:, 344:360], ap["f_lnb_cols"])
        self.dma("S", self.cS, t["colbank"][:, 360:362], ap["f_qs_cols"])
        self.dma("S", self.cS, t["colbank"][:, 362:364], ap["f_ks_cols"])
        self.dma("S", self.cS, t["colbank"][:, 364:408], ap["f_b1a_cols"])
        self.dma("S", self.cS, t["colbank"][:, 408:452], ap["f_b1g_cols"])
        self.dma("S", self.cS, t["colbank"][:, 452:468], ap["f_b2_cols"])
        self.dma("G", self.cG, t["hd01_sb"][:], ap["hd01"])
        self.dma("G", self.cG, t["hd01T_sb"][:], ap["hd01T"])
        self.dma("G", self.cG, t["swap64_sb"][:], ap["swap64"])
        self.dma("G", self.cG, t["poolB"][:, 55552:55552 + 4096].rearrange("p (kt m) -> p kt m", m=512),
                 ap["lmask_first"].rearrange("(kt p) q -> p kt q", p=128))
        self.dma("G", self.cG, t["poolB"][:, 59648:59648 + 4096].rearrange("p (kt m) -> p kt m", m=512),
                 ap["lmask_rest"].rearrange("(kt p) q -> p kt q", p=128))
        self.phase_barrier(self.cS, self.cG)
        zv = self.eG.inc(G.memset(t["stg512"][:], 0.0))
        self.after(G, (self.eG, zv))
        for c in range(8):
            self.dma("G", self.dsem("sZF"), ap["ag_ft_in"][c * 128:(c + 1) * 128, 0:512],
                     t["stg512"][:])
            self.dma("G", self.dsem("sZF"), ap["ag_ft_in"][c * 128:(c + 1) * 128, 512:516],
                     t["stg512"][:, 0:4])
        self._f_wd_tok = 0
        self._f_rt_tok = 0
        self._f_xw_tok = 0
        self._f_kq_evac = 0
        self._f_msk = 0
        self._f_o_tok = 0
        self._f_upd_evac = 0
        self._f_xc_tok = 0
        self._f_w1_tok = 0
        self._f_w2_tok = 0
        self._f_w2e_tok = [0, 0]
        self._f_hb_tok = 0
        sWD = self.dsem("sWD")
        for l in range(2):
            for src_name, dst_name in (("f_w1a", "w1a_bf"), ("f_w1g", "w1g_bf")):
                for c in range(8):
                    self.after(G, sWD)
                    self.dma("G", sWD, t["f_h2"][:, 0:2816], ap[src_name][l, c * 128:(c + 1) * 128, :])
                    self.after(G, sWD)
                    self.dma("G", sWD, ap[dst_name][l, c * 128:(c + 1) * 128, :], t["f_h2"][:, 0:2816])
            for c in range(22):
                self.after(G, sWD)
                self.dma("G", sWD, t["f_h2"][:, 0:1024], ap["f_w2p"][l, c * 128:(c + 1) * 128, :])
                self.after(G, sWD)
                self.dma("G", sWD, ap["w2_bf"][l, c * 128:(c + 1) * 128, :], t["f_h2"][:, 0:1024])
        self.after(G, sWD)

        for layer in range(2):
            self.after(G, eT, self.dsem("sUP"))
            self.dma("G", self.dsem("sWQ2"), t["f_wqkv"][:].rearrange("p (c m) -> p c m", m=1536),
                     ap["f_wqkv"].rearrange("l (c p) e -> l p c e", p=128)[layer])
            self.dma("G", self.dsem("sWQ2"), t["f_wo"][0:64, :].rearrange("p (h m) -> p h m", m=1024),
                     ap["f_wo"].rearrange("l (h p) dm -> l p h dm", p=64)[layer])
            if layer == 0:
                self._ffn_halo_bounce(
                    lambda c: ap["xn1T_in"][c * 128:(c + 1) * 128, 0:TOK], 513, 4096, False)
                self._ffn_halo_bounce(
                    lambda c: ap["xn1T_out"][c * 128:(c + 1) * 128, 3583:4096], 0, 513, True)
            else:
                sFT = self.dsem("sFT")
                sFTl = self.dsem("sFTl")
                self.after(self.G, self.dsem("sZF"))
                for c in range(8):
                    self.after(SY, self.dsem("sUP"), (eV, self._f_hb_tok), sFT)
                    self.dma("S", sFTl, t["f_xw"][:, 0:513],
                             ap["xf_d"][c * 128:(c + 1) * 128, 4096:4609])
                    self.after(V, sFTl, eV)
                    self._f_hb_tok = eV.inc(V.tensor_copy(t["f_xw"][:, 1056:1056 + 513],
                                                          t["f_xw"][:, 0:513]))
                    self.after(G, eV)
                    self.dma("G", sFT, ap["ag_ft_in"][c * 128:(c + 1) * 128, 0:513],
                             t["f_xw"][:, 1056:1056 + 513])
                self.after(G, sFT, self.dsem("sZF"))
                self.ccs.inc(G.collective_compute(
                    "AllGather", ALU.bypass, replica_groups=PAIR_GROUPS,
                    ins=[self.d_ag_ft_in.ap().opt()], outs=[self.d_ag_ft_out.ap().opt()]))
                self.after(SY, self.ccs)
                self._ffn_halo_bounce(
                    lambda c: ap["ag_ft_out"][c * 128:(c + 1) * 128, 0:513], 0, 513, True)
            self.after(SY, self.dsem("sHB"), self.dsem("sFT"))
            self.after(G, self.dsem("sHB"))

            self._ffn_qkv_window(layer, -1)
            for w in range(8):
                self._ffn_qkv_window(layer, w)
                if w > 0:
                    self._ffn_apply_update(w - 1)
                self._ffn_window_attn(layer, w)
            self._ffn_apply_update(7)
            sFC = self.dsem("sFC")
            self.after(SY, self.dsem("sUP"), self.dsem("sHB"))
            for c in range(8):
                self.dma_nc("S", self.dsem("sFCl"), t["f_xw"][:, c * 4:c * 4 + 1],
                            ap["xf_d"][c * 128:(c + 1) * 128, 4608:4609])
            self.after(V, self.dsem("sFCl"), eV)
            hv = eV.inc(V.tensor_copy(t["f_xw"][:, 1056:1056 + 32].rearrange("p (c m) -> p c m", m=4)[:, :, 0:1]
                                      if False else t["f_xw"][:, 1056:1088], t["f_xw"][:, 0:32]))
            self.after(G, (eV, hv))
            self.dma("G", sFC, ap["ag_fc_in"][:, 0:4].rearrange("(c p) m -> p c m", p=128)[:, :, 0:1]
                     if False else ap["ag_fc_in"].rearrange("(c p) m -> p c m", p=128),
                     t["f_xw"][:, 1056:1088].rearrange("p (c m) -> p c m", m=4))
            self.after(G, sFC)
            self.ccs.inc(G.collective_compute(
                "AllGather", ALU.bypass, replica_groups=PAIR_GROUPS,
                ins=[self.d_ag_fc_in.ap().opt()], outs=[self.d_ag_fc_out.ap().opt()]))
            self.after(SY, self.ccs)
            for c in range(8):
                self.dma("S", self.dsem("sFCl"), t["f_xw"][:, c * 4:c * 4 + 4],
                         ap["ag_fc_out"][c * 128:(c + 1) * 128, 0:4])
            self.after(V, self.dsem("sFCl"), eV)
            hv = eV.inc(V.tensor_scalar_mul(t["f_xw"][:, 1056:1088], t["f_xw"][:, 0:32],
                                            t["hcol"][:, 0:1]))
            self.after(G, (eV, hv))
            for c in range(8):
                self.dma_nc("G", sFC, ap["xf_d"][c * 128:(c + 1) * 128, 512:513],
                            t["f_xw"][:, 1056 + c * 4:1057 + c * 4])
            self.after(SY, sFC)
            self.after(G, sFC)
            self._ffn_mlp(layer)

    def l1_gate_out(self):
        t, ap = self.t, self.ap
        V, SC, TE, G, SY = self.V, self.SC, self.TE, self.G, self.SY
        eV, eS, eT = self.eV, self.eS, self.eT
        self.phase_barrier(self.eV, self.eS, self.eT, self.eG, self.dsem("sUP"),
                           self.dsem("sXC"), self.dsem("sW1"), self.dsem("sW2"),
                           self.dsem("sFC"), self.dsem("sS1L"))
        if "dbg_s1" in self.debug:
            self.dma("G", self.dsem("dbgs"), ap["dbg_s1"], ap["s1_d"])
        if "dbg_xf" in self.debug:
            self.dma("G", self.dsem("dbgs"), ap["dbg_xf"], ap["xf_d"])
        sGO = self.dsem("sGO")
        sOU = self.dsem("sOU")
        gtok = 0
        ttok = 0
        for i in range(TOK // 128):
            self.after(SY, (eT, gtok), sGO)
            for c in range(8):
                self.dma("S", sGO, t["xnb"][:, c * 128:(c + 1) * 128],
                         ap["xf_d"][c * 128:(c + 1) * 128, 513 + i * 128:513 + (i + 1) * 128])
            self.after(TE, sGO, (eV, ttok))
            for c in range(8):
                eT.inc(TE.transpose(t["ptr"][:, c * 128:(c + 1) * 128],
                                    t["xnb"][:, c * 128:(c + 1) * 128], t["ident"][:]))
            self.after(V, eT, eS)
            ttok = eV.inc(V.tensor_copy(t["trstage"][:], t["ptr"][:, 0:1024]))
            self.after(SC, (eV, ttok), eS)
            gv = eS.inc(SC.activation(t["xnb2"][:], t["trstage"][:], AF.Silu))
            gtok = eT.v
            self.after(SY, eV, sGO)
            self.dma("S", sGO, t["xt"][:, 0:DM], ap["x1_d"][i * 128:(i + 1) * 128, :])
            self.dma("S", sGO, t["xt"][:, DM:2 * DM], ap["s1_d"][i * 128:(i + 1) * 128, :])
            self.after(V, sGO, (eS, gv), eV, sOU)
            eV.inc(V.tensor_tensor(t["sqscrap"][:], t["xt"][:, DM:2 * DM], t["xnb2"][:], ALU.mult))
            self.after(V, eV)
            ov = eV.inc(V.tensor_tensor(t["sqscrap"][:], t["sqscrap"][:], t["xt"][:, 0:DM], ALU.add))
            self.after(G, (eV, ov))
            self.dma("G", sOU, ap["out"][i * 128:(i + 1) * 128, :], t["sqscrap"][:])

    def finalize(self):
        G = self.G
        for s in [self.cS, self.cG, self.sX, self.sIW, self.sIWst, self.sXN, self.sSR]:
            if s.v:
                G.wait_ge(s.h, s.v)
        for s in self._dyn_sems.values():
            if s.v:
                G.wait_ge(s.h, s.v)


def kernel(**inputs):
    prog = Prog(upto="all")
    in_maps = [_prep_core_inputs(inputs, c) for c in range(NCORES)]
    res = run_bass_kernel_spmd(prog.nc, in_maps, core_ids=list(range(NCORES)))
    out = np.empty((B, S, DM), np.float32)
    for c in range(NCORES):
        b, h = c // 2, c % 2
        out[b, h * TOK:(h + 1) * TOK] = res.results[c]["out"]
    return out


# version 08 (final)
# speedup vs baseline: 1.1678x; 1.1678x over previous
import contextlib
import numpy as np
import ml_dtypes

import concourse.bass as bass
import concourse.mybir as mybir
from concourse.bass_utils import run_bass_kernel_spmd

F32 = mybir.dt.float32
BF16 = mybir.dt.bfloat16
AF = mybir.ActivationFunctionType
ALU = mybir.AluOpType
AX = mybir.AxisListType
BF = ml_dtypes.bfloat16

NCORES = 8
B, S, DM = 4, 8192, 1024
TOK = 4096
HALO = 128
TOKP = HALO + TOK
DI = 2048
C2 = 2 * DI
NS = 16
RK = 64
SL = 512
NSLICE = TOK // SL
NDN = DI // 8
EPS = 1e-5

NEXP = 8
HS = 2048
HR = 1024

ADIM = 1024
HEADS, DH, WIN, INNER = 8, 64, 512, 512
HID = 2730
FFH = 5460

PAIR_GROUPS = [[0, 1], [2, 3], [4, 5], [6, 7]]



def _col_panel(v, p=128):
    k = v.shape[0] // p
    return np.ascontiguousarray(np.asarray(v, np.float32).reshape(k, p).T)


def _host_constants():
    c = {}
    c["ident_bf"] = np.eye(128, dtype=np.float32)
    st = np.zeros((16, 128, 128), np.float32)
    rd = np.zeros((16, 128, 128), np.float32)
    for tp in range(16):
        for p in range(128):
            st[tp, 8 * tp + p // 16, p] = 1.0
            rd[tp, p, 8 * tp + p // 16] = 1.0
    c["st01"] = np.ascontiguousarray(st.transpose(1, 0, 2).reshape(128, 2048))
    c["red01"] = np.ascontiguousarray(rd.transpose(1, 0, 2).reshape(128, 2048))
    br = np.zeros((16, 128), np.float32)
    for p in range(128):
        br[p % 16, p] = 1.0
    c["brep01"] = br
    c["ones_row"] = np.ones((1, 128), np.float32)
    hd = np.zeros((128, 2), np.float32)
    hd[0:64, 0] = 1.0; hd[64:128, 1] = 1.0
    c["hd01"] = hd
    c["hd01T"] = np.ascontiguousarray(hd.T)
    ii = np.arange(128)
    c["mask_ge"] = (ii[:, None] >= ii[None, :]).astype(np.float32)
    c["mask_le"] = (ii[:, None] <= ii[None, :]).astype(np.float32)
    c["trilmask_add"] = np.where(ii[:, None] <= ii[None, :], 0.0, -1.0e30).astype(np.float32)
    return c


_HC = _host_constants()


def _prep_core_inputs(inputs, c):
    b, h = c // 2, c % 2
    x = np.asarray(inputs["L0_attn_x"], np.float32)
    own = x[b, h * TOK:(h + 1) * TOK]
    halo = np.zeros((HALO, DM), np.float32) if h == 0 else x[b, h * TOK - HALO:h * TOK]
    m = {"x": np.ascontiguousarray(np.concatenate([halo, own], axis=0))}

    A = lambda k: np.asarray(inputs[k], np.float32)
    m["inw"] = A("L0_attn_in_w")
    m["normw_cols"] = _col_panel(A("L0_attn_norm_w"))
    m["convw_cols"] = np.ascontiguousarray(
        A("L0_attn_conv_w").reshape(DI, 4).reshape(16, 128, 4).transpose(1, 0, 2).reshape(128, 64))
    m["convb_cols"] = _col_panel(A("L0_attn_conv_b"))
    m["xproj_t"] = np.ascontiguousarray(
        A("L0_attn_x_proj_w").reshape(16, 128, 96).transpose(1, 0, 2).reshape(128, 16 * 96))
    m["dtw"] = A("L0_attn_dt_w")
    m["dtb_cols"] = _col_panel(A("L0_attn_dt_b"))
    m["alog_cols"] = np.ascontiguousarray(
        A("L0_attn_A_log").reshape(NDN, 8, NS).transpose(1, 2, 0).reshape(128, NDN))
    m["D_cols"] = _col_panel(A("L0_attn_D"))
    m["outw"] = A("L0_attn_out_w")

    m["sw1"] = A("L0_ffn_sw1"); m["sw3"] = A("L0_ffn_sw3"); m["sw2"] = A("L0_ffn_sw2")
    m["sb1_cols"] = _col_panel(A("L0_ffn_sb1"))
    m["sb3_cols"] = _col_panel(A("L0_ffn_sb3"))
    m["sb2_cols"] = _col_panel(A("L0_ffn_sb2"))
    m["w1e"] = A("L0_ffn_w1"); m["w2e"] = A("L0_ffn_w2")
    m["b1e_cols"] = np.ascontiguousarray(A("L0_ffn_b1").reshape(NEXP, 8, 128).transpose(2, 0, 1).reshape(128, 64))
    m["b2e_cols"] = np.ascontiguousarray(A("L0_ffn_b2").reshape(NEXP, 8, 128).transpose(2, 0, 1).reshape(128, 64))

    m["wqkvT"] = np.ascontiguousarray(A("L1_attn_W_qkv").T)
    m["bqkv_cols"] = _col_panel(A("L1_attn_b_qkv"))
    m["bqkv_vrow"] = np.ascontiguousarray(A("L1_attn_b_qkv")[None, 2048:3072])

    m["f_wqkv"] = A("L1_ffn_wqkv")
    qs = A("L1_ffn_q_scale"); ks = A("L1_ffn_k_scale")
    m["f_qs_cols"] = np.ascontiguousarray(np.concatenate([np.tile(qs[l], 2)[:, None] for l in range(2)], 1))
    m["f_ks_cols"] = np.ascontiguousarray(np.concatenate([np.tile(ks[l], 2)[:, None] for l in range(2)], 1))
    m["f_wo"] = A("L1_ffn_wo")
    m["f_lng_cols"] = np.ascontiguousarray(A("L1_ffn_ln_g").reshape(2, 8, 128).transpose(2, 0, 1).reshape(128, 16))
    m["f_lnb_cols"] = np.ascontiguousarray(A("L1_ffn_ln_b").reshape(2, 8, 128).transpose(2, 0, 1).reshape(128, 16))
    w1 = A("L1_ffn_w1")
    w1a = np.zeros((2, DM, 2816), np.float32); w1a[:, :, :HID] = w1[:, :, 0:HID]
    w1g = np.zeros((2, DM, 2816), np.float32); w1g[:, :, :HID] = w1[:, :, HID:FFH]
    m["f_w1a"] = w1a; m["f_w1g"] = w1g
    b1 = A("L1_ffn_b1")
    b1a = np.zeros((2, 2816), np.float32); b1a[:, :HID] = b1[:, 0:HID]
    b1g = np.zeros((2, 2816), np.float32); b1g[:, :HID] = b1[:, HID:FFH]
    m["f_b1a_cols"] = np.ascontiguousarray(b1a.reshape(2, 22, 128).transpose(2, 0, 1).reshape(128, 44))
    m["f_b1g_cols"] = np.ascontiguousarray(b1g.reshape(2, 22, 128).transpose(2, 0, 1).reshape(128, 44))
    w2 = A("L1_ffn_w2")
    w2p = np.zeros((2, 2816, DM), np.float32); w2p[:, :HID, :] = w2
    m["f_w2p"] = w2p
    m["f_b2_cols"] = np.ascontiguousarray(A("L1_ffn_b2").reshape(2, 8, 128).transpose(2, 0, 1).reshape(128, 16))
    P = np.zeros((128, 128), np.float32)
    for hh in range(2):
        for dd in range(64):
            P[hh * 64 + (dd + 32) % 64, hh * 64 + dd] = 1.0
    m["swap64"] = P

    m["hflag"] = np.full((128, 1), float(h), np.float32)
    for k, v in _HC.items():
        m[k] = v

    own0 = h * TOK
    pos = np.arange(own0 - WIN, own0 + TOK, dtype=np.float64)
    half = np.arange(0, DH, 2, dtype=np.float64)
    inv = 1.0 / 10000.0 ** (half / DH)
    base = (half + 0.4 * DH) / (1.4 * DH)
    freqs = pos[:, None] * inv[None, :]
    freqs = np.concatenate([freqs, freqs], axis=1)
    power = (pos - S // 2) / float(WIN // 2)
    scale = base[None, :] ** power[:, None]
    scale = np.concatenate([scale, scale], axis=1)
    cos, sin = np.cos(freqs), np.sin(freqs)
    sgn = np.concatenate([-np.ones(32), np.ones(32)])[None, :]
    m["rot_q_cs"] = np.ascontiguousarray(np.tile((cos * scale * 8.0).astype(np.float32).T, (2, 1)))
    m["rot_q_sn"] = np.ascontiguousarray(np.tile((sin * scale * sgn * 8.0).astype(np.float32).T, (2, 1)))
    m["rot_k_cs"] = np.ascontiguousarray(np.tile((cos / scale).astype(np.float32).T, (2, 1)))
    m["rot_k_sn"] = np.ascontiguousarray(np.tile((sin / scale * sgn).astype(np.float32).T, (2, 1)))

    i_l = np.arange(WIN); j_l = np.arange(2 * WIN)
    allow = ((j_l[:, None] >= i_l[None, :]) & (j_l[:, None] <= i_l[None, :] + WIN)).astype(np.float32)
    m["lmask_first"] = np.ascontiguousarray(allow * (j_l[:, None] >= WIN)) if h == 0 \
        else np.ascontiguousarray(allow)
    m["lmask_rest"] = np.ascontiguousarray(allow)
    return m



class Sem:
    def __init__(self, es, nc, name):
        self.h = es.enter_context(nc.semaphore(name))
        self.v = 0

    def inc(self, instr, k=1):
        instr.then_inc(self.h, k)
        self.v += k
        return self.v


DBG_SPECS = {
    "dbg_xn": ((DM, TOKP), BF16),
    "dbg_srow": ((1, TOKP), F32),
    "dbg_s": ((DM, TOK), BF16),
    "dbg_g": ((DM, TOK), BF16),
    "dbg_x1": ((TOK, DM), F32),
    "dbg_xn1": ((2 * DM, TOK), BF16),
    "dbg_s1": ((TOK, DM), F32),
    "dbg_g1": ((TOK, DM), BF16),
    "dbg_xf": ((DM, 4610), BF16),
}


class Prog:
    def __init__(self, upto="all", debug=()):
        self.upto = upto
        self.debug = tuple(debug)
        self.nc = bass.Bass("TRN2", target_bir_lowering=False)
        self.es = contextlib.ExitStack()
        self.t = {}
        self.ap = {}
        nc = self.nc
        self.V, self.SC, self.TE, self.G, self.SY = nc.vector, nc.scalar, nc.tensor, nc.gpsimd, nc.sync
        self._declare_dram()
        self._declare_sbuf()
        self._declare_sems()
        self._emit()

    def sb(self, name, shape, dt=F32):
        self.t[name] = self.es.enter_context(self.nc.sbuf_tensor(name, shape, dt))

    def psum(self, name, shape, dt=F32):
        self.t[name] = self.es.enter_context(self.nc.psum_tensor(name, shape, dt))

    def dram(self, name, shape, dt=F32, kind=None):
        if kind is None:
            h = self.nc.dram_tensor(name, list(shape), dt)
        else:
            h = self.nc.dram_tensor(name, list(shape), dt, kind=kind)
        self.ap[name] = h.ap()
        setattr(self, "d_" + name, h)

    def dma(self, eng, sem, dst, src):
        e = {"S": self.SY, "A": self.SC, "G": self.G}[eng]
        return sem.inc(e.dma_start(out=dst, in_=src), 16)

    def dma_nc(self, eng, sem, dst, src):
        e = {"S": self.SY, "A": self.SC, "G": self.G}[eng]
        return sem.inc(e.dma_start(out=dst, in_=src, allow_slow_non_contiguous=True), 16)

    def after(self, eng, *deps):
        for x in deps:
            if isinstance(x, tuple):
                eng.wait_ge(x[0].h, x[1])
            else:
                eng.wait_ge(x.h, x.v)

    def phase_barrier(self, *sems):
        for e in (self.V, self.SC, self.TE, self.G, self.SY):
            for s in sems:
                if s.v:
                    e.wait_ge(s.h, s.v)

    def _declare_dram(self):
        ins = [
            ("x", (TOKP, DM), F32), ("inw", (DM, C2), F32), ("normw_cols", (128, 8), F32),
            ("convw_cols", (128, 64), F32), ("convb_cols", (128, 16), F32),
            ("xproj_t", (128, 16 * 96), F32), ("dtw", (RK, DI), F32), ("dtb_cols", (128, 16), F32),
            ("alog_cols", (128, NDN), F32), ("D_cols", (128, 16), F32), ("outw", (DI, DM), F32),
            ("sw1", (DM, HS), F32), ("sw3", (DM, HS), F32), ("sw2", (HS, DM), F32),
            ("sb1_cols", (128, 16), F32), ("sb3_cols", (128, 16), F32), ("sb2_cols", (128, 8), F32),
            ("w1e", (NEXP, DM, HR), F32), ("w2e", (NEXP, HR, DM), F32),
            ("b1e_cols", (128, 64), F32), ("b2e_cols", (128, 64), F32),
            ("wqkvT", (DM, 3 * ADIM), F32), ("bqkv_cols", (128, 24), F32),
            ("bqkv_vrow", (1, ADIM), F32),
            ("f_wqkv", (2, DM, 3 * INNER), F32), ("f_qs_cols", (128, 2), F32), ("f_ks_cols", (128, 2), F32),
            ("f_wo", (2, INNER, DM), F32), ("f_lng_cols", (128, 16), F32), ("f_lnb_cols", (128, 16), F32),
            ("f_w1a", (2, DM, 2816), F32), ("f_w1g", (2, DM, 2816), F32),
            ("f_b1a_cols", (128, 44), F32), ("f_b1g_cols", (128, 44), F32),
            ("f_w2p", (2, 2816, DM), F32), ("f_b2_cols", (128, 16), F32), ("swap64", (128, 128), F32),
            ("hflag", (128, 1), F32), ("ident_bf", (128, 128), F32), ("st01", (128, 2048), F32),
            ("red01", (128, 2048), F32), ("brep01", (16, 128), F32), ("ones_row", (1, 128), F32),
            ("rot_q_cs", (128, TOK + WIN), F32), ("rot_q_sn", (128, TOK + WIN), F32),
            ("rot_k_cs", (128, TOK + WIN), F32), ("rot_k_sn", (128, TOK + WIN), F32),
            ("lmask_first", (2 * WIN, WIN), F32), ("lmask_rest", (2 * WIN, WIN), F32),
            ("mask_ge", (128, 128), F32), ("hd01", (128, 2), F32), ("hd01T", (2, 128), F32),
            ("mask_le", (128, 128), F32), ("trilmask_add", (128, 128), F32),
        ]
        for name, shape, dt in ins:
            self.dram(name, shape, dt, kind="ExternalInput")
        self.dram("out", (TOK, DM), F32, kind="ExternalOutput")
        self.dram("xnT", (DM, TOKP), BF16)
        self.dram("srow_d", (1, TOKP), F32)
        self.dram("y0_d", (16, 128, SL), BF16)
        self.dram("gacc_d", (8, 128, TOK), BF16)
        self.dram("inw_t", (128, 32 * 8 * 128), BF16)
        self.dram("smamba", (DM, TOK), BF16)
        self.dram("x1_d", (TOK, DM), F32)
        self.dram("xeP", (16, 128, 8 * SL), BF16)
        self.dram("ag_state_in", (128, NDN), F32)
        self.dram("ag_state_out", (2 * 128, NDN), F32)
        self.dram("xn1T_in", (DM, TOK), BF16)
        self.dram("xn1T_out", (2 * DM, TOK), BF16)
        self.dram("s1_d", (TOK, DM), F32)
        self.dram("attacc", (4096, 1040), F32)
        self.dram("xf_d", (DM, 4610), BF16)
        self.dram("w1a_bf", (2, DM, 2816), BF16)
        self.dram("w1g_bf", (2, DM, 2816), BF16)
        self.dram("w2_bf", (2, 2816, DM), BF16)
        self.dram("ag_s1_in", (2048, DM), F32)
        self.dram("ag_s1_out", (2 * 2048, DM), F32)
        self.dram("ag_ft_in", (DM, 516), BF16)
        self.dram("ag_ft_out", (2 * DM, 516), BF16)
        self.dram("ag_fc_in", (DM, 4), BF16)
        self.dram("ag_fc_out", (2 * DM, 4), BF16)
        for nm in self.debug:
            shape, dt = DBG_SPECS[nm]
            self.dram(nm, shape, dt, kind="ExternalOutput")

    def _declare_sbuf(self):
        sb = self.sb
        sb("ident", [128, 128], BF16)
        sb("st01_sb", [128, 2048], BF16)
        sb("red01_sb", [128, 2048], BF16)
        sb("brep01_sb", [16, 128], BF16)
        sb("ones_row_sb", [1, 128], BF16)
        sb("colbank", [128, 512])
        sb("acols", [128, NDN])
        sb("hcol", [128, 1])
        sb("dtw_sb", [64, DI], BF16)
        sb("xproj_sb", [128, 16 * 96], BF16)
        sb("stats", [128, 8])
        sb("zeros512", [128, SL])
        sb("states", [128, NDN])
        sb("s0init", [128, NDN])
        sb("xdbl", [96, SL])
        sb("dtb16", [64, SL], BF16)
        sb("Bbuf", [16, SL], BF16)
        sb("Cbuf", [16, SL], BF16)
        sb("Cbuf0", [16, SL], BF16)
        sb("trilmask_sb", [128, 128])
        sb("hd01_sb", [128, 2], BF16)
        sb("hd01T_sb", [2, 128], BF16)
        sb("swap64_sb", [128, 128], BF16)
        sb("onescol", [128, 2], BF16)
        sb("bvrow", [1, ADIM], BF16)
        sb("onesrow1", [1, 128], BF16)
        sb("srsl", [1, SL + 16])
        sb("srsl_bf", [1, SL + 16], BF16)
        sb("stg512", [128, SL], BF16)
        sb("poolB", [128, 65536], BF16)
        sb("poolF", [128, 11168], F32)
        for i in range(7):
            self.psum(f"pb{i}", [128, 512], F32)
        self.psum("ptr", [128, 1024], BF16)

        PB, PF = self.t["poolB"], self.t["poolF"]

        def bview(name, off, n):
            self.t[name] = None
            setattr(self, "v_" + name, (PB, off, n))
            return PB[:, off:off + n]

        B_ = {}
        def vb(name, off, n):
            B_[name] = (off, n)
            self.t[name] = PB[:, off:off + n]
        def vf(name, off, n):
            self.t[name] = PF[:, off:off + n]

        vb("ucb", 0, 8192)
        vb("resb", 8192, 8192)
        vb("db", 16384, 8192)
        vb("dub", 24576, 8192)
        vb("sgrp", 32768, 8192)
        vb("ygat", 40960, 8192)
        vb("inw_tile", 49152, 2048)
        vb("outw_tile", 51200, 4096)
        vb("xnsl", 55296, 4224)
        vb("xnres", 59520, 512)
        vb("ycb", 60032, 1024)
        vb("xnb", 0, 1024)
        vb("xnb2", 1024, 1024)
        vb("trstage", 2048, 1024)
        vb("wexp", 0, 8192)
        vb("hssP", 0, 16384)
        vb("shwP", 16384, 2048)
        vb("xep0", 18432, 4096)
        vb("xep1", 22528, 4096)
        vb("h13", 26624, 512)
        vb("he0", 8192, 4096)
        vb("he1", 12288, 4096)
        vb("xe0", 16384, 4096)
        vb("xe1", 20480, 4096)
        vb("hss", 24576, 8192)
        vb("shw", 40960, 2048)
        vb("gslice", 43008, 4096)
        vb("gsilu", 47104, 4096)
        vb("ssl", 51200, 4096)
        vb("a_KT", 0, 8192)
        vb("a_VT", 8192, 8192)
        vb("a_expS", 16384, 4096)
        vb("a_QT", 20480, 4096)
        vb("a_XS", 24576, 4096)
        vb("a_wq", 28672, 8192)
        vb("a_wkv", 36864, 16384)
        vb("a_stg", 53248, 2064)
        vb("f_qhat", 0, 2048)
        vb("f_khat", 2048, 4096)
        vb("f_vtok", 6144, 4096)
        vb("f_expS", 10240, 4096)
        vb("f_oh", 14336, 4096)
        vb("f_upd", 18432, 8192)
        vb("f_xw", 26624, 8448)
        vb("f_wqkv", 35072, 12288)
        vb("f_wo", 47360, 8192)
        vb("f_mfirst", 55552, 4096)
        vb("f_mrest", 59648, 4096)
        vb("f_kqtmp", 63744, 1024)
        vb("f_hln", 0, 8192)
        vb("f_h2", 8192, 22528)
        vb("f_w1s", 30720, 3072)
        vb("f_w2s", 33792, 2816)
        vb("f_bst", 36608, 8)
        vb("f_xc", 36864, 8448)

        vf("xt", 0, 2048)
        vf("sqscrap", 2048, 1024)
        vf("uraw", 3072, SL + 32)
        vf("uacc", 3616, 2 * SL)
        vf("srep", 4640, SL + 16)
        vf("dA", 5184, 2 * SL)
        vf("dBu", 6208, 2 * SL)
        vf("sps", 7232, SL)
        vf("ytmp", 7744, SL)
        vf("silures", 8256, SL)
        vf("Brep", 8768, SL)
        vf("Crep", 9280, SL)
        vf("y0rl", 9792, SL)
        vf("a_stgF", 0, 2080)
        vf("f_rcs", 2080, 512)
        vf("f_rsn", 2592, 512)
        vf("f_mrep", 4128, 512)
        vf("f_rrep", 4640, 512)
        self._poolmaps_l1_done = False

    def _declare_sems(self):
        for nm in ["eT", "eV", "eS", "eG", "ccs"]:
            setattr(self, nm, Sem(self.es, self.nc, nm))
        for nm in ["cS", "cG", "sX", "sIW", "sIWst", "sXN", "sSR", "sXS", "sSRL", "sWa", "sWb",
                   "sBC", "sOWa", "sOWb", "sXR", "sSM", "sY0", "sY0L", "sST", "sSTL"]:
            setattr(self, nm, Sem(self.es, self.nc, nm))
        self._dyn_sems = {}

    def dsem(self, name):
        if name not in self._dyn_sems:
            self._dyn_sems[name] = Sem(self.es, self.nc, name)
        return self._dyn_sems[name]

    def _emit(self):
        self.p0_consts()
        self.p0_inw_prep()
        self.p0_layernorm()
        if self.upto != "p0":
            self.l0_mamba()
            self.l0_moe()
            self.l0_fixup_gate()
        if self.upto in ("all", "l1a", "l1f"):
            self.l1_attn()
        if self.upto in ("all", "l1f"):
            self.l1_ffn()
        if self.upto == "all":
            self.l1_gate_out()
        self.finalize()
        self.es.close()

    def p0_consts(self):
        t, ap = self.t, self.ap
        V, SC = self.V, self.SC
        dma = self.dma
        dma("G", self.cG, t["ident"][:], ap["ident_bf"])
        dma("G", self.cG, t["st01_sb"][:], ap["st01"])
        dma("G", self.cG, t["red01_sb"][:], ap["red01"])
        dma("G", self.cG, t["brep01_sb"][:], ap["brep01"])
        dma("G", self.cG, t["ones_row_sb"][:], ap["ones_row"])
        dma("G", self.cG, t["dtw_sb"][:], ap["dtw"])
        dma("G", self.cG, t["xproj_sb"][:], ap["xproj_t"])
        dma("S", self.cS, t["colbank"][:, 0:16], ap["convb_cols"])
        dma("S", self.cS, t["colbank"][:, 16:32], ap["dtb_cols"])
        dma("S", self.cS, t["colbank"][:, 32:48], ap["D_cols"])
        dma("S", self.cS, t["colbank"][:, 48:56], ap["normw_cols"])
        dma("S", self.cS, t["colbank"][:, 64:128], ap["convw_cols"])
        dma("S", self.cS, t["colbank"][:, 128:144], ap["sb1_cols"])
        dma("S", self.cS, t["colbank"][:, 144:160], ap["sb3_cols"])
        dma("S", self.cS, t["colbank"][:, 160:168], ap["sb2_cols"])
        dma("S", self.cS, t["colbank"][:, 168:232], ap["b1e_cols"])
        dma("S", self.cS, t["colbank"][:, 232:296], ap["b2e_cols"])
        dma("S", self.cS, t["hcol"][:], ap["hflag"])
        dma("S", self.cS, t["acols"][:], ap["alog_cols"])
        self.after(SC, self.cS)
        self.eS.inc(SC.activation(t["acols"][:], t["acols"][:], AF.Exp))
        self.after(V, self.eS, self.cS)
        self.eV.inc(V.tensor_scalar_mul(t["acols"][:], t["acols"][:], -1.0))
        self.eV.inc(V.tensor_scalar_mul(t["colbank"][:, 232:296], t["colbank"][:, 232:296], 0.5))
        self.eV.inc(V.memzero(t["zeros512"][:]))
        self.eV.inc(V.memzero(t["states"][:]))
        self.eV.inc(V.memzero(t["s0init"][:]))

    def p0_inw_prep(self):
        t, ap = self.t, self.ap
        V, G, SY = self.V, self.G, self.SY
        it = ap["inw_t"].rearrange("p (blk m) -> p blk m", m=128)
        for kc in range(8):
            for half in range(2):
                it_i = kc * 2 + half
                if it_i > 0:
                    self.after(SY, (self.eV, self._iw_vdone))
                self.dma("S", self.sIW, t["xt"][:, 0:2048],
                         ap["inw"][kc * 128:(kc + 1) * 128, half * 2048:(half + 1) * 2048])
                self.after(V, self.sIW, self.sIWst, self.cS)
                self._iw_vdone = self.eV.inc(V.tensor_scalar_mul(
                    t["inw_tile"][:], t["xt"][:, 0:2048], t["colbank"][:, 48 + kc:49 + kc]))
                self.after(G, (self.eV, self._iw_vdone))
                self.dma("G", self.sIWst, it[:, half * 128 + kc:half * 128 + 128:8, :],
                         t["inw_tile"][:].rearrange("p (c m) -> p c m", m=128))

    def p0_layernorm(self):
        t, ap = self.t, self.ap
        V, SC, TE, G, SY = self.V, self.SC, self.TE, self.G, self.SY
        eV, eS, eT = self.eV, self.eS, self.eT
        NT = TOKP // 128
        xnT3 = ap["xnT"].rearrange("(c p) tp -> c p tp", p=128)
        xdone = [0] * NT
        self.after(SY, (eV, self._iw_vdone))
        for i in range(min(2, NT)):
            self.dma("S", self.sX, t["xt"][:, (i % 2) * DM:((i % 2) + 1) * DM],
                     ap["x"][i * 128:(i + 1) * 128, :])
        for i in range(NT):
            buf = t["xt"][:, (i % 2) * DM:((i % 2) + 1) * DM]
            st = t["stats"]
            self.after(V, self.sX, eS, self.sSR)
            eV.inc(V.tensor_reduce(st[:, 0:1], buf, axis=AX.X, op=ALU.add))
            self.after(SC, self.sX, eV)
            eS.inc(SC.activation(t["sqscrap"][:], buf, AF.Square, accum_out=st[:, 1:2]))
            self.after(V, eS)
            eV.inc(V.tensor_scalar_mul(st[:, 2:3], st[:, 0:1], 1.0 / DM))
            self.after(V, eV)
            eV.inc(V.tensor_tensor(st[:, 3:4], st[:, 2:3], st[:, 2:3], ALU.mult))
            eV.inc(V.tensor_scalar(st[:, 4:5], st[:, 1:2], 1.0 / DM, EPS, ALU.mult, ALU.add))
            self.after(V, eV)
            eV.inc(V.tensor_tensor(st[:, 4:5], st[:, 4:5], st[:, 3:4], ALU.subtract))
            self.after(SC, eV)
            eS.inc(SC.activation(st[:, 5:6], st[:, 4:5], AF.Sqrt))
            self.after(V, eS)
            eV.inc(V.reciprocal(st[:, 6:7], st[:, 5:6]))
            self.after(V, eV)
            eV.inc(V.tensor_scalar(t["sqscrap"][:], buf, st[:, 2:3], st[:, 6:7],
                                   ALU.subtract, ALU.mult))
            self.after(V, eT, eV)
            xdone[i] = eV.inc(V.tensor_copy(t["xnb"][:], t["sqscrap"][:]))
            if i + 2 < NT:
                self.after(SY, (eV, xdone[i]))
                self.dma("S", self.sX, t["xt"][:, (i % 2) * DM:((i % 2) + 1) * DM],
                         ap["x"][(i + 2) * 128:(i + 3) * 128, :])
            self.after(SC, (eV, xdone[i]))
            eS.inc(SC.activation(t["sqscrap"][:], t["sqscrap"][:], AF.Square, accum_out=st[:, 7:8]))
            self.after(V, eS, self.sSR)
            eV.inc(V.tensor_scalar(st[:, 7:8], st[:, 7:8], 1.0 / DM, EPS, ALU.mult, ALU.add))
            self.after(SC, eV)
            eS.inc(SC.activation(st[:, 7:8], st[:, 7:8], AF.Sqrt))
            self.after(V, eS)
            srd = eV.inc(V.reciprocal(st[:, 7:8], st[:, 7:8]))
            self.after(G, (eV, srd))
            self.dma("G", self.sSR, ap["srow_d"][0:1, i * 128:(i + 1) * 128], st[:, 7:8])
            if i == 0:
                self.after(TE, self.cG)
            self.after(TE, (eV, xdone[i]), eV)
            for j in range(8):
                eT.inc(TE.transpose(t["ptr"][:, j * 128:(j + 1) * 128],
                                    t["xnb"][:, j * 128:(j + 1) * 128], t["ident"][:]))
            self.after(V, eT, self.sXN)
            self._tr_vdone = eV.inc(V.tensor_copy(t["trstage"][:], t["ptr"][:, 0:1024]))
            self.after(G, (eV, self._tr_vdone))
            self.dma("G", self.sXN, xnT3[:, :, i * 128:(i + 1) * 128].rearrange("c p m -> p c m"),
                     t["trstage"][:].rearrange("p (c m) -> p c m", m=128))
        if "dbg_xn" in self.debug:
            self.after(G, self.sXN)
            self.dma("G", self.dsem("dbgs"), ap["dbg_xn"], ap["xnT"])
        if "dbg_srow" in self.debug:
            self.after(G, self.sSR)
            self.dma("G", self.dsem("dbgs"), ap["dbg_srow"], ap["srow_d"])

    def _mamba_frontend(self, sl):
        t, ap = self.t, self.ap
        V, SC, TE, G, SY = self.V, self.SC, self.TE, self.G, self.SY
        eV, eS, eT = self.eV, self.eS, self.eT
        pb = [t[f"pb{i}"] for i in range(7)]
        col0 = HALO + sl * SL - 16
        self.after(SY, (self.eT, self._xnsl_tok), (self.eV, self._srep_tok))
        self.dma("S", self.sXS, t["xnsl"][:].rearrange("p (c m) -> p c m", m=SL + 16),
                 ap["xnT"].rearrange("(c p) tp -> c p tp", p=128)[:, :, col0:col0 + SL + 16]
                 .rearrange("c p m -> p c m"))
        self.dma("S", self.sSRL, t["srsl"][0:1, :], ap["srow_d"][0:1, col0:col0 + SL + 16])
        self.after(V, self.sSRL)
        srb = eV.inc(V.tensor_copy(t["srsl_bf"][0:1, :], t["srsl"][0:1, :]))
        self.after(TE, (eV, srb), eV)
        sr_mm = eT.inc(TE.matmul(pb[6][:, 0:SL], t["ones_row_sb"][:],
                                 t["srsl_bf"][0:1, 16:SL + 16], start=True, stop=True))
        self.after(V, (eT, sr_mm))
        sv1 = eV.inc(V.tensor_copy(t["srep"][:, 16:SL + 16], pb[6][:, 0:SL]))
        self.after(TE, (eV, sv1))
        sr_mm2 = eT.inc(TE.matmul(pb[6][:, 0:16], t["ones_row_sb"][:],
                                  t["srsl_bf"][0:1, 0:16], start=True, stop=True))
        self.after(V, (eT, sr_mm2))
        self._srep_tok = eV.inc(V.tensor_copy(t["srep"][:, 0:16], pb[6][:, 0:16]))

        for cpt in range(32):
            wbuf = cpt % 2
            wsem = self.sWa if wbuf == 0 else self.sWb
            self.after(SY, (eT, self._inw_tok[wbuf]))
            self.dma("S", wsem, t["inw_tile"][:, wbuf * 1024:(wbuf + 1) * 1024],
                     ap["inw_t"][:, cpt * 1024:(cpt + 1) * 1024])
            ppick = cpt % 2
            ps = pb[ppick]
            self.after(TE, wsem, self.sXS, (eV, self._psA_tok[ppick]))
            for kc in range(8):
                eT.inc(TE.matmul(ps[:, 0:SL],
                                 t["inw_tile"][:, wbuf * 1024 + kc * 128:wbuf * 1024 + (kc + 1) * 128],
                                 t["xnsl"][:, kc * (SL + 16) + 16:(kc + 1) * (SL + 16)],
                                 start=(kc == 0), stop=(kc == 7)))
            if cpt < 16:
                self.after(TE, (eV, self._pre_tok))
                for kc in range(8):
                    eT.inc(TE.matmul(pb[2][:, 0:16],
                                     t["inw_tile"][:, wbuf * 1024 + kc * 128:wbuf * 1024 + (kc + 1) * 128],
                                     t["xnsl"][:, kc * (SL + 16):kc * (SL + 16) + 16],
                                     start=(kc == 0), stop=(kc == 7)))
            self._inw_tok[wbuf] = eT.v
            if cpt < 16:
                ur = t["uraw"]
                self.after(V, eT, (eV, self._srep_tok))
                eV.inc(V.tensor_tensor(ur[:, 0:16], pb[2][:, 0:16], t["srep"][:, 0:16], ALU.mult))
                self._pre_tok = eV.v
                self.after(V, eV)
                self._psA_tok[ppick] = eV.inc(V.tensor_tensor(
                    ur[:, 16:16 + SL], ps[:, 0:SL], t["srep"][:, 16:16 + SL], ALU.mult))
                cw = lambda j: t["colbank"][:, 64 + cpt * 4 + j:64 + cpt * 4 + j + 1]
                ua, ub = t["uacc"][:, 0:SL], t["uacc"][:, SL:2 * SL]
                self.after(V, eV, eS)
                eV.inc(V.tensor_scalar(ua, ur[:, 13:13 + SL], cw(0),
                                       t["colbank"][:, cpt:cpt + 1], ALU.mult, ALU.add))
                self.after(V, eV)
                eV.inc(V.scalar_tensor_tensor(ub, ur[:, 14:14 + SL], cw(1), ua, ALU.mult, ALU.add))
                self.after(V, eV)
                eV.inc(V.scalar_tensor_tensor(ua, ur[:, 15:15 + SL], cw(2), ub, ALU.mult, ALU.add))
                self.after(V, eV)
                cvd = eV.inc(V.scalar_tensor_tensor(ub, ur[:, 16:16 + SL], cw(3), ua, ALU.mult, ALU.add))
                self.after(SC, (eV, cvd))
                eS.inc(SC.activation(t["ucb"][:, cpt * SL:(cpt + 1) * SL], ub, AF.Silu))
            else:
                j = cpt - 16
                self.after(V, eT, (eV, self._srep_tok))
                self._psA_tok[ppick] = eV.inc(V.tensor_tensor(
                    t["resb"][:, j * SL:(j + 1) * SL], ps[:, 0:SL],
                    t["srep"][:, 16:16 + SL], ALU.mult))
        self.after(TE, eS, (eV, self._pre_tok), self.cG)
        for jt in range(16):
            eT.inc(TE.matmul(pb[2][0:96, 0:SL], t["xproj_sb"][:, jt * 96:(jt + 1) * 96],
                             t["ucb"][:, jt * SL:(jt + 1) * SL], start=(jt == 0), stop=(jt == 15)))
        self.after(V, eT)
        xdv = eV.inc(V.tensor_copy(t["xdbl"][:], pb[2][0:96, 0:SL]))
        self.after(V, (eV, xdv))
        dtv = eV.inc(V.tensor_copy(t["dtb16"][:], pb[2][0:64, 0:SL]))
        self._pre_tok = dtv
        self.after(G, (eV, xdv))
        self.dma("G", self.sBC, t["Bbuf"][:], t["xdbl"][64:80, :])
        self.dma("G", self.sBC, t["Cbuf"][:], t["xdbl"][80:96, :])
        self.after(TE, self.sBC, (eV, self._brep_tok))
        b_mm = eT.inc(TE.matmul(pb[3][:, 0:SL], t["brep01_sb"][:], t["Bbuf"][:], start=True, stop=True))
        self.after(V, (eT, b_mm))
        brv = eV.inc(V.tensor_copy(t["Brep"][:], pb[3][:, 0:SL]))
        self.after(TE, (eV, brv))
        c_mm = eT.inc(TE.matmul(pb[3][:, 0:SL], t["brep01_sb"][:], t["Cbuf"][:], start=True, stop=True))
        self.after(V, (eT, c_mm))
        self._brep_tok = eV.inc(V.tensor_copy(t["Crep"][:], pb[3][:, 0:SL]))
        dprev = [eV.v, self._brep_tok]
        for jt in range(16):
            pp = pb[2] if jt % 2 == 0 else pb[3]
            self.after(TE, eS, (eV, dprev[jt % 2]))
            m = eT.inc(TE.matmul(pp[:, 0:SL], t["dtw_sb"][:, jt * 128:(jt + 1) * 128],
                                 t["dtb16"][:], start=True, stop=True))
            self.after(SC, (eT, m), eS)
            eS.inc(SC.activation(t["sps"][:], pp[:, 0:SL], AF.Exp,
                                 bias=t["colbank"][:, 16 + jt:17 + jt]))
            self.after(SC, eS)
            dsv = eS.inc(SC.activation(t["db"][:, jt * SL:(jt + 1) * SL], t["sps"][:], AF.Ln, bias=1.0))
            dprev[jt % 2] = 0
        self._delta_tok = eS.v

    def _mamba_scan_group(self, g, fixup):
        t, ap = self.t, self.ap
        V, SC, TE, G, SY = self.V, self.SC, self.TE, self.G, self.SY
        eV, eS, eT, eG = self.eV, self.eS, self.eT, self.eG
        pb = [t[f"pb{i}"] for i in range(7)]
        for j in range(16):
            T = 16 * g + j
            pr = pb[4 + (T % 2)]
            pu = pb[2 + (T % 2)]
            st01 = t["st01_sb"][:, j * 128:(j + 1) * 128]
            self.after(TE, (eS, self._delta_tok), eS)
            m1 = eT.inc(TE.matmul(pr[:, 0:SL], st01, t["db"][:, g * SL:(g + 1) * SL],
                                  start=True, stop=True))
            self.after(SC, (eT, m1), self.cS)
            dav = eS.inc(SC.activation(t["dA"][:, (T % 2) * SL:((T % 2) + 1) * SL], pr[:, 0:SL],
                                       AF.Exp, scale=t["acols"][:, T:T + 1]))
            if not fixup:
                self.after(TE, eV)
                m2 = eT.inc(TE.matmul(pu[:, 0:SL], st01, t["dub"][:, g * SL:(g + 1) * SL],
                                      start=True, stop=True))
                self.after(V, (eT, m2), (eV, self._brep_tok), eV)
                dbv = eV.inc(V.tensor_tensor(t["dBu"][:, (T % 2) * SL:((T % 2) + 1) * SL],
                                             pu[:, 0:SL], t["Brep"][:], ALU.mult))
                data1 = t["dBu"][:, (T % 2) * SL:((T % 2) + 1) * SL]
                init = t["states"][:, T:T + 1]
                self.after(V, (eS, dav), (eV, dbv), eG)
            else:
                data1 = t["zeros512"][:]
                init = t["s0init"][:, T:T + 1]
                self.after(V, (eS, dav), eG)
            scv = eV.inc(V.tensor_tensor_scan(t["sgrp"][:, j * SL:(j + 1) * SL],
                                              t["dA"][:, (T % 2) * SL:((T % 2) + 1) * SL],
                                              data1, init, ALU.mult, ALU.add))
            self.after(G, (eV, scv), (eV, self._brep_tok), eT)
            ycv = eG.inc(G.tensor_tensor(t["ycb"][:, (T % 2) * SL:((T % 2) + 1) * SL],
                                         t["sgrp"][:, j * SL:(j + 1) * SL], t["Crep"][:], ALU.mult))
            self.after(TE, (eG, ycv), eV)
            eT.inc(TE.matmul(pb[6][:, 0:SL], t["red01_sb"][:, j * 128:(j + 1) * 128],
                             t["ycb"][:, (T % 2) * SL:((T % 2) + 1) * SL],
                             start=(j == 0), stop=(j == 15)))
        if not fixup:
            self.after(V, eV, eG)
            self.eV.inc(V.tensor_copy(
                t["states"][:, 16 * g:16 * g + 16],
                t["sgrp"][:].rearrange("p (j m) -> p j m", m=SL)[:, :, SL - 1]))

    def _mamba_tail(self, sl, g, fixup):
        t, ap = self.t, self.ap
        V, SC, G = self.V, self.SC, self.G
        eV, eS, eT = self.eV, self.eS, self.eT
        pb6 = self.t["pb6"]
        if sl == 0 and not fixup:
            self.after(V, eT, self.sY0)
            vv = eV.inc(V.tensor_copy(t["stg512"][:], pb6[:, 0:SL]))
            self.after(G, (eV, vv))
            self.dma("G", self.sY0, ap["y0_d"][g], t["stg512"][:])
            return
        self.after(SC, eS)
        sil = eS.inc(SC.activation(t["silures"][:], t["resb"][:, g * SL:(g + 1) * SL], AF.Silu))
        self.after(V, eT, eV)
        if fixup:
            self.after(V, self.sY0L)
            vv = eV.inc(V.tensor_tensor(t["ytmp"][:], pb6[:, 0:SL], t["dub"][:, 0:SL], ALU.add))
            self.after(V, (eV, vv))
            vv = eV.inc(V.scalar_tensor_tensor(t["ytmp"][:], t["ucb"][:, g * SL:(g + 1) * SL],
                                               t["colbank"][:, 32 + g:33 + g], t["ytmp"][:],
                                               ALU.mult, ALU.add))
        else:
            vv = eV.inc(V.scalar_tensor_tensor(t["ytmp"][:], t["ucb"][:, g * SL:(g + 1) * SL],
                                               t["colbank"][:, 32 + g:33 + g], pb6[:, 0:SL],
                                               ALU.mult, ALU.add))
        self.after(V, (eV, vv), (eS, sil))
        self.eV.inc(V.tensor_tensor(t["ygat"][:, g * SL:(g + 1) * SL], t["ytmp"][:],
                                    t["silures"][:], ALU.mult))

    def _mamba_outproj(self, sl):
        t, ap = self.t, self.ap
        V, TE, G, SY, SC = self.V, self.TE, self.G, self.SY, self.SC
        eV, eT = self.eV, self.eT
        pb = [t[f"pb{i}"] for i in range(7)]
        col0 = HALO + sl * SL
        for dmt in range(8):
            wbuf = dmt % 2
            wsem = self.sOWa if wbuf == 0 else self.sOWb
            self.after(G, (eT, self._ow_tok[wbuf]))
            self.dma("G", wsem,
                     t["outw_tile"][:, wbuf * 2048:(wbuf + 1) * 2048].rearrange(
                         "p (gch m) -> p gch m", m=128),
                     ap["outw"].rearrange("(gch p) dm -> p gch dm", p=128)[:, :, dmt * 128:(dmt + 1) * 128])
            self.after(SY, (self.eV, self._xnres_tok))
            self.dma("S", self.sXR, t["xnres"][:, 0:SL],
                     ap["xnT"][dmt * 128:(dmt + 1) * 128, col0:col0 + SL])
            ps = pb[4 + (dmt % 2)]
            self.after(TE, wsem, eV, (self.eS, self._delta_tok))
            for gg in range(16):
                eT.inc(TE.matmul(ps[:, 0:SL],
                                 t["outw_tile"][:, wbuf * 2048 + gg * 128:wbuf * 2048 + (gg + 1) * 128],
                                 t["ygat"][:, gg * SL:(gg + 1) * SL],
                                 start=(gg == 0), stop=(gg == 15)))
            self._ow_tok[wbuf] = eT.v
            self.after(V, eT, self.sXR, self.sSM, self.sY0)
            self._xnres_tok = eV.inc(V.tensor_tensor(t["stg512"][:], ps[:, 0:SL],
                                                     t["xnres"][:, 0:SL], ALU.add))
            self.after(G, (eV, self._xnres_tok))
            self.dma("G", self.sSM, ap["smamba"][dmt * 128:(dmt + 1) * 128, sl * SL:(sl + 1) * SL],
                     t["stg512"][:])

    def l0_mamba(self):
        t, ap = self.t, self.ap
        V, G = self.V, self.G
        self._xnsl_tok = 0
        self._srep_tok = self.eV.v
        self._inw_tok = [0, 0]
        self._ow_tok = [0, 0]
        self._psA_tok = [0, 0]
        self._pre_tok = 0
        self._brep_tok = 0
        self._xnres_tok = 0
        self._delta_tok = 0
        self._y0rl_tok = 0
        self._moe_h1_tok = [0, 0]
        self._moe_hs_tok = [0, 0]
        self._moe_sw2_tok = 0
        self._moe_sh_evac = [0, 0]
        self._moe_he_tok = 0
        self._moe_w_tok = 0
        self._moe_gelu_tok = [0, 0]
        self._moe_y_tok = [0, 0]
        self._sg_evac = 0
        self._p5_xt_tok = 0
        self.phase_barrier(self.sIWst, self.sXN, self.sSR, self.sX, self.cS, self.cG,
                           self.eV, self.eS, self.eT)
        for sl in range(NSLICE):
            self._mamba_frontend(sl)
            self._xnsl_tok = self.eT.v
            for jt in range(16):
                self.after(V, (self.eS, self._delta_tok), self.eV, self.eT)
                self.eV.inc(V.tensor_tensor(t["dub"][:, jt * SL:(jt + 1) * SL],
                                            t["db"][:, jt * SL:(jt + 1) * SL],
                                            t["ucb"][:, jt * SL:(jt + 1) * SL], ALU.mult))
            for g in range(16):
                self._mamba_scan_group(g, fixup=False)
                self._mamba_tail(sl, g, fixup=False)
            if sl > 0:
                self._mamba_outproj(sl)
        self.after(G, self.eV)
        self.dma("G", self.sST, ap["ag_state_in"], t["states"][:])
        self.after(G, self.sST)
        self.ccs.inc(G.collective_compute(
            "AllGather", ALU.bypass, replica_groups=PAIR_GROUPS,
            ins=[self.d_ag_state_in.ap().opt()], outs=[self.d_ag_state_out.ap().opt()]))

    def l0_fixup_gate(self):
        t, ap = self.t, self.ap
        V, SC, TE, G, SY = self.V, self.SC, self.TE, self.G, self.SY
        eV, eS, eT = self.eV, self.eS, self.eT
        self.phase_barrier(self.eV, self.eS, self.eT, self.eG, self.dsem("sGA"),
                           self.dsem("sWS"), self.dsem("sWE"), self.dsem("sXR2"),
                           self.dsem("sXE"), self.dsem("sXL"))
        self.after(SY, self.ccs)
        self.dma("S", self.sSTL, t["dA"][:, 0:NDN], ap["ag_state_out"][0:128, :])
        self.after(V, self.sSTL, self.eG, eV)
        eV.inc(V.tensor_scalar_mul(t["s0init"][:], t["dA"][:, 0:NDN], t["hcol"][:, 0:1]))
        self._mamba_frontend(0)
        self._xnsl_tok = self.eT.v
        for g in range(16):
            self.after(self.SY, (self.eV, self._y0rl_tok))
            self.dma("S", self.sY0L, t["dub"][:, 0:SL], ap["y0_d"][g, :, :])
            self._mamba_scan_group(g, fixup=True)
            self._mamba_tail(0, g, fixup=True)
            self._y0rl_tok = self.eV.v
        self._mamba_outproj(0)
        if "dbg_s" in self.debug:
            self.after(self.G, self.sSM)
            self.dma("G", self.dsem("dbgs"), self.ap["dbg_s"], self.ap["smamba"])
        self.l0_gate()

    def l0_gate(self):
        pass

    def l0_moe(self):
        t, ap = self.t, self.ap
        V, SC, TE, G, SY = self.V, self.SC, self.TE, self.G, self.SY
        eV, eS, eT = self.eV, self.eS, self.eT
        pb = [t[f"pb{i}"] for i in range(7)]
        self.phase_barrier(self.eV, self.eS, self.eT, self.eG, self.sSM, self.sY0,
                           self.sXS, self.sWa, self.sWb, self.sOWa, self.sOWb, self.sXR)
        sXE = self.dsem("sXE")
        sXL = self.dsem("sXL")
        sWS = self.dsem("sWS")
        sGA = self.dsem("sGA")
        sXR2 = self.dsem("sXR2")
        sWE = self.dsem("sWE")
        xnT3 = ap["xnT"].rearrange("(c p) tp -> c p tp", p=128)
        xeP3 = ap["xeP"]

        stg_tok = eV.v
        for sl in range(NSLICE):
            self.after(SY, (eV, stg_tok))
            self.dma("S", sXL, t["xnsl"][:].rearrange("p (c m) -> p c m", m=SL + 16)[:, :, 0:SL],
                     xnT3[:, :, HALO + sl * SL:HALO + (sl + 1) * SL].rearrange("c p m -> p c m"))
            for r in range(8):
                self.after(V, sXL, self.dsem("sXE"))
                cp = eV.inc(V.tensor_copy(
                    t["stg512"][:].rearrange("p (c j) -> p c j", j=64),
                    t["xnsl"][:].rearrange("p (c m) -> p c m", m=SL + 16)[:, :, r:SL:8]))
                stg_tok = cp
                self.after(G, (eV, cp))
                self.dma("G", sXE, xeP3[r, :, :].rearrange("p (c m) -> p c m", m=SL)[:, :, sl * 64:(sl + 1) * 64],
                         t["stg512"][:].rearrange("p (c j) -> p c j", j=64))

        hblk = lambda hpt: t["hssP"][:, hpt * 1024:hpt * 1024 + 1024]
        sh_tok = [eT.v, eT.v]
        xl_tok = eT.v
        for q in range(4):
            r0, r1 = 2 * q, 2 * q + 1
            self.after(SY, (eT, xl_tok), sXE)
            self.dma("S", sXR2, t["xep0"][:], xeP3[r0, :, :])
            self.dma("S", sXR2, t["xep1"][:], xeP3[r1, :, :])
            for hpt in range(16):
                self.after(G, (eT, sh_tok[0]))
                self.dma("G", sWS, t["shwP"][:, 0:1024].rearrange("p (c m) -> p c m", m=128),
                         ap["sw1"].rearrange("(c p) hs -> p c hs", p=128)[:, :, hpt * 128:(hpt + 1) * 128])
                self.dma("G", sWS, t["shwP"][:, 1024:2048].rearrange("p (c m) -> p c m", m=128),
                         ap["sw3"].rearrange("(c p) hs -> p c hs", p=128)[:, :, hpt * 128:(hpt + 1) * 128])
                for ri, rr in enumerate((r0, r1)):
                    xe_t = t["xep0"] if ri == 0 else t["xep1"]
                    p1, p3 = pb[0 + ri], pb[2 + ri]
                    self.after(TE, sWS, sXR2, (eS, self._moe_h1_tok[ri]), (eV, self._moe_hs_tok[ri]))
                    for kc in range(8):
                        eT.inc(TE.matmul(p1[:, 0:SL], t["shwP"][:, kc * 128:(kc + 1) * 128],
                                         xe_t[:, kc * SL:(kc + 1) * SL], start=(kc == 0), stop=(kc == 7)))
                    for kc in range(8):
                        eT.inc(TE.matmul(p3[:, 0:SL], t["shwP"][:, 1024 + kc * 128:1024 + (kc + 1) * 128],
                                         xe_t[:, kc * SL:(kc + 1) * SL], start=(kc == 0), stop=(kc == 7)))
                    self.after(SC, eT, eV)
                    h1v = eS.inc(SC.activation(t["h13"][:], p1[:, 0:SL], AF.Silu,
                                               bias=t["colbank"][:, 128 + hpt:129 + hpt]))
                    self._moe_h1_tok[ri] = h1v
                    self.after(V, (eS, h1v))
                    self._moe_hs_tok[ri] = eV.inc(V.scalar_tensor_tensor(
                        hblk(hpt)[:, ri * SL:(ri + 1) * SL], p3[:, 0:SL],
                        t["colbank"][:, 144 + hpt:145 + hpt], t["h13"][:], ALU.add, ALU.mult))
                sh_tok[0] = eT.v
            for dmt in range(8):
                self.after(G, eT)
                self.dma("G", sWS, t["shwP"][:, 0:2048].rearrange("p (c m) -> p c m", m=128),
                         ap["sw2"].rearrange("(c p) dm -> p c dm", p=128)[:, :, dmt * 128:(dmt + 1) * 128])
                for ri, rr in enumerate((r0, r1)):
                    ps = pb[4 + ri]
                    self.after(TE, sWS, eV, (eV, self._moe_sh_evac[ri]))
                    for hpt in range(16):
                        eT.inc(TE.matmul(ps[:, 0:SL], t["shwP"][:, hpt * 128:(hpt + 1) * 128],
                                         hblk(hpt)[:, ri * SL:(ri + 1) * SL],
                                         start=(hpt == 0), stop=(hpt == 15)))
                    self._moe_sw2_tok = eT.v
                    self.after(V, eT, self.dsem("sGA"))
                    ev = eV.inc(V.tensor_scalar_add(t["stg512"][:], ps[:, 0:SL],
                                                    t["colbank"][:, 160 + dmt:161 + dmt]))
                    self._moe_sh_evac[ri] = ev
                    self.after(G, (eV, ev))
                    self.dma("G", sGA, ap["gacc_d"][dmt, :, rr * SL:(rr + 1) * SL], t["stg512"][:])
            xl_tok = eT.v

        wexp3 = t["wexp"][:].rearrange("p (b m) -> p b m", m=128)
        self._moe_he_tok = eT.v
        self._moe_w_tok = eT.v
        for e in range(NEXP):
            rr = [(3 * e) % 8, (3 * e + 3) % 8]
            self.after(SY, (eT, self._moe_he_tok))
            self.dma("S", sXR2, t["xe0"][:], xeP3[rr[0], :, :])
            self.dma("S", sXR2, t["xe1"][:], xeP3[rr[1], :, :])
            self.after(G, (eT, self._moe_w_tok))
            self.dma("G", sWE, t["wexp"][:].rearrange("p (ht kc m) -> p ht kc m", ht=8, kc=8),
                     ap["w1e"].rearrange("e (kc p) hr -> e p kc hr", p=128)[e]
                     .rearrange("p kc (ht m) -> p ht kc m", m=128))
            for ki in range(2):
                xe_t = t["xe0"] if ki == 0 else t["xe1"]
                he_t = t["he0"] if ki == 0 else t["he1"]
                for ht in range(8):
                    ps = pb[ht % 2]
                    self.after(TE, sWE, sXR2, (eS, self._moe_gelu_tok[ht % 2]))
                    for kc in range(8):
                        eT.inc(TE.matmul(ps[:, 0:SL], wexp3[:, ht * 8 + kc, :],
                                         xe_t[:, kc * SL:(kc + 1) * SL], start=(kc == 0), stop=(kc == 7)))
                    self.after(SC, eT)
                    self._moe_gelu_tok[ht % 2] = eS.inc(SC.activation(
                        he_t[:, ht * SL:(ht + 1) * SL], ps[:, 0:SL], AF.Gelu,
                        bias=t["colbank"][:, 168 + e * 8 + ht:169 + e * 8 + ht]))
            self._moe_he_tok = eT.v
            self.after(G, (eT, self._moe_he_tok))
            self.dma("G", sWE, t["wexp"][:].rearrange("p (dmt ht m) -> p dmt ht m", dmt=8, ht=8),
                     ap["w2e"].rearrange("e (ht p) dm -> e p ht dm", p=128)[e]
                     .rearrange("p ht (dmt m) -> p dmt ht m", m=128))
            for ki in range(2):
                he_t = t["he0"] if ki == 0 else t["he1"]
                for dmt in range(8):
                    ps = pb[4 + dmt % 2]
                    self.after(TE, sWE, eS, (eV, self._moe_y_tok[dmt % 2]))
                    for ht in range(8):
                        eT.inc(TE.matmul(ps[:, 0:SL], wexp3[:, dmt * 8 + ht, :],
                                         he_t[:, ht * SL:(ht + 1) * SL], start=(ht == 0), stop=(ht == 7)))
                    self.after(V, eT, sGA)
                    yv = eV.inc(V.tensor_scalar(t["stg512"][:], ps[:, 0:SL], 0.5,
                                                t["colbank"][:, 232 + e * 8 + dmt:233 + e * 8 + dmt],
                                                ALU.mult, ALU.add))
                    self._moe_y_tok[dmt % 2] = yv
                    self.after(G, (eV, yv))
                    self.G.dma_start(out=ap["gacc_d"][dmt, :, rr[ki] * SL:(rr[ki] + 1) * SL],
                                     in_=t["stg512"][:], accum_op=ALU.add).then_inc(sGA.h, 16)
                    sGA.v += 16
            self._moe_w_tok = eT.v
        self._moe_done = True

    def l0_gate(self):
        t, ap = self.t, self.ap
        V, SC, TE, G, SY = self.V, self.SC, self.TE, self.G, self.SY
        eV, eS, eT = self.eV, self.eS, self.eT
        self.phase_barrier(self.eV, self.eS, self.eT, self.eG,
                           self.dsem("sGA"), self.sSM, self.sY0L)
        sGL = self.dsem("sGL")
        sSG = self.dsem("sSG")
        sX1 = self.dsem("sX1")
        sX1S = self.dsem("sX1S")
        sXN1 = self.dsem("sXN1")
        self.dram("sgT", (TOK, DM), BF16)
        apsg = self.ap["sgT"]
        vA = eV.v
        for dmt in range(8):
            self.after(SY, (eV, vA))
            self.dma("S", sGL, t["gslice"][:], ap["gacc_d"][dmt, :, :])
            self.after(SC, sGL, (eV, vA), eT)
            gs = eS.inc(SC.activation(t["gsilu"][:], t["gslice"][:], AF.Silu, scale=0.5))
            for r in range(8):
                self.after(V, (eS, gs), self.dsem("dbgs"))
                eV.inc(V.tensor_copy(t["ssl"][:, 0:TOK][:, r:TOK:8], t["gsilu"][:, r * SL:(r + 1) * SL]))
            self.after(SY, (eS, gs))
            self.dma("S", sGL, t["gslice"][:], ap["smamba"][dmt * 128:(dmt + 1) * 128, :])
            self.after(V, sGL, eV)
            vA = eV.inc(V.tensor_tensor(t["gsilu"][:], t["gslice"][:], t["ssl"][:, 0:TOK], ALU.mult))
            if "dbg_g" in self.debug:
                self.after(G, eV)
                self.dma("G", self.dsem("dbgs"), ap["dbg_g"][dmt * 128:(dmt + 1) * 128, :], t["ssl"][:, 0:TOK])
            for tt in range(32):
                pcol = (tt % 8) * 128
                if tt % 8 == 0:
                    self.after(TE, (eV, vA), eV if tt else eV)
                    if tt > 0 or dmt > 0:
                        self.after(TE, (eV, self._sg_evac))
                eT.inc(TE.transpose(t["ptr"][:, pcol:pcol + 128],
                                    t["gsilu"][:, tt * 128:(tt + 1) * 128], t["ident"][:]))
                if tt % 8 == 7:
                    self.after(V, eT, sSG)
                    self._sg_evac = eV.inc(V.tensor_copy(t["trstage"][:], t["ptr"][:, 0:1024]))
                    self.after(G, (eV, self._sg_evac))
                    self.dma("G", sSG, apsg[(tt - 7) * 128:(tt + 1) * 128, dmt * 128:(dmt + 1) * 128]
                             .rearrange("(nt p) dm -> p nt dm", p=128),
                             t["trstage"][:].rearrange("p (nt m) -> p nt m", m=128))
        NTT = TOK // 128
        xnT1 = ap["xn1T_in"].rearrange("(c p) tp -> c p tp", p=128)
        for i in range(NTT):
            self.after(SY, (eV, self._p5_xt_tok), sSG)
            self.dma("S", sX1, t["xt"][:, 0:DM], ap["x"][HALO + i * 128:HALO + (i + 1) * 128, :])
            self.dma("S", sX1, t["xt"][:, DM:2 * DM].bitcast(BF16)[:, 0:DM] if False else t["xnb2"][:],
                     apsg[i * 128:(i + 1) * 128, :])
            st = t["stats"]
            self.after(V, sX1, sSG, eS, eV, sX1S, self.dsem("dbgs"))
            x1v = eV.inc(V.tensor_tensor(t["sqscrap"][:], t["xt"][:, 0:DM], t["xnb2"][:], ALU.add))
            self._p5_xt_tok = x1v
            self.after(G, (eV, x1v))
            self.dma("G", sX1S, ap["x1_d"][i * 128:(i + 1) * 128, :], t["sqscrap"][:])
            if self.upto == "l0":
                self.dma("G", sX1S, ap["out"][i * 128:(i + 1) * 128, :], t["sqscrap"][:])
            if "dbg_x1" in self.debug:
                self.dma("G", self.dsem("dbgs"), ap["dbg_x1"][i * 128:(i + 1) * 128, :], t["sqscrap"][:])
            self.after(V, (eV, x1v))
            eV.inc(V.tensor_reduce(st[:, 0:1], t["sqscrap"][:], axis=AX.X, op=ALU.add))
            self.after(SC, (eV, x1v), eV)
            eS.inc(SC.activation(t["xt"][:, DM:2 * DM], t["sqscrap"][:], AF.Square, accum_out=st[:, 1:2]))
            self.after(V, eS)
            eV.inc(V.tensor_scalar_mul(st[:, 2:3], st[:, 0:1], 1.0 / DM))
            self.after(V, eV)
            eV.inc(V.tensor_tensor(st[:, 3:4], st[:, 2:3], st[:, 2:3], ALU.mult))
            eV.inc(V.tensor_scalar(st[:, 4:5], st[:, 1:2], 1.0 / DM, EPS, ALU.mult, ALU.add))
            self.after(V, eV)
            eV.inc(V.tensor_tensor(st[:, 4:5], st[:, 4:5], st[:, 3:4], ALU.subtract))
            self.after(SC, eV)
            eS.inc(SC.activation(st[:, 5:6], st[:, 4:5], AF.Sqrt))
            self.after(V, eS)
            eV.inc(V.reciprocal(st[:, 6:7], st[:, 5:6]))
            self.after(V, eV, eT)
            xnv = eV.inc(V.tensor_scalar(t["xnb"][:], t["sqscrap"][:], st[:, 2:3], st[:, 6:7],
                                         ALU.subtract, ALU.mult))
            self.after(TE, (eV, xnv), eV)
            for j in range(8):
                eT.inc(TE.transpose(t["ptr"][:, j * 128:(j + 1) * 128],
                                    t["xnb"][:, j * 128:(j + 1) * 128], t["ident"][:]))
            self.after(V, eT, sXN1)
            trv = eV.inc(V.tensor_copy(t["trstage"][:], t["ptr"][:, 0:1024]))
            self.after(G, (eV, trv))
            self.dma("G", sXN1, xnT1[:, :, i * 128:(i + 1) * 128].rearrange("c p m -> p c m"),
                     t["trstage"][:].rearrange("p (c m) -> p c m", m=128))
        self.after(G, sXN1)
        self.ccs.inc(G.collective_compute(
            "AllGather", ALU.bypass, replica_groups=PAIR_GROUPS,
            ins=[self.d_xn1T_in.ap().opt()], outs=[self.d_xn1T_out.ap().opt()]))
        if "dbg_xn1" in self.debug:
            self.after(G, self.ccs)
            self.dma("G", self.dsem("dbgs"), ap["dbg_xn1"], ap["xn1T_out"])

    def _attn_body(self, blocks):
        t, ap = self.t, self.ap
        V, SC, TE, G, SY = self.V, self.SC, self.TE, self.G, self.SY
        eV, eS, eT = self.eV, self.eS, self.eT
        pb = [t[f"pb{i}"] for i in range(7)]
        sKV = self.dsem("sKV")
        sWQ = self.dsem("sWQ")
        sAC = self.dsem("sAC")
        sQB = self.dsem("sQB")
        xnpair = ap["xn1T_out"]

        def xn_cols(g0, n):
            slot = g0 // TOK
            c0 = g0 % TOK
            return xnpair.rearrange("(s c p) tp -> s c p tp", s=2, p=128)[slot, :, :, c0:c0 + n] \
                .rearrange("c p m -> p c m")

        kps = sorted({kp for blk in blocks for kp in blk[1]})
        pad_kps = [0] * (8 - len(kps))
        KT, VT, QT, XS = t["a_KT"], t["a_VT"], t["a_QT"], t["a_XS"]
        expS = t["a_expS"]
        astg = t["a_stgF"]
        wq, wkv = t["a_wq"], t["a_wkv"]
        self.after(self.G, (eT, self._at_w_tok))
        wt3 = ap["wqkvT"].rearrange("(c p) e -> p c e", p=128)
        self.dma("G", sWQ, wq[:].rearrange("p (c m) -> p c m", m=1024), wt3[:, :, 0:ADIM])
        self.dma("G", sWQ, wkv[:].rearrange("p (c m) -> p c m", m=2048), wt3[:, :, ADIM:3 * ADIM])
        def build_kv(kp):
            for ks in range(2):
                k0 = kp * 1024 + ks * 512
                self.after(SY, (eT, self._at_xs_tok), (eT, self._at_q_tok), sKV)
                self.dma("S", sKV, XS[:].rearrange("p (c m) -> p c m", m=512), xn_cols(k0, 512))
                for et in range(8):
                    ps = pb[5]
                    self.after(TE, sKV, sWQ, (eV, self._at_kv_evac))
                    for kc in range(8):
                        eT.inc(TE.matmul(ps[:, 0:512],
                                         wkv[:, kc * 2048 + et * 128:kc * 2048 + (et + 1) * 128],
                                         XS[:, kc * 512:(kc + 1) * 512],
                                         start=(kc == 0), stop=(kc == 7)))
                    self.after(V, eT)
                    self._at_kv_evac = eV.inc(V.tensor_scalar_add(
                        KT[:, et * 1024 + ks * 512:et * 1024 + (ks + 1) * 512], ps[:, 0:512],
                        t["colbank"][:, 304 + 8 + et:305 + 8 + et]))
                for ktl in range(4):
                    for vh in range(2):
                        ps = pb[6]
                        self.after(TE, (eV, self._at_kv_evac), eV)
                        for kc in range(8):
                            eT.inc(TE.matmul(
                                ps[:, 0:512],
                                XS[:, kc * 512 + ktl * 128:kc * 512 + (ktl + 1) * 128],
                                wkv[:, kc * 2048 + 1024 + vh * 512:kc * 2048 + 1024 + (vh + 1) * 512],
                                start=(kc == 0), stop=False))
                        eT.inc(TE.matmul(ps[:, 0:512], t["onesrow1"][:],
                                         t["bvrow"][0:1, vh * 512:(vh + 1) * 512],
                                         start=False, stop=True))
                        self.after(V, eT)
                        self._at_kv_evac = eV.inc(V.tensor_copy(
                            VT[:, (ks * 4 + ktl) * 1024 + vh * 512:
                               (ks * 4 + ktl) * 1024 + (vh + 1) * 512],
                            ps[:, 0:512]))
                self._at_xs_tok = eT.v

        for kp in kps:
            build_kv(kp)
            for blk_i, (g0b, kplist) in enumerate(blocks):
                if kp not in kplist:
                    continue
                first_kp = (kp == kplist[0])
                for q5 in range(4):
                    g0 = g0b + q5 * 512
                    self.after(SY, (eT, self._at_q_tok), (eT, self._at_xs_tok), sKV)
                    self.dma("S", sKV, XS[:].rearrange("p (c m) -> p c m", m=512), xn_cols(g0, 512))
                    for et in range(8):
                        ps = pb[5]
                        self.after(TE, sKV, (eV, self._at_q_evac))
                        for kc in range(8):
                            eT.inc(TE.matmul(ps[:, 0:512],
                                             wq[:, kc * 1024 + et * 128:kc * 1024 + (et + 1) * 128],
                                             XS[:, kc * 512:(kc + 1) * 512],
                                             start=(kc == 0), stop=(kc == 7)))
                        self.after(V, eT)
                        self._at_q_evac = eV.inc(V.tensor_scalar_add(
                            QT[:, et * 512:(et + 1) * 512], ps[:, 0:512],
                            t["colbank"][:, 304 + et:305 + et]))
                    self._at_q_tok = eT.v
                    for ktl in range(8):
                        k0 = kp * 1024 + ktl * 128
                        if k0 > g0 + 384:
                            continue
                        ps = pb[ktl % 2]
                        self.after(TE, (eV, self._at_q_evac), (eS, self._at_exp_tok[ktl % 2]),
                                   (eV, self._at_mask_tok[ktl % 2]))
                        for et in range(8):
                            eT.inc(TE.matmul(ps[:, 0:512],
                                             KT[:, et * 1024 + (k0 % 1024):et * 1024 + (k0 % 1024) + 128],
                                             QT[:, et * 512:(et + 1) * 512],
                                             start=(et == 0), stop=(et == 7)))
                        sdiag = (k0 - g0) // 128
                        if 0 <= sdiag < 4:
                            self.after(V, eT, self.cS)
                            self._at_mask_tok[ktl % 2] = eV.inc(V.tensor_tensor(
                                ps[:, sdiag * 128:(sdiag + 1) * 128],
                                ps[:, sdiag * 128:(sdiag + 1) * 128],
                                t["trilmask_sb"][:], ALU.add))
                        else:
                            self._at_mask_tok[ktl % 2] = eV.v
                        self.after(SC, eT, (eV, self._at_mask_tok[ktl % 2]), eS)
                        self._at_exp_tok[ktl % 2] = eS.inc(SC.activation(
                            expS[:, ktl * 512:(ktl + 1) * 512], ps[:, 0:512], AF.Exp,
                            scale=1.0 / 32.0, bias=t["colbank"][:, 440:441]))
                    for s in range(4):
                        q0 = g0 + s * 128
                        kt_list = [ktl for ktl in range(8) if kp * 1024 + ktl * 128 <= q0]
                        if not kt_list:
                            continue
                        self.after(TE, eS, (eV, self._at_pv_evac))
                        for ktl in kt_list:
                            st_ap = expS[:, ktl * 512 + s * 128:ktl * 512 + (s + 1) * 128]
                            first = (ktl == kt_list[0])
                            last = (ktl == kt_list[-1])
                            eT.inc(TE.matmul(pb[2][:, 0:512], st_ap, VT[:, ktl * 1024:ktl * 1024 + 512],
                                             start=first, stop=last))
                            eT.inc(TE.matmul(pb[3][:, 0:512], st_ap, VT[:, ktl * 1024 + 512:(ktl + 1) * 1024],
                                             start=first, stop=last))
                            eT.inc(TE.matmul(pb[4][:, 0:1], st_ap, t["onescol"][:, 0:1],
                                             start=first, stop=last))
                        self.after(V, eT, sAC)
                        eV.inc(V.tensor_copy(astg[:, 0:512], pb[2][:, 0:512]))
                        self.after(V, eV)
                        eV.inc(V.tensor_copy(astg[:, 512:1024], pb[3][:, 0:512]))
                        self.after(V, eV)
                        self._at_pv_evac = eV.inc(V.tensor_copy(astg[:, 1024:1025], pb[4][:, 0:1]))
                        self.after(G, (eV, self._at_pv_evac))
                        lrow = g0b // 2048
                        arows = ap["attacc"][(blk_i * 2048 + q5 * 512 + s * 128):
                                             (blk_i * 2048 + q5 * 512 + (s + 1) * 128), 0:1025]
                        if first_kp:
                            self.dma("G", sAC, arows, astg[:, 0:1025])
                        else:
                            self.G.dma_start(out=arows, in_=astg[:, 0:1025],
                                             accum_op=ALU.add).then_inc(sAC.h, 16)
                            sAC.v += 16
        for kp in pad_kps:
            build_kv(kp)

    def _attn_norm_out(self, blocks, h):
        t, ap = self.t, self.ap
        V, SC, G, SY = self.V, self.SC, self.G, self.SY
        eV, eS = self.eV, self.eS
        sAC = self.dsem("sAC")
        sS1 = self.dsem("sS1")
        astg = t["a_stgF"]
        for blk_i, (g0b, kplist) in enumerate(blocks):
            own = (g0b // TOK) == h
            for ss in range(16):
                self.after(SY, sAC, (eV, self._at_no_tok), sS1)
                self.dma("S", self.dsem("sNL"), astg[:, 0:1025],
                         ap["attacc"][blk_i * 2048 + ss * 128:blk_i * 2048 + (ss + 1) * 128, 0:1025])
                self.after(V, self.dsem("sNL"), eV)
                eV.inc(V.reciprocal(t["stats"][:, 0:1], astg[:, 1024:1025]))
                self.after(V, eV, sS1)
                nv = eV.inc(V.tensor_scalar_mul(astg[:, 1040:2064], astg[:, 0:1024], t["stats"][:, 0:1]))
                self._at_no_tok = nv
                self.after(G, (eV, nv))
                dst_row = (g0b % TOK) + ss * 128
                if own:
                    self.dma("G", sS1, ap["s1_d"][dst_row:dst_row + 128, :], astg[:, 1040:2064])
                else:
                    self.dma("G", sS1, ap["ag_s1_in"][ss * 128:(ss + 1) * 128, :], astg[:, 1040:2064])

    def l1_attn(self):
        t, ap = self.t, self.ap
        eV, eS, eT = self.eV, self.eS, self.eT
        self.phase_barrier(self.eV, self.eS, self.eT, self.eG, self.dsem("sGL"),
                           self.dsem("sSG"), self.dsem("sX1"), self.dsem("sX1S"),
                           self.dsem("sXN1"), self.dsem("dbgs"))
        self.dma("S", self.cS, t["colbank"][:, 304:304 + 24], ap["bqkv_cols"])
        self.dma("S", self.cS, t["trilmask_sb"][:], ap["trilmask_add"])
        self.phase_barrier(self.cS)
        self.eG.inc(self.G.memset(t["onescol"][:], 1.0))
        self.eG.inc(self.G.memset(t["onesrow1"][:], 1.0))
        self.eG.inc(self.G.memset(t["colbank"][:, 440:441], -8.0))
        self.dma("G", self.cG, t["bvrow"][:], ap["bqkv_vrow"])
        self.phase_barrier(self.cG, self.eG)
        self.after(self.G, self.ccs)
        self.after(self.SY, self.ccs)

        def reset_toks():
            self._at_w_tok = 0
            self._at_xs_tok = 0
            self._at_kv_evac = 0
            self._at_q_tok = 0
            self._at_q_evac = 0
            self._at_exp_tok = [0, 0]
            self._at_mask_tok = [0, 0]
            self._at_pv_evac = 0
            self._at_no_tok = 0
        reset_toks()

        body_blocks = [
            [(0, [0, 1]), (6144, [0, 1, 2, 3, 4, 5, 6, 7])],
            [(2048, [0, 1, 2, 3]), (4096, [0, 1, 2, 3, 4, 5])],
        ]
        engines = [self.V, self.SC, self.TE, self.G, self.SY]
        idx = []
        for e in engines:
            pid = e.partition_id()
            idx.append(e.scalar_reg_alu(ALU.bitwise_and, pid, 1))
        sems_before = None
        for body in self.nc.Switch(engines=engines, index=idx, n=2):
            snap = {k: s.v for k, s in self.__dict__.items() if isinstance(s, Sem)}
            snap.update({k: s.v for k, s in self._dyn_sems.items()})
            if body == 0:
                self._attn_body(body_blocks[0])
                self._attn_norm_out(body_blocks[0], 0)
                sems_before = ({k: s.v for k, s in self.__dict__.items() if isinstance(s, Sem)},
                               {k: s.v for k, s in self._dyn_sems.items()}, snap)
            else:
                b0_named, b0_dyn, snap0 = sems_before
                for k, s in self.__dict__.items():
                    if isinstance(s, Sem):
                        s.v = snap0.get(k, s.v)
                for k, s in self._dyn_sems.items():
                    s.v = snap0.get(k, 0)
                reset_toks()
                self._attn_body(body_blocks[1])
                self._attn_norm_out(body_blocks[1], 1)
                for k, s in self.__dict__.items():
                    if isinstance(s, Sem):
                        assert s.v == b0_named.get(k, s.v), f"sem {k} mismatch {s.v} vs {b0_named.get(k)}"
                for k, s in self._dyn_sems.items():
                    assert s.v == b0_dyn.get(k, s.v), f"dyn sem {k} mismatch"
        sS1 = self.dsem("sS1")
        self.after(self.G, sS1)
        self.ccs.inc(self.G.collective_compute(
            "AllGather", ALU.bypass, replica_groups=PAIR_GROUPS,
            ins=[self.d_ag_s1_in.ap().opt()], outs=[self.d_ag_s1_out.ap().opt()]))
        sS1L = self.dsem("sS1L")
        idx2 = [self.SY.scalar_reg_alu(ALU.bitwise_and, self.SY.partition_id(), 1)]
        self.after(self.SY, self.ccs)
        v0 = sS1L.v
        for body in self.nc.Switch(engines=[self.SY], index=idx2, n=2):
            slot = 1 - body
            self.SY.dma_start(out=ap["s1_d"][2048:4096, :],
                              in_=ap["ag_s1_out"][slot * 2048:(slot + 1) * 2048, :]).then_inc(sS1L.h, 16)
        sS1L.v = v0 + 16

    def _ffn_qknorm_rot(self, buf_off, tbl0, is_q, layer):
        t, ap = self.t, self.ap
        V, SC, TE, SY = self.V, self.SC, self.TE, self.SY
        eV, eS, eT = self.eV, self.eS, self.eT
        pb = [t[f"pb{i}"] for i in range(7)]
        PB = t["poolB"]
        sRT = self.dsem("sRT")
        self.after(SY, (self.eV, self._f_rt_tok))
        cs_ap = ap["rot_q_cs"] if is_q else ap["rot_k_cs"]
        sn_ap = ap["rot_q_sn"] if is_q else ap["rot_k_sn"]
        self.dma("S", sRT, t["f_rcs"][:], cs_ap[:, tbl0:tbl0 + 512])
        self.dma("S", sRT, t["f_rsn"][:], sn_ap[:, tbl0:tbl0 + 512])
        qscol = t["colbank"][:, 360 + layer:361 + layer] if is_q else \
            t["colbank"][:, 362 + layer:363 + layer]
        for pt in range(4):
            blk = PB[:, buf_off + pt * 512:buf_off + (pt + 1) * 512]
            tmp = t["f_kqtmp"]
            self.after(SC, eV, eS)
            sq = eS.inc(SC.activation(tmp[:, 0:512], blk, AF.Square))
            self.after(TE, (eS, sq), eV, eS)
            m1 = eT.inc(TE.matmul(pb[5][0:2, 0:512], t["hd01_sb"][:], tmp[:, 0:512],
                                  start=True, stop=True))
            self.after(SC, (eT, m1), eS)
            eS.inc(SC.activation(t["sps"][0:2, 0:512], pb[5][0:2, 0:512], AF.Sqrt))
            self.after(V, eS, eV)
            eV.inc(V.tensor_scalar_max(t["sps"][0:2, 0:512], t["sps"][0:2, 0:512], 1e-12))
            self.after(V, eV)
            eV.inc(V.reciprocal(t["sps"][0:2, 0:512], t["sps"][0:2, 0:512]))
            self.after(V, eV)
            rb = eV.inc(V.tensor_copy(tmp[0:2, 0:512], t["sps"][0:2, 0:512]))
            self.after(TE, (eV, rb))
            m2 = eT.inc(TE.matmul(pb[5][:, 0:512], t["hd01T_sb"][:], tmp[0:2, 0:512],
                                  start=True, stop=True))
            self.after(V, (eT, m2), eV)
            nv = eV.inc(V.scalar_tensor_tensor(blk, blk, qscol, pb[5][:, 0:512],
                                               ALU.mult, ALU.mult))
            self.after(TE, (eV, nv), eV)
            m3 = eT.inc(TE.matmul(pb[6][:, 0:512], t["swap64_sb"][:], blk, start=True, stop=True))
            self.after(V, sRT, eV)
            c1 = eV.inc(V.tensor_tensor(tmp[:, 0:512], blk, t["f_rcs"][:], ALU.mult))
            self.after(V, (eT, m3), (eV, c1))
            c2 = eV.inc(V.tensor_tensor(tmp[:, 512:1024], pb[6][:, 0:512], t["f_rsn"][:], ALU.mult))
            self.after(V, (eV, c2), eV)
            self._f_rt_tok = eV.inc(V.tensor_tensor(blk, tmp[:, 0:512], tmp[:, 512:1024], ALU.add))

    def _ffn_qkv_window(self, layer, w):
        t, ap = self.t, self.ap
        V, TE, SY = self.V, self.TE, self.SY
        eV, eT = self.eV, self.eT
        pb = [t[f"pb{i}"] for i in range(7)]
        PB = t["poolB"]
        sXW = self.dsem("sXW")
        t0 = w * 512
        self.after(SY, (eT, self._f_xw_tok), sXW)
        fxw3 = t["f_xw"][:].rearrange("p (c m) -> p c m", m=1056)
        xf3 = ap["xf_d"].rearrange("(c p) m -> p c m", p=128)
        self.dma("S", sXW, fxw3[:, 0:4, 0:512], xf3[:, 0:4, 513 + t0:513 + t0 + 512])
        self.dma("S", sXW, fxw3[:, 4:8, 0:512], xf3[:, 4:8, 512 + t0:512 + t0 + 512])
        kbuf = 2048 + ((w + 2) % 2) * 2048
        vbuf = 6144 + ((w + 2) % 2) * 2048
        wq = t["f_wqkv"]
        for pt in range(4):
            ps = pb[5]
            self.after(TE, sXW, self.dsem("sWQ2"), (eV, self._f_kq_evac))
            for c in range(8):
                eT.inc(TE.matmul(ps[:, 0:512],
                                 wq[:, c * 1536 + 512 + pt * 128:c * 1536 + 512 + (pt + 1) * 128],
                                 t["f_xw"][:, c * 1056:c * 1056 + 512],
                                 start=(c == 0), stop=(c == 7)))
            self.after(V, eT)
            self._f_kq_evac = eV.inc(V.tensor_copy(PB[:, kbuf + pt * 512:kbuf + (pt + 1) * 512],
                                                   ps[:, 0:512]))
        for ktl in range(4):
            ps = pb[6]
            self.after(TE, (eV, self._f_kq_evac), eV)
            for c in range(8):
                eT.inc(TE.matmul(ps[:, 0:512],
                                 t["f_xw"][:, c * 1056 + ktl * 128:c * 1056 + (ktl + 1) * 128],
                                 wq[:, c * 1536 + 1024:c * 1536 + 1536],
                                 start=(c == 0), stop=(c == 7)))
            self.after(V, eT)
            self._f_kq_evac = eV.inc(V.tensor_copy(PB[:, vbuf + ktl * 512:vbuf + (ktl + 1) * 512],
                                                   ps[:, 0:512]))
        self._ffn_qknorm_rot(kbuf, (w + 1) * 512, False, layer)
        if w >= 0:
            for pt in range(4):
                ps = pb[5]
                self.after(TE, sXW, (eV, self._f_kq_evac), eV)
                for c in range(8):
                    eT.inc(TE.matmul(ps[:, 0:512],
                                     wq[:, c * 1536 + pt * 128:c * 1536 + (pt + 1) * 128],
                                     t["f_xw"][:, c * 1056:c * 1056 + 512],
                                     start=(c == 0), stop=(c == 7)))
                self.after(V, eT)
                self._f_kq_evac = eV.inc(V.tensor_copy(PB[:, pt * 512:(pt + 1) * 512], ps[:, 0:512]))
            self._ffn_qknorm_rot(0, (w + 1) * 512, True, layer)
        self._f_xw_tok = eT.v

    def _ffn_window_attn(self, layer, w):
        t, ap = self.t, self.ap
        V, SC, TE = self.V, self.SC, self.TE
        eV, eS, eT = self.eV, self.eS, self.eT
        pb = [t[f"pb{i}"] for i in range(7)]
        PB = t["poolB"]
        mask_off = 55552 if w == 0 else 59648
        for h in range(HEADS):
            pt = h // 2
            po = (h % 2) * 64
            for kwin in range(2):
                kb = 2048 + ((w - 1 + kwin + 2) % 2) * 2048
                for ktl in range(4):
                    kt_g = kwin * 4 + ktl
                    ps = pb[kt_g % 2]
                    self.after(TE, eS, (eV, self._f_msk))
                    eT.inc(TE.matmul(
                        ps[:, 0:512],
                        PB[po:po + 64, kb + pt * 512 + ktl * 128:kb + pt * 512 + (ktl + 1) * 128],
                        PB[po:po + 64, pt * 512:(pt + 1) * 512],
                        start=True, stop=True))
                    self.after(SC, eT, eS)
                    ex = eS.inc(SC.activation(t["f_expS"][:, kt_g * 512:(kt_g + 1) * 512],
                                              ps[:, 0:512], AF.Exp))
                    self.after(V, (eS, ex), eT)
                    self._f_msk = eV.inc(V.tensor_tensor(
                        t["f_expS"][:, kt_g * 512:(kt_g + 1) * 512],
                        t["f_expS"][:, kt_g * 512:(kt_g + 1) * 512],
                        PB[:, mask_off + kt_g * 512:mask_off + (kt_g + 1) * 512], ALU.mult))
            self.after(TE, (eV, self._f_msk), eV, eS)
            for kt_g in range(8):
                vb = 6144 + ((w - 1 + (kt_g // 4) + 2) % 2) * 2048
                vtile = PB[:, vb + (kt_g % 4) * 512:vb + (kt_g % 4) * 512 + 512]
                eT.inc(TE.matmul(pb[2][0:64, 0:512], vtile[:, 64 * h:64 * h + 64],
                                 t["f_expS"][:, kt_g * 512:(kt_g + 1) * 512],
                                 start=(kt_g == 0), stop=(kt_g == 7)))
                eT.inc(TE.matmul(pb[3][0:1, 0:512], t["onescol"][:, 0:1],
                                 t["f_expS"][:, kt_g * 512:(kt_g + 1) * 512],
                                 start=(kt_g == 0), stop=(kt_g == 7)))
            self.after(V, eT, eV)
            ov = eV.inc(V.tensor_copy(t["f_kqtmp"][0:64, 0:512], pb[2][0:64, 0:512]))
            self.after(V, eV)
            rv = eV.inc(V.reciprocal(t["sps"][0:1, 0:512], pb[3][0:1, 0:512]))
            self.after(V, (eV, rv))
            rb = eV.inc(V.tensor_copy(t["f_kqtmp"][0:1, 512:1024], t["sps"][0:1, 0:512]))
            self.after(TE, (eV, rb), eV)
            mb = eT.inc(TE.matmul(pb[4][0:64, 0:512], t["hd01T_sb"][0:1, 0:64],
                                  t["f_kqtmp"][0:1, 512:1024], start=True, stop=True))
            self.after(V, (eT, mb), (eV, ov))
            self._f_o_tok = eV.inc(V.tensor_tensor(t["f_oh"][0:64, h * 512:(h + 1) * 512],
                                                   t["f_kqtmp"][0:64, 0:512],
                                                   pb[4][0:64, 0:512], ALU.mult))
        ub = 18432 + (w % 2) * 4096
        for dmt in range(8):
            ps = pb[5 + dmt % 2]
            self.after(TE, (eV, self._f_o_tok), (eV, self._f_upd_evac), eV)
            for h in range(HEADS):
                eT.inc(TE.matmul(ps[:, 0:512],
                                 t["f_wo"][0:64, h * 1024 + dmt * 128:h * 1024 + (dmt + 1) * 128],
                                 t["f_oh"][0:64, h * 512:(h + 1) * 512],
                                 start=(h == 0), stop=(h == 7)))
            self.after(V, eT, self.dsem("sUP"))
            self._f_upd_evac = eV.inc(V.tensor_copy(PB[:, ub + dmt * 512:ub + (dmt + 1) * 512],
                                                    ps[:, 0:512]))

    def _ffn_apply_update(self, w):
        t, ap = self.t, self.ap
        V, SY = self.V, self.SY
        eV = self.eV
        sUP = self.dsem("sUP")
        sUR = self.dsem("sUR")
        ub = 18432 + (w % 2) * 4096
        ur = 18432 + ((w + 1) % 2) * 4096
        self.after(SY, self.dsem("sXW"), sUP, (eV, self._f_upd_evac), sUR)
        xf3 = ap["xf_d"].rearrange("(c p) m -> p c m", p=128)
        self.dma("S", sUR, t["poolB"][:, ur:ur + 4096].rearrange("p (c m) -> p c m", m=512),
                 xf3[:, :, 513 + w * 512:513 + (w + 1) * 512])
        self.after(V, sUR, eV)
        addtok = 0
        for c in range(8):
            addtok = eV.inc(V.tensor_tensor(
                t["poolB"][:, ur + c * 512:ur + (c + 1) * 512],
                t["poolB"][:, ur + c * 512:ur + (c + 1) * 512],
                t["poolB"][:, ub + c * 512:ub + (c + 1) * 512], ALU.add))
        self.after(self.G, (eV, addtok))
        self.dma("G", sUP, xf3[:, :, 513 + w * 512:513 + (w + 1) * 512],
                 t["poolB"][:, ur:ur + 4096].rearrange("p (c m) -> p c m", m=512))

    def _ffn_mlp(self, layer):
        t, ap = self.t, self.ap
        V, SC, TE, G, SY = self.V, self.SC, self.TE, self.G, self.SY
        eV, eS, eT = self.eV, self.eS, self.eT
        pb = [t[f"pb{i}"] for i in range(7)]
        self._f_w1_btok = [eT.v, eT.v]
        self._f_w2_btok = [eT.v, eT.v]
        sXC = self.dsem("sXC")
        sW1 = self.dsem("sW1")
        sW2 = self.dsem("sW2")
        sUP = self.dsem("sUP")
        for tc in range(4):
            t0 = tc * 1024
            self.after(SY, (eT, self._f_xc_tok), sXC, sUP)
            fxc3 = t["f_xc"][:].rearrange("p (c m) -> p c m", m=1056)
            xf3 = ap["xf_d"].rearrange("(c p) m -> p c m", p=128)
            self.dma("S", sXC, fxc3[:, 0:4, 0:1024], xf3[:, 0:4, 513 + t0:513 + t0 + 1024])
            self.dma("S", sXC, fxc3[:, 4:8, 0:1025], xf3[:, 4:8, 512 + t0:512 + t0 + 1025])
            self.after(V, sXC, eV)
            for c in range(4, 8):
                if tc > 0:
                    eV.inc(V.tensor_copy(
                        t["f_xc"][:, c * 1056:c * 1056 + 1],
                        t["f_bst"][:, ((tc + 1) % 2) * 4 + c - 4:((tc + 1) % 2) * 4 + c - 3]))
                eV.inc(V.tensor_copy(
                    t["f_bst"][:, (tc % 2) * 4 + c - 4:(tc % 2) * 4 + c - 3],
                    t["f_xc"][:, c * 1056 + 1024:c * 1056 + 1025]))
            for half in range(2):
                self.after(TE, sXC, eV, eS)
                for c in range(8):
                    eT.inc(TE.matmul(pb[5][0:1, 0:512], t["onescol"][:, 0:1],
                                     t["f_xc"][:, c * 1056 + half * 512:c * 1056 + half * 512 + 512],
                                     start=(c == 0), stop=(c == 7)))
                for c in range(8):
                    self.after(SC, eT, eS)
                    eS.inc(SC.activation(t["f_kqtmp"][:, 0:512],
                                         t["f_xc"][:, c * 1056 + half * 512:c * 1056 + half * 512 + 512],
                                         AF.Square))
                    self.after(TE, eS)
                    eT.inc(TE.matmul(pb[6][0:1, 0:512], t["onescol"][:, 0:1],
                                     t["f_kqtmp"][:, 0:512], start=(c == 0), stop=(c == 7)))
                ua = t["uacc"]
                self.after(V, eT, eV)
                eV.inc(V.tensor_scalar_mul(t["sps"][0:1, 0:512], pb[5][0:1, 0:512], 1.0 / DM))
                self.after(V, eV)
                eV.inc(V.tensor_tensor(ua[0:1, 0:512], t["sps"][0:1, 0:512],
                                       t["sps"][0:1, 0:512], ALU.mult))
                eV.inc(V.tensor_scalar(ua[0:1, 512:1024], pb[6][0:1, 0:512], 1.0 / DM, EPS,
                                       ALU.mult, ALU.add))
                self.after(V, eV)
                eV.inc(V.tensor_tensor(ua[0:1, 512:1024], ua[0:1, 512:1024],
                                       ua[0:1, 0:512], ALU.subtract))
                self.after(SC, eV, eS)
                eS.inc(SC.activation(ua[0:1, 512:1024], ua[0:1, 512:1024], AF.Sqrt))
                self.after(V, eS, eV)
                eV.inc(V.reciprocal(ua[0:1, 512:1024], ua[0:1, 512:1024]))
                self.after(V, eV)
                bv = eV.inc(V.tensor_copy(t["f_kqtmp"][0:1, 0:512], t["sps"][0:1, 0:512]))
                self.after(V, eV)
                bv2 = eV.inc(V.tensor_copy(t["f_kqtmp"][0:1, 512:1024], ua[0:1, 512:1024]))
                self.after(TE, (eV, bv2), eV)
                eT.inc(TE.matmul(pb[5][:, 0:512], t["ones_row_sb"][:], t["f_kqtmp"][0:1, 0:512],
                                 start=True, stop=True))
                mm2 = eT.inc(TE.matmul(pb[6][:, 0:512], t["ones_row_sb"][:],
                                       t["f_kqtmp"][0:1, 512:1024], start=True, stop=True))
                self.after(V, (eT, mm2))
                eV.inc(V.tensor_copy(t["f_mrep"][:], pb[5][:, 0:512]))
                self.after(V, eV)
                mrv = eV.inc(V.tensor_copy(t["f_rrep"][:], pb[6][:, 0:512]))
                for c in range(8):
                    self.after(V, (eV, mrv), eV)
                    eV.inc(V.tensor_tensor(t["f_kqtmp"][:, 0:512],
                                           t["f_xc"][:, c * 1056 + half * 512:c * 1056 + half * 512 + 512],
                                           t["f_mrep"][:], ALU.subtract))
                    self.after(V, eV)
                    eV.inc(V.tensor_tensor(t["f_kqtmp"][:, 0:512], t["f_kqtmp"][:, 0:512],
                                           t["f_rrep"][:], ALU.mult))
                    self.after(V, eV)
                    eV.inc(V.tensor_scalar(
                        t["f_hln"][:, c * 1024 + half * 512:c * 1024 + half * 512 + 512],
                        t["f_kqtmp"][:, 0:512],
                        t["colbank"][:, 328 + layer * 8 + c:329 + layer * 8 + c],
                        t["colbank"][:, 344 + layer * 8 + c:345 + layer * 8 + c],
                        ALU.mult, ALU.add))
            PB = t["poolB"]
            for ht in range(22):
                wb = 30720 if ht % 2 == 0 else 47360
                sw = self.dsem("sW1e") if ht % 2 == 0 else self.dsem("sW1o")
                self.after(G, (eT, self._f_w1_btok[ht % 2]))
                self.dma("G", sw, PB[:, wb:wb + 1024].rearrange("p (c m) -> p c m", m=128),
                         ap["w1a_bf"].rearrange("l (c p) hh -> l p c hh", p=128)[layer, :, :, ht * 128:(ht + 1) * 128])
                self.dma("G", sw, PB[:, wb + 1024:wb + 2048].rearrange("p (c m) -> p c m", m=128),
                         ap["w1g_bf"].rearrange("l (c p) hh -> l p c hh", p=128)[layer, :, :, ht * 128:(ht + 1) * 128])
                for half in range(2):
                    pa, pg = pb[5], pb[6]
                    self.after(TE, sw, eV, eS)
                    for c in range(8):
                        eT.inc(TE.matmul(pa[:, 0:512], PB[:, wb + c * 128:wb + (c + 1) * 128],
                                         t["f_hln"][:, c * 1024 + half * 512:c * 1024 + half * 512 + 512],
                                         start=(c == 0), stop=(c == 7)))
                    for c in range(8):
                        eT.inc(TE.matmul(pg[:, 0:512], PB[:, wb + 1024 + c * 128:wb + 1024 + (c + 1) * 128],
                                         t["f_hln"][:, c * 1024 + half * 512:c * 1024 + half * 512 + 512],
                                         start=(c == 0), stop=(c == 7)))
                    self.after(SC, eT, eS)
                    gl = eS.inc(SC.activation(t["f_kqtmp"][:, 0:512], pg[:, 0:512], AF.Gelu,
                                              bias=t["colbank"][:, 408 + layer * 22 + ht:409 + layer * 22 + ht]))
                    self.after(V, (eS, gl), eV)
                    eV.inc(V.scalar_tensor_tensor(
                        t["f_h2"][:, ht * 1024 + half * 512:ht * 1024 + half * 512 + 512],
                        pa[:, 0:512], t["colbank"][:, 364 + layer * 22 + ht:365 + layer * 22 + ht],
                        t["f_kqtmp"][:, 0:512], ALU.add, ALU.mult))
                self._f_w1_btok[ht % 2] = eT.v
            h2v = eV.v
            for dmt in range(8):
                wb2 = 33792 if dmt % 2 == 0 else 50432
                sw2 = self.dsem("sW2e") if dmt % 2 == 0 else self.dsem("sW2o")
                self.after(G, (eT, self._f_w2_btok[dmt % 2]))
                self.dma("G", sw2, PB[:, wb2:wb2 + 2816].rearrange("p (hc m) -> p hc m", m=128),
                         ap["w2_bf"].rearrange("l (hc p) dm -> l p hc dm", p=128)[layer, :, :, dmt * 128:(dmt + 1) * 128])
                for half in range(2):
                    ps = pb[dmt % 2]
                    self.after(TE, sw2, (eV, h2v), (eV, self._f_w2e_tok[dmt % 2]))
                    for ht in range(22):
                        eT.inc(TE.matmul(ps[:, 0:512], PB[:, wb2 + ht * 128:wb2 + (ht + 1) * 128],
                                         t["f_h2"][:, ht * 1024 + half * 512:ht * 1024 + half * 512 + 512],
                                         start=(ht == 0), stop=(ht == 21)))
                    self._f_w2_btok[dmt % 2] = eT.v
                    self.after(V, eT, sUP)
                    uv = eV.inc(V.tensor_scalar_add(
                        t["f_kqtmp"][:, 0:512], ps[:, 0:512],
                        t["colbank"][:, 452 + layer * 8 + dmt:453 + layer * 8 + dmt]))
                    self._f_w2e_tok[dmt % 2] = uv
                    cc = dmt * 1056 + half * 512 + (0 if dmt < 4 else 1)
                    self.after(V, eV)
                    uv2 = eV.inc(V.tensor_tensor(
                        t["f_xc"][:, cc:cc + 512], t["f_xc"][:, cc:cc + 512],
                        t["f_kqtmp"][:, 0:512], ALU.add))
                    self.after(G, (eV, uv2))
                    self.dma("G", sUP,
                             ap["xf_d"][dmt * 128:(dmt + 1) * 128,
                                        513 + t0 + half * 512:513 + t0 + half * 512 + 512],
                             t["f_xc"][:, cc:cc + 512])
            self._f_xc_tok = eT.v

    def _ffn_halo_bounce(self, src_ap_fn, dst_col0, ncols, mult_flag):
        t, ap = self.t, self.ap
        V, SY, G = self.V, self.SY, self.G
        eV = self.eV
        sHB = self.dsem("sHB")
        sHBl = self.dsem("sHBl")
        assert not (mult_flag and ncols > 1056), "staged copy would overlap src"
        for c in range(8):
            self.after(SY, (eV, self._f_hb_tok), sHB)
            self.dma("S", sHBl, t["f_xw"][:, 0:ncols], src_ap_fn(c))
            if ncols > 1056:
                self.after(G, sHBl)
                self.dma("G", sHB, ap["xf_d"][c * 128:(c + 1) * 128, dst_col0:dst_col0 + ncols],
                         t["f_xw"][:, 0:ncols])
                continue
            self.after(V, sHBl, eV)
            if mult_flag:
                hv = eV.inc(V.tensor_scalar_mul(t["f_xw"][:, 1056:1056 + ncols],
                                                t["f_xw"][:, 0:ncols], t["hcol"][:, 0:1]))
            else:
                hv = eV.inc(V.tensor_copy(t["f_xw"][:, 1056:1056 + ncols], t["f_xw"][:, 0:ncols]))
            self._f_hb_tok = hv
            self.after(G, (eV, hv))
            self.dma("G", sHB, ap["xf_d"][c * 128:(c + 1) * 128, dst_col0:dst_col0 + ncols],
                     t["f_xw"][:, 1056:1056 + ncols])

    def l1_ffn(self):
        t, ap = self.t, self.ap
        V, SC, TE, G, SY = self.V, self.SC, self.TE, self.G, self.SY
        eV, eS, eT = self.eV, self.eS, self.eT
        self.phase_barrier(self.eV, self.eS, self.eT, self.eG, self.dsem("sAC"),
                           self.dsem("sS1"), self.dsem("sS1L"), self.dsem("sNL"),
                           self.dsem("sKV"), self.dsem("sWQ"))
        self.dma("S", self.cS, t["colbank"][:, 328:344], ap["f_lng_cols"])
        self.dma("S", self.cS, t["colbank"][:, 344:360], ap["f_lnb_cols"])
        self.dma("S", self.cS, t["colbank"][:, 360:362], ap["f_qs_cols"])
        self.dma("S", self.cS, t["colbank"][:, 362:364], ap["f_ks_cols"])
        self.dma("S", self.cS, t["colbank"][:, 364:408], ap["f_b1a_cols"])
        self.dma("S", self.cS, t["colbank"][:, 408:452], ap["f_b1g_cols"])
        self.dma("S", self.cS, t["colbank"][:, 452:468], ap["f_b2_cols"])
        self.dma("G", self.cG, t["hd01_sb"][:], ap["hd01"])
        self.dma("G", self.cG, t["hd01T_sb"][:], ap["hd01T"])
        self.dma("G", self.cG, t["swap64_sb"][:], ap["swap64"])
        self.dma("G", self.cG, t["poolB"][:, 55552:55552 + 4096].rearrange("p (kt m) -> p kt m", m=512),
                 ap["lmask_first"].rearrange("(kt p) q -> p kt q", p=128))
        self.dma("G", self.cG, t["poolB"][:, 59648:59648 + 4096].rearrange("p (kt m) -> p kt m", m=512),
                 ap["lmask_rest"].rearrange("(kt p) q -> p kt q", p=128))
        self.phase_barrier(self.cS, self.cG)
        zv = self.eG.inc(G.memset(t["stg512"][:], 0.0))
        self.after(G, (self.eG, zv))
        for c in range(8):
            self.dma("G", self.dsem("sZF"), ap["ag_ft_in"][c * 128:(c + 1) * 128, 0:512],
                     t["stg512"][:])
            self.dma("G", self.dsem("sZF"), ap["ag_ft_in"][c * 128:(c + 1) * 128, 512:516],
                     t["stg512"][:, 0:4])
        self._f_wd_tok = 0
        self._f_rt_tok = 0
        self._f_xw_tok = 0
        self._f_kq_evac = 0
        self._f_msk = 0
        self._f_o_tok = 0
        self._f_upd_evac = 0
        self._f_xc_tok = 0
        self._f_w1_tok = 0
        self._f_w2_tok = 0
        self._f_w2e_tok = [0, 0]
        self._f_hb_tok = 0
        sWD = self.dsem("sWD")
        for l in range(2):
            for src_name, dst_name in (("f_w1a", "w1a_bf"), ("f_w1g", "w1g_bf")):
                for c in range(8):
                    self.after(G, sWD)
                    self.dma("G", sWD, t["f_h2"][:, 0:2816], ap[src_name][l, c * 128:(c + 1) * 128, :])
                    self.after(G, sWD)
                    self.dma("G", sWD, ap[dst_name][l, c * 128:(c + 1) * 128, :], t["f_h2"][:, 0:2816])
            for c in range(22):
                self.after(G, sWD)
                self.dma("G", sWD, t["f_h2"][:, 0:1024], ap["f_w2p"][l, c * 128:(c + 1) * 128, :])
                self.after(G, sWD)
                self.dma("G", sWD, ap["w2_bf"][l, c * 128:(c + 1) * 128, :], t["f_h2"][:, 0:1024])
        self.after(G, sWD)

        for layer in range(2):
            self.after(G, eT, self.dsem("sUP"))
            self.dma("G", self.dsem("sWQ2"), t["f_wqkv"][:].rearrange("p (c m) -> p c m", m=1536),
                     ap["f_wqkv"].rearrange("l (c p) e -> l p c e", p=128)[layer])
            self.dma("G", self.dsem("sWQ2"), t["f_wo"][0:64, :].rearrange("p (h m) -> p h m", m=1024),
                     ap["f_wo"].rearrange("l (h p) dm -> l p h dm", p=64)[layer])
            if layer == 0:
                self._ffn_halo_bounce(
                    lambda c: ap["xn1T_in"][c * 128:(c + 1) * 128, 0:TOK], 513, 4096, False)
                self._ffn_halo_bounce(
                    lambda c: ap["xn1T_out"][c * 128:(c + 1) * 128, 3583:4096], 0, 513, True)
            else:
                sFT = self.dsem("sFT")
                sFTl = self.dsem("sFTl")
                self.after(self.G, self.dsem("sZF"))
                for c in range(8):
                    self.after(SY, self.dsem("sUP"), (eV, self._f_hb_tok), sFT)
                    self.dma("S", sFTl, t["f_xw"][:, 0:513],
                             ap["xf_d"][c * 128:(c + 1) * 128, 4096:4609])
                    self.after(V, sFTl, eV)
                    self._f_hb_tok = eV.inc(V.tensor_copy(t["f_xw"][:, 1056:1056 + 513],
                                                          t["f_xw"][:, 0:513]))
                    self.after(G, eV)
                    self.dma("G", sFT, ap["ag_ft_in"][c * 128:(c + 1) * 128, 0:513],
                             t["f_xw"][:, 1056:1056 + 513])
                self.after(G, sFT, self.dsem("sZF"))
                self.ccs.inc(G.collective_compute(
                    "AllGather", ALU.bypass, replica_groups=PAIR_GROUPS,
                    ins=[self.d_ag_ft_in.ap().opt()], outs=[self.d_ag_ft_out.ap().opt()]))
                self.after(SY, self.ccs)
                self._ffn_halo_bounce(
                    lambda c: ap["ag_ft_out"][c * 128:(c + 1) * 128, 0:513], 0, 513, True)
            self.after(SY, self.dsem("sHB"), self.dsem("sFT"))
            self.after(G, self.dsem("sHB"))

            self._ffn_qkv_window(layer, -1)
            for w in range(8):
                self._ffn_qkv_window(layer, w)
                if w > 0:
                    self._ffn_apply_update(w - 1)
                self._ffn_window_attn(layer, w)
            self._ffn_apply_update(7)
            sFC = self.dsem("sFC")
            self.after(SY, self.dsem("sUP"), self.dsem("sHB"))
            for c in range(8):
                self.dma_nc("S", self.dsem("sFCl"), t["f_xw"][:, c * 4:c * 4 + 1],
                            ap["xf_d"][c * 128:(c + 1) * 128, 4608:4609])
            self.after(V, self.dsem("sFCl"), eV)
            hv = eV.inc(V.tensor_copy(t["f_xw"][:, 1056:1056 + 32].rearrange("p (c m) -> p c m", m=4)[:, :, 0:1]
                                      if False else t["f_xw"][:, 1056:1088], t["f_xw"][:, 0:32]))
            self.after(G, (eV, hv))
            self.dma("G", sFC, ap["ag_fc_in"][:, 0:4].rearrange("(c p) m -> p c m", p=128)[:, :, 0:1]
                     if False else ap["ag_fc_in"].rearrange("(c p) m -> p c m", p=128),
                     t["f_xw"][:, 1056:1088].rearrange("p (c m) -> p c m", m=4))
            self.after(G, sFC)
            self.ccs.inc(G.collective_compute(
                "AllGather", ALU.bypass, replica_groups=PAIR_GROUPS,
                ins=[self.d_ag_fc_in.ap().opt()], outs=[self.d_ag_fc_out.ap().opt()]))
            self.after(SY, self.ccs)
            for c in range(8):
                self.dma("S", self.dsem("sFCl"), t["f_xw"][:, c * 4:c * 4 + 4],
                         ap["ag_fc_out"][c * 128:(c + 1) * 128, 0:4])
            self.after(V, self.dsem("sFCl"), eV)
            hv = eV.inc(V.tensor_scalar_mul(t["f_xw"][:, 1056:1088], t["f_xw"][:, 0:32],
                                            t["hcol"][:, 0:1]))
            self.after(G, (eV, hv))
            for c in range(8):
                self.dma_nc("G", sFC, ap["xf_d"][c * 128:(c + 1) * 128, 512:513],
                            t["f_xw"][:, 1056 + c * 4:1057 + c * 4])
            self.after(SY, sFC)
            self.after(G, sFC)
            self._ffn_mlp(layer)

    def l1_gate_out(self):
        t, ap = self.t, self.ap
        V, SC, TE, G, SY = self.V, self.SC, self.TE, self.G, self.SY
        eV, eS, eT = self.eV, self.eS, self.eT
        self.phase_barrier(self.eV, self.eS, self.eT, self.eG, self.dsem("sUP"),
                           self.dsem("sXC"), self.dsem("sW1"), self.dsem("sW2"),
                           self.dsem("sFC"), self.dsem("sS1L"))
        if "dbg_s1" in self.debug:
            self.dma("G", self.dsem("dbgs"), ap["dbg_s1"], ap["s1_d"])
        if "dbg_xf" in self.debug:
            self.dma("G", self.dsem("dbgs"), ap["dbg_xf"], ap["xf_d"])
        sGO = self.dsem("sGO")
        sOU = self.dsem("sOU")
        gtok = 0
        ttok = 0
        for i in range(TOK // 128):
            self.after(SY, (eT, gtok), sGO)
            self.dma("S", sGO, t["xnb"][:].rearrange("p (c m) -> p c m", m=128),
                     ap["xf_d"].rearrange("(c p) m -> p c m", p=128)[:, :, 513 + i * 128:513 + (i + 1) * 128])
            self.after(TE, sGO, (eV, ttok))
            for c in range(8):
                eT.inc(TE.transpose(t["ptr"][:, c * 128:(c + 1) * 128],
                                    t["xnb"][:, c * 128:(c + 1) * 128], t["ident"][:]))
            self.after(V, eT, eS)
            ttok = eV.inc(V.tensor_copy(t["trstage"][:], t["ptr"][:, 0:1024]))
            self.after(SC, (eV, ttok), eS)
            gv = eS.inc(SC.activation(t["xnb2"][:], t["trstage"][:], AF.Silu))
            gtok = eT.v
            self.after(SY, eV, sGO)
            self.dma("S", sGO, t["xt"][:, 0:DM], ap["x1_d"][i * 128:(i + 1) * 128, :])
            self.dma("S", sGO, t["xt"][:, DM:2 * DM], ap["s1_d"][i * 128:(i + 1) * 128, :])
            self.after(V, sGO, (eS, gv), eV, sOU)
            eV.inc(V.tensor_tensor(t["sqscrap"][:], t["xt"][:, DM:2 * DM], t["xnb2"][:], ALU.mult))
            self.after(V, eV)
            ov = eV.inc(V.tensor_tensor(t["sqscrap"][:], t["sqscrap"][:], t["xt"][:, 0:DM], ALU.add))
            self.after(G, (eV, ov))
            self.dma("G", sOU, ap["out"][i * 128:(i + 1) * 128, :], t["sqscrap"][:])

    def finalize(self):
        G = self.G
        for s in [self.cS, self.cG, self.sX, self.sIW, self.sIWst, self.sXN, self.sSR]:
            if s.v:
                G.wait_ge(s.h, s.v)
        for s in self._dyn_sems.values():
            if s.v:
                G.wait_ge(s.h, s.v)


def kernel(**inputs):
    prog = Prog(upto="all")
    in_maps = [_prep_core_inputs(inputs, c) for c in range(NCORES)]
    res = run_bass_kernel_spmd(prog.nc, in_maps, core_ids=list(range(NCORES)))
    out = np.empty((B, S, DM), np.float32)
    for c in range(NCORES):
        b, h = c // 2, c % 2
        out[b, h * TOK:(h + 1) * TOK] = res.results[c]["out"]
    return out
